# version 22
# baseline (speedup 1.0000x reference)
import numpy as np
from contextlib import ExitStack

import concourse.bass as bass
import concourse.mybir as mybir
from concourse.bass_utils import run_bass_kernel_spmd

F32 = mybir.dt.float32
BF16 = mybir.dt.bfloat16
U16 = mybir.dt.uint16
AF = mybir.ActivationFunctionType
ALU = mybir.AluOpType
AX = mybir.AxisListType

P = 128
NTOK = 4096
D = 1024
HID = 512
NH = 8
DH = 64
I1 = 2816
I2 = 2 * I1
NIC = I1 // P
NG = NTOK // P
NST = NTOK // 512
MAGIC = 12582912.0
EPS = 1e-5
SCALE = DH ** -0.5
QMAX = 127.0
NCORE = 8

ENGS = ("sync", "gpsimd", "vector", "scalar", "tensor")


class Sched:

    def __init__(self):
        self.ops = {e: [] for e in ENGS}
        self.cnt = {}
        self.sems = None

    def emit(self, eng, build=None, waits=(), incs=()):
        post = {}
        for s, d in incs:
            self.cnt[s] = self.cnt.get(s, 0) + d
            post[s] = self.cnt[s]
        waits = [w for w in waits if w is not None and w[1] > 0]

        def run(e, build=build, waits=tuple(waits), incs=tuple(incs)):
            for s, v in waits:
                e.wait_ge(self.sems[s], v)
            ins = build(e) if build is not None else None
            for s, d in incs:
                ins.then_inc(self.sems[s], d)

        self.ops[eng].append(run)
        return post


S = None


def _dma(queue, out, in_, waits=(), sem=None):
    if sem is None:
        sem = "dg" if queue == "gpsimd" else "ds"
    return S.emit(queue, lambda e, o=out, i=in_: e.dma_start(out=o, in_=i),
                  waits=waits, incs=[(sem, 16)])




def _finish(nc, S, engs, block_ctx, blocks):
    dq = [("ds", S.cnt.get("ds", 0)), ("dg", S.cnt.get("dg", 0)),
          ("so", S.cnt.get("so", 0)), ("dc", S.cnt.get("dc", 0)),
          ("dl", S.cnt.get("dl", 0))]
    for eng in engs:
        S.emit(eng, waits=dq)
    with nc.Block() as block:
        @block.sync
        def _(e):
            for f in S.ops["sync"]:
                f(e)

        @block.gpsimd
        def _(e):
            for f in S.ops["gpsimd"]:
                f(e)

        @block.vector
        def _(e):
            for f in S.ops["vector"]:
                f(e)

        @block.scalar
        def _(e):
            for f in S.ops["scalar"]:
                f(e)

        @block.tensor
        def _(e):
            for f in S.ops["tensor"]:
                f(e)
    return nc

PHASE_LIMIT = 99


def build_nc():
    global S
    nc = bass.Bass("TRN2", target_bir_lowering=False)

    x_hb = nc.dram_tensor("x", [NTOK, D], F32, kind="ExternalInput").ap()
    wq_hb = nc.dram_tensor("wq", [D, HID], F32, kind="ExternalInput").ap()
    wk_hb = nc.dram_tensor("wk", [D, HID], F32, kind="ExternalInput").ap()
    wv_hb = nc.dram_tensor("wv", [D, HID], F32, kind="ExternalInput").ap()
    wo_hb = nc.dram_tensor("wo", [HID, D], F32, kind="ExternalInput").ap()
    bo_hb = nc.dram_tensor("bo", [1, D], F32, kind="ExternalInput").ap()
    cx_hb = nc.dram_tensor("cx", [1, 1], F32, kind="ExternalInput").ap()
    cd_hb = nc.dram_tensor("cd", [1, 1], F32, kind="ExternalInput").ap()
    wgu_hb = nc.dram_tensor("wgu", [D, I2], F32, kind="ExternalInput").ap()
    wdt_hb = nc.dram_tensor("wdt", [I1, D], F32, kind="ExternalInput").ap()
    out_hb = nc.dram_tensor("out", [NTOK, D], F32, kind="ExternalOutput").ap()
    dbg_hb = nc.dram_tensor("dbg", [P, 32], F32, kind="ExternalOutput").ap()
    dbg2_hb = nc.dram_tensor("dbg2", [P, 64], F32, kind="ExternalOutput").ap()

    _dbg_kind = "ExternalOutput" if globals().get("DEBUG_IO") else "Internal"
    qx_dbg = nc.dram_tensor("qx_dbg", [NTOK, D], BF16, kind=_dbg_kind).ap()
    qf_dbg = nc.dram_tensor("qf_dbg", [P, 8 * 4096], BF16, kind=_dbg_kind).ap()
    xn_hb = nc.dram_tensor("xn_scr", [D, NTOK], F32, kind=_dbg_kind).ap()
    h_hb = nc.dram_tensor("h_scr", [I1, NTOK], F32, kind=_dbg_kind).ap()
    ao_hb = nc.dram_tensor("ao_scr", [NTOK, D], BF16, kind=_dbg_kind).ap()
    qgu_hb = nc.dram_tensor("qgu_scr", [D, I2], BF16, kind=_dbg_kind).ap()
    qwd_hb = nc.dram_tensor("qwd_scr", [I1, D], BF16, kind=_dbg_kind).ap()
    gx_in = nc.dram_tensor("gx_in", [1, 1], F32, kind="Internal").ap()
    gx_out = nc.dram_tensor("gx_out", [1, 1], F32, kind="Internal").ap()
    gh_in = nc.dram_tensor("gh_in", [1, 1], F32, kind="Internal").ap()
    gh_out = nc.dram_tensor("gh_out", [1, 1], F32, kind="Internal").ap()

    with ExitStack() as stack:
        ent = stack.enter_context

        ARENA_COLS = 51456
        arena = ent(nc.sbuf_tensor("arena", [P, ARENA_COLS], F32))
        spans = []

        def alloc(name, cols, t0, t1, dtype=F32, parts=P):
            cols = (cols + 7) & ~7
            taken = sorted((o, c) for (o, c, a, b, n) in spans
                           if not (b < t0 or a > t1))
            off = 0
            for o, c in taken:
                if off + cols <= o:
                    break
                off = max(off, o + c)
            assert off + cols <= ARENA_COLS, \
                f"arena overflow: {name} needs {cols} cols at {off}"
            spans.append((off, cols, t0, t1, name))
            globals()["DEBUG_SPANS"] = {n: (o, c) for (o, c, a, b, n) in spans}
            view = arena[0:parts, off:off + cols]
            if dtype == BF16:
                view = view.bitcast(BF16)
            return view

        ident_b = alloc("ident_b", 64, 0, 7, BF16)
        ident_f = alloc("ident_f", 128, 0, 7)
        ones_row = alloc("ones_row", 128, 0, 7, parts=1)
        wq_sb = alloc("wq_sb", 2048, 0, 1, BF16)
        wk_sb = alloc("wk_sb", 2048, 0, 1, BF16)
        wv_sb = alloc("wv_sb", 2048, 0, 1, BF16)
        wo_sb = alloc("wo_sb", 2048, 0, 3, BF16)
        bo_bc = alloc("bo_bc", D, 0, 3)
        sgu_cols = alloc("sgu_cols", 48, 0, 7)
        gsc_cols = alloc("gsc_cols", 48, 0, 7)
        swd_bc = alloc("swd_bc", D, 4, 5)
        sc_tile = alloc("sc_tile", D, 5, 6)
        rswd_bc = alloc("rswd_bc", D, 4, 5)
        mx_xn = alloc("mx_xn", 8, 0, 7)
        mx_h = alloc("mx_h", 8, 0, 7)
        ctx_acc = alloc("ctx_acc", 4 * 132, 0, 7)
        ctx_sb = alloc("ctx_sb", 4 * 64, 0, 7, BF16)
        v_tok = [alloc(f"v_tok{j}", 264, 0, 1, BF16) for j in range(2)]
        expk = [alloc(f"expk{j}", 256, 0, 1, BF16) for j in range(2)]
        expq = alloc("expq", 512, 0, 1)
        qs_tok = [alloc(f"qs_tok{j}", 256, 0, 1, BF16) for j in range(2)]
        stat = [alloc(f"stat{j}", 16, 0, 7) for j in range(2)]
        sqd = alloc("sqd", D, 1, 1)
        qsr = alloc("qsr", 8, 0, 7)
        qsums = alloc("qsums", 8, 0, 7)
        hist = alloc("hist", 64, 0, 7)
        cst = alloc("cst", 16, 0, 7)
        barsc = alloc("barsc", 8, 0, 7)
        bc_cols = alloc("bc_cols", 8, 0, 7)
        scal = alloc("scal", 24, 0, 7, parts=1)
        rs_row = alloc("rs_row", 2048, 0, 5, parts=1)
        qx_tl = [alloc(f"qx_tl{j}", 512, 2, 6, BF16) for j in range(2)]
        xq_ld = [alloc(f"xq_ld{j}", 2048, 2, 6) for j in range(2)]
        tqs = [alloc(f"tqs{j}", 2048, 2, 6) for j in range(2)]

        xA = [alloc(f"xA{j}", D, 1, 1) for j in range(2)]
        xnA = [alloc(f"xnA{j}", D, 1, 1) for j in range(2)]
        xnfm = [alloc(f"xnfm{j}", 512, 1, 1, BF16) for j in range(2)]
        xnf_st = [alloc(f"xnf_st{j}", 1024, 1, 1) for j in range(2)]
        wgub = [alloc(f"wgub{j}", 1408, 1, 1) for j in range(2)]
        ygu = alloc("ygu", 1408, 1, 1)
        Mgu = alloc("Mgu", I1, 1, 1)
        rsgu_bc = alloc("rsgu_bc", I1, 1, 1)
        qgu_o = [alloc(f"qgu_o{j}", 704, 1, 1, BF16) for j in range(2)]
        qsoft = alloc("qsoft", 8192, 1, 2, BF16)

        oa_s = [alloc(f"oa_s{j}", 1024, 2, 3, BF16) for j in range(2)]
        ao_b = [alloc(f"ao_b{j}", 512, 2, 3, BF16) for j in range(2)]

        qwgu = alloc("qwgu", 8 * I2 // 2, 3, 4, BF16)
        qxfm = [alloc(f"qxfm{j}", 2048, 2, 4, BF16) for j in range(2)]
        silu_b = [alloc(f"silu{j}", 512, 4, 4) for j in range(2)]
        h_b = [alloc(f"h_b{j}", D, 4, 4) for j in range(2)]
        wdb = [alloc(f"wdb{j}", D, 4, 5) for j in range(2)]
        ywd = alloc("ywd", D, 4, 5)
        Mwd = alloc("Mwd", 2 * D, 4, 5)
        qwd_o = [alloc(f"qwd_o{j}", 512, 4, 5, BF16) for j in range(2)]

        qwd_sb = alloc("qwd_sb", NIC * 512, 5, 6, BF16)
        qh_s = [alloc(f"qh_s{j}", NIC * 256, 5, 6, BF16) for j in range(2)]
        hr = [alloc(f"hr{j}", D, 5, 6) for j in range(2)]
        xr = [alloc(f"xr{j}", D, 6, 6) for j in range(2)]
        aor = [alloc(f"aor{j}", 512, 6, 6, BF16) for j in range(2)]
        asm = [alloc(f"asm{j}", D, 6, 6) for j in range(2)]
        asm2 = [alloc(f"asm2{j}", D, 6, 6) for j in range(2)]
        outb = [alloc(f"outb{j}", D, 6, 6) for j in range(2)]

        p_mm = [ent(nc.psum_tensor(f"p{i}", [P, 512], F32)) for i in range(4)]
        pt = ent(nc.psum_tensor("pt", [P, 1024], BF16))
        ptf = pt.bitcast(F32)
        pc = [ent(nc.psum_tensor(f"pc{i}", [P, 512], F32)) for i in range(3)]
        ctx_banks = [pc[0], pc[1], pc[2], p_mm[3]]

        sem_names = ["ds", "dg", "pe", "act", "dve", "pool", "cc", "so",
                     "bar", "dc", "dl"]
        sem_handles = {n: ent(nc.semaphore(n)) for n in sem_names}
        S = Sched()
        S.sems = sem_handles
        globals()["S"] = S

        state = {
            "pt_free": ("dve", 0),
            "pc0_free": ("dve", 0),
        }

        def barrier():
            dq = [("ds", S.cnt.get("ds", 0)), ("dg", S.cnt.get("dg", 0)),
                  ("so", S.cnt.get("so", 0)), ("dl", S.cnt.get("dl", 0))]
            S.emit("vector", lambda e: e.memset(barsc[:, 0:1], 0.0),
                   waits=dq, incs=[("bar", 1)])
            S.emit("gpsimd", lambda e: e.memset(barsc[:, 1:2], 0.0),
                   waits=dq, incs=[("bar", 1)])
            S.emit("scalar", lambda e: e.memzero(barsc[:, 2:3]),
                   waits=dq, incs=[("bar", 1)])
            S.emit("tensor", lambda e: e.nop(), waits=dq, incs=[("bar", 1)])
            S.emit("sync", lambda e: e.nop(), waits=dq, incs=[("bar", 1)])
            bv = S.cnt["bar"]
            for eng in ENGS:
                S.emit(eng, waits=[("bar", bv)])

        S.emit("gpsimd", lambda e: e.memset(ident_b, 0.0))
        S.emit("gpsimd", lambda e: e.affine_select(
            out=ident_b, in_=ident_b, compare_op=ALU.not_equal, fill=1.0,
            base=0, pattern=[[-1, P]], channel_multiplier=1))
        S.emit("gpsimd", lambda e: e.memset(ident_f, 0.0))
        S.emit("gpsimd", lambda e: e.affine_select(
            out=ident_f, in_=ident_f, compare_op=ALU.not_equal, fill=1.0,
            base=0, pattern=[[-1, P]], channel_multiplier=1))
        iv = S.emit("gpsimd", lambda e: e.memset(ones_row, 1.0),
                    incs=[("pool", 1)])
        ident_ready = ("pool", iv["pool"])
        S.emit("vector", lambda e: e.memset(cst[:, 0:1], EPS))
        cm = S.emit("vector", lambda e: e.memset(cst[:, 1:2], MAGIC),
                    incs=[("dve", 1)])
        nc.const_aps.register(F32, MAGIC, cst[:, 1:2])
        cst_ready = ("dve", cm["dve"])
        for buf in (stat[0], stat[1], scal, cst[0:P, 2:16], qsr, qsums, hist,
                    bc_cols, mx_xn[:, 0:1], mx_h[:, 0:1], ctx_acc, ctx_sb):
            S.emit("vector", lambda e, b=buf: e.memset(b, 0.0),
                   incs=[("dve", 1)])
        for j in range(2):
            ms = S.emit("vector", lambda e, jj=j: e.memset(
                v_tok[jj].rearrange("p (g c) -> p g c", c=132)[:, :, 128:129],
                1.0), incs=[("dve", 1)])
        memset_done = ("dve", ms["dve"])

        for (w_sb, w_hb, nchunk) in ((wq_sb, wq_hb, 8), (wk_sb, wk_hb, 8),
                                     (wv_sb, wv_hb, 8), (wo_sb, wo_hb, 4)):
            qv = _dma("gpsimd",
                      w_sb.rearrange("p (f n) -> p f n", f=nchunk),
                      w_hb.rearrange("(f p) n -> p f n", p=P))
        w_loaded = ("dg", qv["dg"])
        S.emit("scalar", lambda e: e.activation(cst[:, 12:13], cst[:, 1:2],
                                                AF.Square),
               waits=[cst_ready])
        S.emit("scalar", lambda e: e.activation(cst[:, 12:13], cst[:, 1:2],
                                                AF.Ln))
        S.emit("scalar", lambda e: e.activation(cst[:, 12:13], cst[:, 1:2],
                                                AF.Exp, scale=-1e-6))
        wu = S.emit("scalar", lambda e: e.activation(cst[:, 13:14],
                                                     cst[:, 0:1], AF.Copy),
                    incs=[("act", 1)])
        act_warm = ("act", wu["act"])
        qv = _dma("sync", scal[0:1, 0:1], cx_hb)
        qv = _dma("sync", scal[0:1, 1:2], cd_hb)
        qv = _dma("sync", rs_row[0:1, 0:D], bo_hb)
        bo_row_v = ("ds", qv["ds"])
        ev = None
        for cb in range(2):
            pv = S.emit("tensor", lambda e, cb=cb: e.matmul(
                p_mm[cb][:, 0:512], ones_row,
                rs_row[0:1, cb * 512:(cb + 1) * 512], start=True, stop=True),
                waits=[bo_row_v, ident_ready], incs=[("pe", 1)])
            ev = S.emit("vector", lambda e, cb=cb: e.tensor_copy(
                bo_bc[:, cb * 512:(cb + 1) * 512], p_mm[cb][:, 0:512]),
                waits=[("pe", pv["pe"])], incs=[("dve", 1)])
        qkv_free = [("dve", ev["dve"] - 1), ("dve", ev["dve"]), ("dve", 0)]

        gu = {"p1rd": {0: None, 1: None}, "p2rd": {0: None, 1: None},
              "yrd": None, "qo_j": 1, "qord": {0: None, 1: None},
              "qlrd": {0: None, 1: None}, "rsbc": {}, "laststore": None}

        def gu_steps():
            steps = []
            for qt in range(4):
                col0 = qt * 1408
                hcol = (qt % 2) * 1408
                ic_base = qt * 11
                for fc in range(8):
                    j = fc % 2

                    def s_load(qt=qt, fc=fc, j=j, col0=col0):
                        qv = _dma("sync", wgub[j],
                                  wgu_hb[fc * P:(fc + 1) * P, col0:col0 + 1408],
                                  waits=[gu["p1rd"][j], gu["p2rd"][j]])
                        gu[("ld", fc)] = ("ds", qv["ds"])

                    def s_max(fc=fc, j=j):
                        ab = S.emit("scalar", lambda e, s=wgub[j]: e.activation(
                            ygu, s, AF.Abs),
                            waits=[gu[("ld", fc)], gu["yrd"]],
                            incs=[("act", 1)])
                        gu["p1rd"][j] = ("act", ab["act"])
                        mx = Mgu[:, 0:1408]
                        if fc == 0:
                            qv = S.emit("vector", lambda e, mx=mx:
                                        e.tensor_copy(mx, ygu),
                                        waits=[("act", ab["act"]),
                                               gu.get("m_rd")],
                                        incs=[("dve", 1)])
                        else:
                            qv = S.emit("vector", lambda e, mx=mx:
                                        e.tensor_tensor(mx, mx, ygu, ALU.max),
                                        waits=[("act", ab["act"])],
                                        incs=[("dve", 1)])
                        gu["yrd"] = ("dve", qv["dve"])
                        gu["m_wr"] = ("dve", qv["dve"])

                    steps.append(s_load)
                    steps.append(s_max)

                def s_scales(qt=qt, hcol=hcol, ic_base=ic_base):
                    mrd = [gu["m_wr"]]
                    for b in range(11):
                        blk = Mgu[:, b * P: (b + 1) * P]
                        tv = S.emit("tensor", lambda e, s=blk: e.transpose(
                            ptf[:, 0:P], s, ident_f),
                            waits=mrd + [state["pt_free"]], incs=[("pe", 1)])
                        col = sgu_cols[:, ic_base + b: ic_base + b + 1]
                        rv = S.emit("vector", lambda e, c=col: e.tensor_reduce(
                            c, ptf[:, 0:P], AX.X, ALU.max),
                            waits=[("pe", tv["pe"])], incs=[("dve", 1)])
                        state["pt_free"] = ("dve", rv["dve"])
                        tv2 = S.emit("tensor", lambda e, c=col: e.transpose(
                            ptf[0:1, 0:P], c, ident_f),
                            waits=[("dve", rv["dve"]), state["pt_free"]],
                            incs=[("pe", 1)])
                        rv2 = S.emit("vector", lambda e, b=b: e.tensor_copy(
                            rs_row[0:1, b * P:(b + 1) * P], ptf[0:1, 0:P]),
                            waits=[("pe", tv2["pe"])], incs=[("dve", 1)])
                        state["pt_free"] = ("dve", rv2["dve"])
                        gu["m_rd"] = ("pe", tv2["pe"])
                    S.emit("vector", waits=[("dve", rv2["dve"])])
                    fr = S.emit("vector", lambda e: e.reciprocal(
                        rs_row[0:1, 0:1408], rs_row[0:1, 0:1408]),
                        incs=[("dve", 1)])
                    rv3 = S.emit("vector", lambda e: e.tensor_scalar_mul(
                        rs_row[0:1, 0:1408], rs_row[0:1, 0:1408], QMAX),
                        waits=[("dve", fr["dve"])], incs=[("dve", 1)])
                    last = None
                    for cb in range(0, 1408, 512):
                        n = min(512, 1408 - cb)
                        tv3 = S.emit("tensor", lambda e, cb=cb, n=n: e.matmul(
                            p_mm[0][:, 0:n], ones_row, rs_row[0:1, cb:cb + n],
                            start=True, stop=True),
                            waits=[("dve", rv3["dve"]), qkv_free[0]],
                            incs=[("pe", 1)])
                        last = S.emit("vector", lambda e, cb=cb, n=n, hcol=hcol:
                                      e.tensor_copy(
                                          rsgu_bc[:, hcol + cb:hcol + cb + n],
                                          p_mm[0][:, 0:n]),
                                      waits=[("pe", tv3["pe"])],
                                      incs=[("dve", 1)])
                        qkv_free[0] = ("dve", last["dve"])
                    gu["rsbc"][qt] = ("dve", last["dve"])

                steps.append(s_scales)

                for fc in range(8):
                    j = fc % 2

                    def s_load2(qt=qt, fc=fc, j=j, col0=col0):
                        qv = _dma("sync", wgub[j],
                                  wgu_hb[fc * P:(fc + 1) * P, col0:col0 + 1408],
                                  waits=[gu["p2rd"][j], gu["p1rd"][j]])
                        gu[("ld2", fc)] = ("ds", qv["ds"])

                    def s_quant(qt=qt, fc=fc, j=j, hcol=hcol, col0=col0):
                        yv = S.emit("vector", lambda e, s=wgub[j], hcol=hcol:
                                    e.tensor_tensor(
                                        ygu, s, rsgu_bc[:, hcol:hcol + 1408],
                                        ALU.mult),
                                    waits=[gu[("ld2", fc)], gu["rsbc"][qt],
                                           gu["yrd"]],
                                    incs=[("dve", 1)])
                        gu["p2rd"][j] = ("dve", yv["dve"])
                        jo = gu["qo_j"] = 1 - gu["qo_j"]
                        qv = S.emit("vector", lambda e, jo=jo: e.tensor_scalar(
                            qgu_o[jo], ygu, MAGIC, MAGIC, ALU.add, ALU.subtract),
                            waits=[gu["qord"][jo]],
                            incs=[("dve", 1)])
                        gu["yrd"] = ("dve", qv["dve"])
                        sv = _dma("scalar",
                                  qgu_hb[fc * P:(fc + 1) * P, col0:col0 + 1408],
                                  qgu_o[jo], waits=[("dve", qv["dve"])],
                                  sem="so")
                        gu["qord"][jo] = ("so", sv["so"])
                        gu["laststore"] = ("so", sv["so"])

                    steps.append(s_load2)
                    steps.append(s_quant)
            return steps

        prep_steps = gu_steps()
        prep_i = [0]

        def emit_prep(n):
            while n > 0 and prep_i[0] < len(prep_steps):
                prep_steps[prep_i[0]]()
                prep_i[0] += 1
                n -= 1


        xA_rd = [[], []]
        xnA_rd = [[], []]
        xnst_rd = [None, None]
        xnfm_rd = [None, None]
        stat_rd = [None, None]
        expk_rd = [None, None]
        vtok_rd = [None, None]
        qstok_rd = [None, None]
        expq_rd = None
        ctx_done = None

        for g in range(NG):
            sst, t = divmod(g, 4)
            j = g % 2
            st = stat[j]
            qv = _dma("sync", xA[j], x_hb[g * P:(g + 1) * P, :],
                      waits=xA_rd[j])
            x_ld = ("ds", qv["ds"])
            rv = S.emit("vector", lambda e, j=j, st=st: e.reduce_sum(
                st[:, 0:1], xA[j], AX.X),
                waits=[x_ld, stat_rd[j]] + ([memset_done] if g == 0 else []),
                incs=[("dve", 1)])
            av = S.emit("scalar", lambda e, j=j, st=st: e.activation(
                sqd, xA[j], AF.Square, accum_out=st[:, 8:9]),
                waits=[x_ld, stat_rd[j]] + ([act_warm] if g == 0 else []),
                incs=[("act", 1)])
            m2 = S.emit("vector", lambda e, st=st: e.scalar_tensor_tensor(
                st[:, 3:4], st[:, 0:1], 1.0 / (D * D), st[:, 0:1],
                ALU.mult, ALU.mult),
                waits=[("dve", rv["dve"]), ("act", av["act"])],
                incs=[("dve", 1)])
            nm = S.emit("vector", lambda e, st=st: e.tensor_scalar_mul(
                st[:, 5:6], st[:, 0:1], -1.0 / D), incs=[("dve", 1)])
            S.emit("vector", waits=[("dve", nm["dve"])])
            vv = S.emit("vector", lambda e, st=st: e.scalar_tensor_tensor(
                st[:, 4:5], st[:, 8:9], 1.0 / D, st[:, 3:4],
                ALU.mult, ALU.subtract), incs=[("dve", 1)])
            lnv_ = S.emit("scalar", lambda e, st=st: e.activation(
                st[:, 9:10], st[:, 4:5], AF.Ln, bias=cst[:, 13:14]),
                waits=[("dve", vv["dve"])], incs=[("act", 1)])
            avr = S.emit("scalar", lambda e, st=st: e.activation(
                st[:, 10:11], st[:, 9:10], AF.Exp, scale=-0.5),
                waits=[("act", lnv_["act"])], incs=[("act", 1)])
            xv = S.emit("vector", lambda e, j=j, st=st: e.tensor_scalar(
                xnA[j], xA[j], st[:, 5:6], st[:, 10:11], ALU.add, ALU.mult),
                waits=list(xnA_rd[j]) + [("act", avr["act"])],
                incs=[("dve", 1)])
            xA_rd[j] = [("dve", xv["dve"]), ("act", av["act"])]
            stat_rd[j] = ("dve", xv["dve"])
            fr = S.emit("vector", lambda e, j=j, st=st: e.tensor_reduce(
                st[:, 6:7], xnA[j], AX.X, ALU.max, apply_absolute_value=True),
                incs=[("dve", 1)])
            mxv = S.emit("vector", lambda e, st=st: e.tensor_max(
                mx_xn[:, 0:1], mx_xn[:, 0:1], st[:, 6:7]),
                waits=[("dve", fr["dve"])], incs=[("dve", 1)])
            mx_last = ("dve", mxv["dve"])
            fv = None
            for q4 in range(2):
                tv = None
                for c4 in range(4):
                    fc = q4 * 4 + c4
                    tv = S.emit("tensor", lambda e, j=j, fc=fc, c4=c4:
                                e.transpose(
                                    ptf[:, c4 * P:(c4 + 1) * P],
                                    xnA[j][:, fc * P:(fc + 1) * P], ident_f),
                                waits=([("dve", xv["dve"]), state["pt_free"]]
                                       if c4 == 0 else []),
                                incs=([("pe", 1)] if c4 == 3 else []))
                S.emit("vector", lambda e, j=j, q4=q4: e.tensor_copy(
                    xnf_st[j][:, q4 * 512:(q4 + 1) * 512], ptf[:, 0:512]),
                    waits=[("pe", tv["pe"])] + ([xnst_rd[j]] if q4 == 0 else []),
                    incs=[("dve", 1)])
                fv = S.emit("vector", lambda e, j=j, q4=q4: e.tensor_copy(
                    xnfm[j][:, q4 * 512:(q4 + 1) * 512], ptf[:, 0:512]),
                    waits=([xnfm_rd[j]] if q4 == 0 else []), incs=[("dve", 1)])
                state["pt_free"] = ("dve", fv["dve"])
            sv = _dma("scalar",
                      xn_hb[:, g * P:(g + 1) * P]
                      .rearrange("(c p) n -> p c n", p=P),
                      xnf_st[j].rearrange("p (c n) -> p c n", c=8),
                      waits=[("dve", fv["dve"])], sem="so")
            xnst_rd[j] = ("so", sv["so"])
            xnA_rd[j] = [("pe", tv["pe"]), ("dve", fr["dve"])]
            mmv = {}
            for fc in range(8):
                first, last = fc == 0, fc == 7
                for pi, w_sb in enumerate((wq_sb, wk_sb, wv_sb)):
                    w0 = []
                    if first:
                        w0 = [qkv_free[pi]]
                        if pi == 0:
                            w0.append(("dve", fv["dve"]))
                        if g == 0 and pi == 0:
                            w0.append(w_loaded)
                    m = S.emit("tensor", lambda e, j=j, fc=fc, pi=pi, w_sb=w_sb,
                               first=first, last=last: e.matmul(
                                   p_mm[pi][:, 0:512],
                                   xnfm[j][:, fc * P:(fc + 1) * P],
                                   w_sb[:, fc * 512:(fc + 1) * 512],
                                   start=first, stop=last),
                               waits=w0, incs=([("pe", 1)] if last else []))
                    if last:
                        mmv[pi] = m["pe"]
            xnfm_rd[j] = ("pe", mmv[2])
            aq = None
            for hh in range(NH):
                aq = S.emit("scalar", lambda e, hh=hh: e.activation(
                    expq[:, hh * DH:(hh + 1) * DH],
                    p_mm[0][:, hh * DH:(hh + 1) * DH],
                    AF.Exp, accum_out=qsums[:, hh:hh + 1]),
                    waits=([("pe", mmv[0]), expq_rd] if hh == 0 else []),
                    incs=[("act", 1)])
            qkv_free[0] = ("act", aq["act"])
            fr = S.emit("vector", lambda e: e.reciprocal(qsr[:, 0:8],
                                                         qsums[:, 0:8]),
                        waits=[("act", aq["act"])], incs=[("dve", 1)])
            f2 = S.emit("vector", lambda e: e.tensor_scalar_mul(
                qsr[:, 0:8], qsr[:, 0:8], SCALE),
                waits=[("dve", fr["dve"])], incs=[("dve", 1)])
            S.emit("vector", waits=[("dve", f2["dve"])])
            qs = None
            for hh in range(NH):
                qs = S.emit("vector", lambda e, j=j, hh=hh: e.tensor_scalar_mul(
                    qs_tok[j][:, hh * DH:(hh + 1) * DH],
                    expq[:, hh * DH:(hh + 1) * DH], qsr[:, hh:hh + 1]),
                    waits=([qstok_rd[j]] if hh == 0 else []),
                    incs=[("dve", 1)])
            expq_rd = ("dve", qs["dve"])
            ek = S.emit("scalar", lambda e, j=j: e.activation(
                expk[j], p_mm[1][:, 0:512], AF.Exp),
                waits=[("pe", mmv[1]), expk_rd[j]], incs=[("act", 1)])
            qkv_free[1] = ("act", ek["act"])
            vt = S.emit("vector", lambda e, j=j: e.tensor_copy(
                v_tok[j].rearrange("p (g c) -> p g c", c=132)[:, :, 0:128],
                p_mm[2][:, 0:512].rearrange("p (g c) -> p g c", c=128)),
                waits=[("pe", mmv[2]), vtok_rd[j]], incs=[("dve", 1)])
            qkv_free[2] = ("dve", vt["dve"])
            cm = None
            for pr in range(4):
                w0 = [("act", ek["act"]), ("dve", vt["dve"])]
                if pr == 0 and t == 0:
                    w0.append(state["pc0_free"])
                cm = S.emit("tensor", lambda e, j=j, pr=pr, t=t: e.matmul(
                    ctx_banks[pr][:, 0:129],
                    expk[j][:, pr * P:(pr + 1) * P],
                    v_tok[j][:, pr * 132:pr * 132 + 129],
                    start=(t == 0), stop=(t == 3)),
                    waits=w0, incs=([("pe", 1)] if pr == 3 else []))
            expk_rd[j] = ("pe", cm["pe"])
            vtok_rd[j] = ("pe", cm["pe"])
            if t == 3:
                ca = None
                for pr in range(4):
                    ca = S.emit("vector", lambda e, pr=pr: e.tensor_add(
                        ctx_acc[:, pr * 132:pr * 132 + 129],
                        ctx_acc[:, pr * 132:pr * 132 + 129],
                        ctx_banks[pr][:, 0:129]),
                        waits=([("pe", cm["pe"])] if pr == 0 else []),
                        incs=[("dve", 1)])
                state["pc0_free"] = ("dve", ca["dve"])
                ctx_done = ("dve", ca["dve"])
            tv = None
            for c4 in range(4):
                tv = S.emit("tensor", lambda e, j=j, c4=c4: e.transpose(
                    pt[:, c4 * P:(c4 + 1) * P],
                    qs_tok[j][:, c4 * P:(c4 + 1) * P], ident_b),
                    waits=([("dve", qs["dve"]), state["pt_free"]]
                           if c4 == 0 else []),
                    incs=([("pe", 1)] if c4 == 3 else []))
            qstok_rd[j] = ("pe", tv["pe"])
            qf = S.emit("vector", lambda e, g=g: e.tensor_copy(
                qsoft.rearrange("p (c n) -> p c n", c=4)[:, :, g * P:(g + 1) * P],
                pt[:, 0:512].rearrange("p (c n) -> p c n", c=4)),
                waits=[("pe", tv["pe"])], incs=[("dve", 1)])
            state["pt_free"] = ("dve", qf["dve"])
            if g == NG - 1:
                tv = S.emit("tensor", lambda e: e.transpose(
                    ptf[0:1, 0:P], mx_xn[:, 0:1], ident_f),
                    waits=[mx_last, state["pt_free"]], incs=[("pe", 1)])
                rv = S.emit("vector", lambda e: e.tensor_reduce(
                    scal[0:1, 8:9], ptf[0:1, 0:P], AX.X, ALU.max),
                    waits=[("pe", tv["pe"])], incs=[("dve", 1)])
                state["pt_free"] = ("dve", rv["dve"])
                qv = _dma("gpsimd", gx_in, scal[0:1, 8:9],
                          waits=[("dve", rv["dve"])])
                cv1 = S.emit("gpsimd", lambda e: e.collective_compute(
                    "AllReduce", ALU.max,
                    replica_groups=[list(range(NCORE))],
                    ins=[gx_in.opt()], outs=[gx_out.opt()]),
                    waits=[("dg", qv["dg"])], incs=[("cc", 1)])
            emit_prep(3)

        emit_prep(len(prep_steps))

        barrier()
        qv = _dma("gpsimd", scal[0:1, 2:3], gx_out,
                  waits=[("cc", cv1["cc"])], sem="dc")
        gmax_x_ld = ("dc", qv["dc"])

        if PHASE_LIMIT < 2:
            return _finish(nc, S, ENGS, block_ctx=None, blocks=locals())

        xq_rd = [None, None]
        tqs_rd = [None, None]
        qxfm_rd = [None, None]
        qxfm_ok = {}

        def qx_half(sst, hs):
            j = (2 * sst + hs) % 2
            qv = _dma("sync",
                      xq_ld[j].rearrange("p (c n) -> p c n", c=4),
                      xn_hb[hs * 512:(hs + 1) * 512, sst * 512:(sst + 1) * 512]
                      .rearrange("(c p) n -> p c n", p=P),
                      waits=[xq_rd[j]])
            t1 = S.emit("vector", lambda e, j=j: e.tensor_scalar(
                tqs[j], xq_ld[j], pc[2][:, 0:1], MAGIC, ALU.mult, ALU.add),
                waits=[("ds", qv["ds"]), ("pe", bc_pe[0]), tqs_rd[j]],
                incs=[("dve", 1)])
            u1 = S.emit("vector", lambda e, j=j: e.tensor_scalar(
                xq_ld[j], tqs[j], MAGIC + QMAX, MAGIC - QMAX - 1.0,
                ALU.min, ALU.max), incs=[("dve", 1)])
            xq_rd[j] = ("dve", u1["dve"])
            tqs_rd[j] = ("dve", u1["dve"])
            w0 = [qxfm_rd[sst % 2]] if hs == 0 else []
            q1 = S.emit("vector", lambda e, sst=sst, hs=hs, j=j:
                        e.tensor_single_scalar(
                            qxfm[sst % 2][:, hs * 2048:(hs + 1) * 2048],
                            xq_ld[j], MAGIC, ALU.subtract),
                        waits=w0, incs=[("dve", 1)])
            if hs == 1:
                qxfm_ok[sst] = ("dve", q1["dve"])
            return q1

        cs = None
        for pr in range(4):
            rv = S.emit("vector", lambda e, pr=pr: e.reciprocal(
                stat[0][:, 6:7], ctx_acc[:, pr * 132 + 128:pr * 132 + 129]),
                waits=[ctx_done], incs=[("dve", 1)])
            S.emit("vector", waits=[("dve", rv["dve"])])
            for (r0, c0) in ((0, 0), (64, 64)):
                cs = S.emit("vector", lambda e, pr=pr, r0=r0, c0=c0:
                            e.tensor_scalar_mul(
                                ctx_sb[r0:r0 + 64,
                                       pr * 128 + c0:pr * 128 + c0 + 64],
                                ctx_acc[r0:r0 + 64,
                                        pr * 132 + c0:pr * 132 + c0 + 64],
                                stat[0][r0:r0 + 64, 6:7]),
                            incs=[("dve", 1)])
        sl = S.emit("vector", lambda e: e.tensor_copy(
            qx_tl[1][:, 0:1024], qsoft[:, 0:1024]),
            waits=[("dve", cs["dve"])], incs=[("dve", 1)])
        ctx_sb_ok = ("dve", sl["dve"])

        oa_rd = [None, None]
        ao_rd = [None, None]
        ap_free = [None, None]
        aoq_free = [None, None]
        last_attn_out_pe = 0
        for ns in range(NST):
            jn = ns % 2
            evs = None
            for pr in range(4):
                pb = p_mm[pr % 2]
                m = S.emit("tensor", lambda e, pr=pr, ns=ns, pb=pb: e.matmul(
                    pb[:, 0:512], ctx_sb[:, pr * 128:(pr + 1) * 128],
                    qsoft[:, pr * 4096 + ns * 512: pr * 4096 + (ns + 1) * 512],
                    start=True, stop=True),
                    waits=[ctx_sb_ok, ap_free[pr % 2]], incs=[("pe", 1)])
                last_attn_out_pe = m["pe"]
                evs = S.emit("vector", lambda e, pr=pr, jn=jn, pb=pb:
                             e.tensor_copy(
                                 oa_s[jn][:, pr * 512:(pr + 1) * 512],
                                 pb[:, 0:512]),
                             waits=[("pe", m["pe"])] + ([oa_rd[jn]]
                                                        if pr == 0 else []),
                             incs=[("dve", 1)])
                ap_free[pr % 2] = ("dve", evs["dve"])
            sl = S.emit("vector", lambda e, jn=jn: e.tensor_copy(
                qx_tl[0][:, 0:1024], oa_s[jn][:, 0:1024]),
                incs=[("dve", 1)])
            oa_ok = ("dve", sl["dve"])
            for nc2 in range(4):
                jb = nc2 % 2
                g2 = ns * 4 + nc2
                aov = None
                for half in range(2):
                    pb = p_mm[2 + ((nc2 * 2 + half) % 2)]
                    m = None
                    for ec in range(4):
                        w0 = []
                        if ec == 0:
                            w0 = [oa_ok, aoq_free[(nc2 * 2 + half) % 2]]
                        m = S.emit("tensor", lambda e, ec=ec, jn=jn, nc2=nc2,
                                   half=half, pb=pb: e.matmul(
                                       pb[:, 0:512],
                                       oa_s[jn][:, ec * 512 + nc2 * P:
                                                ec * 512 + (nc2 + 1) * P],
                                       wo_sb[:, ec * 1024 + half * 512:
                                             ec * 1024 + (half + 1) * 512],
                                       start=(ec == 0), stop=(ec == 3)),
                                   waits=w0,
                                   incs=([("pe", 1)] if ec == 3 else []))
                    oa_rd[jn] = ("pe", m["pe"])
                    aov = S.emit("vector", lambda e, jb=jb, half=half, pb=pb:
                                 e.tensor_tensor(
                                     ao_b[jb][:, half * 512:(half + 1) * 512],
                                     pb[:, 0:512],
                                     bo_bc[:, half * 512:(half + 1) * 512],
                                     ALU.add),
                                 waits=[("pe", m["pe"])] + ([ao_rd[jb]]
                                                            if half == 0 else []),
                                 incs=[("dve", 1)])
                    aoq_free[(nc2 * 2 + half) % 2] = ("dve", aov["dve"])
                sv = _dma("scalar", ao_hb[g2 * P:(g2 + 1) * P, :], ao_b[jb],
                          waits=[("dve", aov["dve"])], sem="so")
                ao_rd[jb] = ("so", sv["so"])
        lb = None
        for fc in range(8):
            lb = _dma("sync", qwgu[:, fc * I2:(fc + 1) * I2],
                      qgu_hb[fc * P:(fc + 1) * P, :],
                      waits=[("pe", last_attn_out_pe), gu["laststore"]])
        qwgu_ok = ("ds", lb["ds"])

        sg = S.emit("scalar", lambda e: e.activation(
            cst[0:1, 8:9], scal[0:1, 0:1], AF.Sigmoid), incs=[("act", 1)])
        t0 = S.emit("vector", lambda e: e.tensor_tensor(
            scal[0:1, 9:10], cst[0:1, 8:9], scal[0:1, 2:3], ALU.mult),
            waits=[("act", sg["act"]), gmax_x_ld], incs=[("dve", 1)])
        t1_ = S.emit("vector", lambda e: e.tensor_scalar_mul(
            scal[0:1, 10:11], scal[0:1, 9:10], 1.0 / QMAX),
            waits=[("dve", t0["dve"])], incs=[("dve", 1)])
        S.emit("vector", waits=[("dve", t1_["dve"])])
        fr = S.emit("vector", lambda e: e.reciprocal(
            scal[0:1, 11:12], scal[0:1, 10:11]), incs=[("dve", 1)])
        sxv = S.emit("vector", lambda e: e.tensor_scalar_mul(
            scal[0:1, 12:13], scal[0:1, 10:11], 1.0 / QMAX),
            waits=[("dve", fr["dve"])], incs=[("dve", 1)])
        S.emit("vector", waits=[("dve", sxv["dve"])])
        bcv = {}
        bc_pe = {}
        for (slot, col) in ((11, 0), (12, 1)):
            tv = S.emit("tensor", lambda e, slot=slot, col=col: e.matmul(
                pc[2][:, col:col + 1], ones_row, scal[0:1, slot:slot + 1],
                start=True, stop=True),
                waits=[("dve", sxv["dve"])], incs=[("pe", 1)])
            bc_pe[col] = tv["pe"]
            ev = S.emit("vector", lambda e, col=col: e.tensor_copy(
                bc_cols[:, col:col + 1], pc[2][:, col:col + 1]),
                waits=[("pe", tv["pe"])], incs=[("dve", 1)])
            bcv[col] = ev["dve"]
        rx = S.emit("scalar", lambda e: e.activation(
            cst[:, 10:11], bc_cols[:, 0:1], AF.Copy),
            waits=[("dve", bcv[0])], incs=[("act", 1)])
        rsx_col = cst[:, 10:11]
        gv = S.emit("vector", lambda e: e.tensor_scalar_mul(
            gsc_cols[:, 0:44], sgu_cols[:, 0:44], bc_cols[:, 1:2]),
            waits=[("dve", bcv[1])], incs=[("dve", 1)])
        gsc_ok = ("dve", gv["dve"])
        rsx_ok = ("act", rx["act"])

        for sst0 in range(2):
            for hs0 in range(2):
                qx_half(sst0, hs0)

        barrier()

        if PHASE_LIMIT < 3:
            return _finish(nc, S, ENGS, block_ctx=None, blocks=locals())
        wd = {"p1rd": {0: None, 1: None}, "p2rd": {0: None, 1: None},
              "yrd": None, "qord": {0: None, 1: None},
              "qlrd": {0: None, 1: None}, "rsbc": None, "laststore": None,
              "swdbc": None}

        def wd_steps():
            steps = []
            for ic in range(NIC):
                j = ic % 2

                def s_load(ic=ic, j=j):
                    qv = _dma("sync", wdb[j], wdt_hb[ic * P:(ic + 1) * P, :],
                              waits=[wd["p1rd"][j], wd["p2rd"][j]])
                    wd[("ld", ic)] = ("ds", qv["ds"])

                def s_max(ic=ic, j=j):
                    ab = S.emit("scalar", lambda e, s=wdb[j]: e.activation(
                        ywd, s, AF.Abs),
                        waits=[wd[("ld", ic)], wd["yrd"]], incs=[("act", 1)])
                    wd["p1rd"][j] = ("act", ab["act"])
                    mx = Mwd[:, 0:D]
                    if ic == 0:
                        qv = S.emit("vector", lambda e, mx=mx:
                                    e.tensor_copy(mx, ywd),
                                    waits=[("act", ab["act"])],
                                    incs=[("dve", 1)])
                    else:
                        qv = S.emit("vector", lambda e, mx=mx:
                                    e.tensor_tensor(mx, mx, ywd, ALU.max),
                                    waits=[("act", ab["act"])],
                                    incs=[("dve", 1)])
                    wd["yrd"] = ("dve", qv["dve"])
                    wd["m_wr"] = ("dve", qv["dve"])

                steps.append(s_load)
                steps.append(s_max)

            def s_scales():
                mrd = [wd["m_wr"]]
                for b in range(8):
                    blk = Mwd[:, b * P: (b + 1) * P]
                    tv = S.emit("tensor", lambda e, s=blk: e.transpose(
                        ptf[:, 0:P], s, ident_f),
                        waits=mrd + [state["pt_free"]], incs=[("pe", 1)])
                    rv = S.emit("vector", lambda e: e.tensor_reduce(
                        stat[0][:, 5:6], ptf[:, 0:P], AX.X, ALU.max),
                        waits=[("pe", tv["pe"])], incs=[("dve", 1)])
                    state["pt_free"] = ("dve", rv["dve"])
                    tv2 = S.emit("tensor", lambda e: e.transpose(
                        ptf[0:1, 0:P], stat[0][:, 5:6], ident_f),
                        waits=[("dve", rv["dve"]), state["pt_free"]],
                        incs=[("pe", 1)])
                    rv2 = S.emit("vector", lambda e, b=b: e.tensor_copy(
                        rs_row[0:1, b * P:(b + 1) * P], ptf[0:1, 0:P]),
                        waits=[("pe", tv2["pe"])], incs=[("dve", 1)])
                    state["pt_free"] = ("dve", rv2["dve"])
                S.emit("vector", waits=[("dve", rv2["dve"])])
                c2 = S.emit("vector", lambda e: e.tensor_copy(
                    rs_row[0:1, 1024:2048], rs_row[0:1, 0:1024]),
                    incs=[("dve", 1)])
                S.emit("vector", waits=[("dve", c2["dve"])])
                fr = S.emit("vector", lambda e: e.reciprocal(
                    rs_row[0:1, 1024:2048], rs_row[0:1, 1024:2048]),
                    incs=[("dve", 1)])
                rv3 = S.emit("vector", lambda e: e.tensor_scalar_mul(
                    rs_row[0:1, 1024:2048], rs_row[0:1, 1024:2048], QMAX),
                    waits=[("dve", fr["dve"])], incs=[("dve", 1)])
                pc1_free = ("dve", 0)
                for cb in range(0, 1024, 512):
                    tv3 = S.emit("tensor", lambda e, cb=cb: e.matmul(
                        pc[1][:, 0:512], ones_row,
                        rs_row[0:1, cb:cb + 512], start=True, stop=True),
                        waits=[("dve", rv3["dve"]), pc1_free],
                        incs=[("pe", 1)])
                    rv4 = S.emit("vector", lambda e, cb=cb: e.tensor_scalar_mul(
                        swd_bc[:, cb:cb + 512], pc[1][:, 0:512], 1.0 / QMAX),
                        waits=[("pe", tv3["pe"])], incs=[("dve", 1)])
                    tv4 = S.emit("tensor", lambda e, cb=cb: e.matmul(
                        pc[1][:, 0:512], ones_row,
                        rs_row[0:1, 1024 + cb:1024 + cb + 512],
                        start=True, stop=True),
                        waits=[("dve", rv4["dve"])], incs=[("pe", 1)])
                    rv5 = S.emit("vector", lambda e, cb=cb: e.tensor_copy(
                        rswd_bc[:, cb:cb + 512], pc[1][:, 0:512]),
                        waits=[("pe", tv4["pe"])], incs=[("dve", 1)])
                    pc1_free = ("dve", rv5["dve"])
                wd["rsbc"] = ("dve", rv5["dve"])

            steps.append(s_scales)

            for ic in range(NIC):
                j = ic % 2

                def s_load2(ic=ic, j=j):
                    qv = _dma("sync", wdb[j], wdt_hb[ic * P:(ic + 1) * P, :],
                              waits=[wd["p2rd"][j], wd["p1rd"][j]])
                    wd[("ld2", ic)] = ("ds", qv["ds"])

                def s_quant(ic=ic, j=j):
                    yv = S.emit("vector", lambda e, s=wdb[j]: e.tensor_tensor(
                        ywd, s, rswd_bc, ALU.mult),
                        waits=[wd[("ld2", ic)], wd["rsbc"], wd["yrd"]],
                        incs=[("dve", 1)])
                    wd["p2rd"][j] = ("dve", yv["dve"])
                    qv = S.emit("vector", lambda e, j=j: e.tensor_scalar(
                        qwd_o[j], ywd, MAGIC, MAGIC, ALU.add, ALU.subtract),
                        waits=[wd["qord"][j]],
                        incs=[("dve", 1)])
                    wd["yrd"] = ("dve", qv["dve"])
                    sv = _dma("scalar", qwd_hb[ic * P:(ic + 1) * P, :],
                              qwd_o[j], waits=[("dve", qv["dve"])], sem="so")
                    wd["qord"][j] = ("so", sv["so"])
                    wd["laststore"] = ("so", sv["so"])

                steps.append(s_load2)
                steps.append(s_quant)
            return steps

        wd_prep = wd_steps()
        wd_i = [0]

        def emit_wd(n):
            while n > 0 and wd_i[0] < len(wd_prep):
                wd_prep[wd_i[0]]()
                wd_i[0] += 1
                n -= 1

        gate_free = [None, None]
        up_free = [None, None]
        silu_rd = [None, None]
        hb_rd = [None, None]
        last_ffn1_pe = 0
        for sst in range(NST):
            for ic in range(NIC):
                jb = ic % 2
                m = None
                for fc in range(8):
                    w0 = []
                    if fc == 0:
                        w0 = [gate_free[jb], qxfm_ok[sst]]
                        if sst == 0 and ic == 0:
                            w0.append(qwgu_ok)
                    m = S.emit("tensor", lambda e, sst=sst, ic=ic, fc=fc, jb=jb:
                               e.matmul(
                                   p_mm[jb][:, 0:512],
                                   qwgu[:, fc * I2 + ic * P:
                                        fc * I2 + (ic + 1) * P],
                                   qxfm[sst % 2][:, fc * 512:(fc + 1) * 512],
                                   start=(fc == 0), stop=(fc == 7)),
                               waits=w0, incs=([("pe", 1)] if fc == 7 else []))
                g_pe = m["pe"]
                for fc in range(8):
                    w0 = [up_free[jb]] if fc == 0 else []
                    m = S.emit("tensor", lambda e, sst=sst, ic=ic, fc=fc, jb=jb:
                               e.matmul(
                                   p_mm[2 + jb][:, 0:512],
                                   qwgu[:, fc * I2 + (NIC + ic) * P:
                                        fc * I2 + (NIC + ic + 1) * P],
                                   qxfm[sst % 2][:, fc * 512:(fc + 1) * 512],
                                   start=(fc == 0), stop=(fc == 7)),
                               waits=w0, incs=([("pe", 1)] if fc == 7 else []))
                u_pe = m["pe"]
                last_ffn1_pe = u_pe
                sl = S.emit("scalar", lambda e, ic=ic, jb=jb: e.activation(
                    silu_b[jb], p_mm[jb][:, 0:512], AF.Silu,
                    scale=gsc_cols[:, ic:ic + 1]),
                    waits=[("pe", g_pe), gsc_ok, silu_rd[jb]],
                    incs=[("act", 1)])
                gate_free[jb] = ("act", sl["act"])
                jh = (ic // 2) % 2
                w0 = [("pe", u_pe), ("act", sl["act"])]
                if ic % 2 == 0:
                    w0.append(hb_rd[jh])
                hv = S.emit("vector", lambda e, ic=ic, jb=jb, jh=jh:
                            e.scalar_tensor_tensor(
                                h_b[jh][:, (ic % 2) * 512:(ic % 2 + 1) * 512],
                                p_mm[2 + jb][:, 0:512],
                                gsc_cols[:, NIC + ic:NIC + ic + 1],
                                silu_b[jb], ALU.mult, ALU.mult),
                            waits=w0, incs=[("dve", 1)])
                up_free[jb] = ("dve", hv["dve"])
                silu_rd[jb] = ("dve", hv["dve"])
                fr = S.emit("vector", lambda e, ic=ic, jh=jh: e.tensor_reduce(
                    stat[1][:, 5:6],
                    h_b[jh][:, (ic % 2) * 512:(ic % 2 + 1) * 512],
                    AX.X, ALU.max, apply_absolute_value=True),
                    incs=[("dve", 1)])
                mv = S.emit("vector", lambda e: e.tensor_max(
                    mx_h[:, 0:1], mx_h[:, 0:1], stat[1][:, 5:6]),
                    waits=[("dve", fr["dve"])], incs=[("dve", 1)])
                if ic % 2 == 1:
                    m0 = ic - 1
                    sv = _dma("scalar",
                              h_hb[m0 * P:(m0 + 2) * P,
                                   sst * 512:(sst + 1) * 512]
                              .rearrange("(c p) n -> p c n", p=P),
                              h_b[jh].rearrange("p (c n) -> p c n", c=2),
                              waits=[("dve", mv["dve"])], sem="so")
                    hb_rd[jh] = ("so", sv["so"])
                if ic in (5, 11, 16, 21):
                    emit_wd(2)
            qxfm_rd[sst % 2] = ("pe", last_ffn1_pe)
            if sst + 2 < NST:
                for hs0 in range(2):
                    qx_half(sst + 2, hs0)
        emit_wd(len(wd_prep))

        tv = S.emit("tensor", lambda e: e.transpose(
            ptf[0:1, 0:P], mx_h[:, 0:1], ident_f),
            waits=[("dve", S.cnt["dve"]), state["pt_free"]], incs=[("pe", 1)])
        rv = S.emit("vector", lambda e: e.tensor_reduce(
            scal[0:1, 13:14], ptf[0:1, 0:P], AX.X, ALU.max),
            waits=[("pe", tv["pe"])], incs=[("dve", 1)])
        state["pt_free"] = ("dve", rv["dve"])
        qv = _dma("gpsimd", gh_in, scal[0:1, 13:14], waits=[("dve", rv["dve"])])
        cv2 = S.emit("gpsimd", lambda e: e.collective_compute(
            "AllReduce", ALU.max, replica_groups=[list(range(NCORE))],
            ins=[gh_in.opt()], outs=[gh_out.opt()]),
            waits=[("dg", qv["dg"])], incs=[("cc", 1)])
        barrier()
        qv = _dma("gpsimd", scal[0:1, 3:4], gh_out, waits=[("cc", cv2["cc"])],
                  sem="dc")
        gmax_h_ld = ("dc", qv["dc"])

        if PHASE_LIMIT < 4:
            return _finish(nc, S, ENGS, block_ctx=None, blocks=locals())
        lb = _dma("sync",
                  qwd_sb.rearrange("p (c n) -> p c n", c=NIC),
                  qwd_hb.rearrange("(c p) n -> p c n", p=P),
                  waits=[("pe", last_ffn1_pe), wd["laststore"]])
        qwd_ok = ("ds", lb["ds"])

        sg = S.emit("scalar", lambda e: e.activation(
            cst[0:1, 9:10], scal[0:1, 1:2], AF.Sigmoid), incs=[("act", 1)])
        t0 = S.emit("vector", lambda e: e.tensor_tensor(
            scal[0:1, 14:15], cst[0:1, 9:10], scal[0:1, 3:4], ALU.mult),
            waits=[("act", sg["act"]), gmax_h_ld], incs=[("dve", 1)])
        t1_ = S.emit("vector", lambda e: e.tensor_scalar_mul(
            scal[0:1, 15:16], scal[0:1, 14:15], 1.0 / QMAX),
            waits=[("dve", t0["dve"])], incs=[("dve", 1)])
        S.emit("vector", waits=[("dve", t1_["dve"])])
        shv = S.emit("vector", lambda e: e.reciprocal(
            scal[0:1, 16:17], scal[0:1, 15:16]), incs=[("dve", 1)])
        bcv = {}
        for (slot, col) in ((16, 2), (15, 3)):
            tv = S.emit("tensor", lambda e, slot=slot, col=col: e.matmul(
                pc[2][:, col:col + 1], ones_row, scal[0:1, slot:slot + 1],
                start=True, stop=True),
                waits=[("dve", shv["dve"])], incs=[("pe", 1)])
            bc_pe[col] = tv["pe"]
            ev = S.emit("vector", lambda e, col=col: e.tensor_copy(
                bc_cols[:, col:col + 1], pc[2][:, col:col + 1]),
                waits=[("pe", tv["pe"])], incs=[("dve", 1)])
            bcv[col] = ev["dve"]
        rh = S.emit("scalar", lambda e: e.activation(
            cst[:, 11:12], bc_cols[:, 2:3], AF.Copy),
            waits=[("dve", bcv[2])], incs=[("act", 1)])
        rsh_col = cst[:, 11:12]
        rsh_ok = ("act", rh["act"])
        scv = S.emit("vector", lambda e: e.tensor_scalar_mul(
            sc_tile, swd_bc, bc_cols[:, 3:4]),
            waits=[("dve", bcv[3])], incs=[("dve", 1)])
        sc_ok = ("dve", scv["dve"])

        hr_rd = [None, None]
        qhs_rd = [None, None]
        qhs_ok = {}

        def emit_qh_pair(sst, m):
            j = m % 2
            qv = _dma("sync",
                      hr[j].rearrange("p (c n) -> p c n", c=2),
                      h_hb[2 * m * P:(2 * m + 2) * P,
                           sst * 512:(sst + 1) * 512]
                      .rearrange("(c p) n -> p c n", p=P),
                      waits=[hr_rd[j]])
            tv0 = S.emit("scalar", lambda e, j=j: e.activation(
                tqs[j][:, 0:1024], hr[j], AF.Copy,
                scale=rsh_col, bias=MAGIC),
                waits=[("ds", qv["ds"]), rsh_ok, tqs_rd[j]],
                incs=[("act", 1)])
            hr_rd[j] = ("act", tv0["act"])
            uv = S.emit("vector", lambda e, j=j: e.tensor_scalar(
                xq_ld[j][:, 0:1024], tqs[j][:, 0:1024],
                MAGIC + QMAX, MAGIC - QMAX - 1.0,
                ALU.min, ALU.max),
                waits=[("act", tv0["act"]), xq_rd[j]], incs=[("dve", 1)])
            tqs_rd[j] = ("dve", uv["dve"])
            w0 = [qhs_rd[sst % 2]] if m == 0 else []
            qv2 = S.emit("vector", lambda e, j=j, sst=sst, m=m:
                         e.tensor_single_scalar(
                             qh_s[sst % 2][:, m * 1024:(m + 1) * 1024],
                             xq_ld[j][:, 0:1024], MAGIC, ALU.subtract),
                         waits=w0, incs=[("dve", 1)])
            xq_rd[j] = ("dve", qv2["dve"])
            if m == 10:
                sl = S.emit("vector", lambda e, sst=sst: e.tensor_copy(
                    qx_tl[0], qh_s[sst % 2][:, 10 * 1024:11 * 1024]),
                    incs=[("dve", 1)])
                qhs_ok[sst] = ("dve", sl["dve"])

        for m in range(11):
            emit_qh_pair(0, m)
        for m in range(11):
            emit_qh_pair(1, m)

        if PHASE_LIMIT < 5:
            return _finish(nc, S, ENGS, block_ctx=None, blocks=locals())
        dn_free = [None, None, None, None]
        out_pend = []
        asm_rd = [None, None]
        asm2_rd = [None, None]
        outb_rd = [None, None]
        xr_rd = [None, None]
        aor_rd = [None, None]
        for sst in range(NST):
            for nc2 in range(4):
                jn = nc2 % 2
                g2 = sst * 4 + nc2
                qv = _dma("scalar", xr[jn], x_hb[g2 * P:(g2 + 1) * P, :],
                          waits=[xr_rd[jn]], sem="dl")
                x_ld = ("dl", qv["dl"])
                qa = _dma("scalar", aor[jn], ao_hb[g2 * P:(g2 + 1) * P, :],
                          waits=[aor_rd[jn]], sem="dl")
                ao_ld = ("dl", qa["dl"])
                for half in range(2):
                    bk = (nc2 * 2 + half) % 4
                    m = None
                    for ic in range(NIC):
                        w0 = []
                        if ic == 0:
                            w0 = [dn_free[bk], qhs_ok[sst]]
                            if sst == 0 and nc2 == 0 and half == 0:
                                w0.append(qwd_ok)
                        m = S.emit("tensor", lambda e, sst=sst, nc2=nc2,
                                   half=half, ic=ic, bk=bk: e.matmul(
                                       p_mm[bk][:, 0:512],
                                       qh_s[sst % 2][:, ic * 512 + nc2 * P:
                                                     ic * 512 + (nc2 + 1) * P],
                                       qwd_sb[:, ic * 1024 + half * 512:
                                              ic * 1024 + (half + 1) * 512],
                                       start=(ic == 0), stop=(ic == NIC - 1)),
                                   waits=w0,
                                   incs=([("pe", 1)] if ic == NIC - 1 else []))
                    w0 = [("pe", m["pe"]), sc_ok]
                    if half == 0:
                        w0.append(asm_rd[jn])
                    t1 = S.emit("vector", lambda e, jn=jn, half=half, bk=bk:
                                e.tensor_tensor(
                                    asm[jn][:, half * 512:(half + 1) * 512],
                                    p_mm[bk][:, 0:512],
                                    sc_tile[:, half * 512:(half + 1) * 512],
                                    ALU.mult),
                                waits=w0, incs=[("dve", 1)])
                    dn_free[bk] = ("dve", t1["dve"])
                t2 = S.emit("vector", lambda e, jn=jn: e.tensor_tensor(
                    asm2[jn], asm[jn], aor[jn], ALU.add),
                    waits=[ao_ld, asm2_rd[jn]],
                    incs=[("dve", 1)])
                asm_rd[jn] = ("dve", t2["dve"])
                aor_rd[jn] = ("dve", t2["dve"])
                t3 = S.emit("vector", lambda e, jn=jn: e.tensor_add(
                    outb[jn], asm2[jn], xr[jn]),
                    waits=[x_ld, outb_rd[jn]],
                    incs=[("dve", 1)])
                asm2_rd[jn] = ("dve", t3["dve"])
                xr_rd[jn] = ("dve", t3["dve"])
                out_pend.append((g2, jn, t3["dve"]))
                if len(out_pend) > 1:
                    og, oj, ot = out_pend.pop(0)
                    sv = _dma("scalar", out_hb[og * P:(og + 1) * P, :], outb[oj],
                              waits=[("dve", t3["dve"])], sem="so")
                    outb_rd[oj] = ("so", sv["so"])
                last_ffn2_pe = m["pe"]
            qhs_rd[sst % 2] = ("pe", last_ffn2_pe)
            if sst + 2 < NST:
                for m2 in range(11):
                    emit_qh_pair(sst + 2, m2)

        dwaits = [("dve", S.cnt["dve"]), ("act", S.cnt["act"]),
                  ("pe", S.cnt["pe"]), ("pool", S.cnt["pool"])]

        def _dbg_dma(dst, srcv, waits=()):
            if not globals().get("DEBUG_IO"):
                return
            S.emit("gpsimd", lambda e, d=dst, s=srcv: e.dma_start(
                out=d, in_=s, allow_slow_non_contiguous=True),
                waits=waits, incs=[("dg", 16)])
        _dbg_dma(dbg_hb[0:1, 0:16], scal[0:1, 0:16], dwaits)
        _dbg_dma(dbg_hb[0:P, 16:20], bc_cols[0:P, 0:4])
        _dbg_dma(dbg_hb[0:P, 20:21], mx_xn[0:P, 0:1])
        _dbg_dma(dbg_hb[0:P, 21:22], mx_h[0:P, 0:1])
        _dbg_dma(dbg_hb[0:P, 22:26], sgu_cols[0:P, 0:4])
        _dbg_dma(dbg_hb[0:P, 26:32], stat[1][0:P, 4:10])
        _dbg_dma(dbg2_hb[0:P, 0:16], stat[0][0:P, 0:16])
        _dbg_dma(dbg2_hb[0:P, 16:32], stat[1][0:P, 0:16])
        _dbg_dma(dbg2_hb[0:1, 32:56], scal[0:1, 0:24])
        _dbg_dma(dbg2_hb[0:P, 56:60], qsr[0:P, 0:4])
        _dbg_dma(dbg2_hb[0:P, 60:64], gsc_cols[0:P, 0:4])

        while out_pend:
            og, oj, ot = out_pend.pop(0)
            d1 = S.emit("vector", lambda e: e.tensor_copy(qx_tl[0], qh_s[1][:, 0:1024]),
                        waits=[("dve", ot)], incs=[("dve", 1)])
            d2 = S.emit("vector", lambda e: e.tensor_copy(qx_tl[1], qh_s[1][:, 0:1024]),
                        incs=[("dve", 1)])
            sv = _dma("scalar", out_hb[og * P:(og + 1) * P, :], outb[oj],
                      waits=[("dve", d2["dve"])], sem="so")
            outb_rd[oj] = ("so", sv["so"])

        so_total = S.cnt["so"]
        dg_total = S.cnt["dg"]
        for eng in ("scalar", "vector", "gpsimd"):
            S.emit(eng, waits=[("so", so_total), ("dg", dg_total)])

        with nc.Block() as block:
            @block.sync
            def _(e):
                for f in S.ops["sync"]:
                    f(e)

            @block.gpsimd
            def _(e):
                for f in S.ops["gpsimd"]:
                    f(e)

            @block.vector
            def _(e):
                for f in S.ops["vector"]:
                    f(e)

            @block.scalar
            def _(e):
                for f in S.ops["scalar"]:
                    f(e)

            @block.tensor
            def _(e):
                for f in S.ops["tensor"]:
                    f(e)

    return nc


_NC_CACHE = None


def kernel(L0_attn_x, L0_attn_Wq, L0_attn_Wk, L0_attn_Wv, L0_attn_Wo,
           L0_attn_bo, L0_ffn_clip_x, L0_ffn_clip_down, L0_ffn_w_gate,
           L0_ffn_w_up, L0_ffn_w_down):
    global _NC_CACHE
    x = np.ascontiguousarray(np.asarray(L0_attn_x, dtype=np.float32))
    wq = np.ascontiguousarray(np.asarray(L0_attn_Wq, dtype=np.float32))
    wk = np.ascontiguousarray(np.asarray(L0_attn_Wk, dtype=np.float32))
    wv = np.ascontiguousarray(np.asarray(L0_attn_Wv, dtype=np.float32))
    wo = np.ascontiguousarray(np.asarray(L0_attn_Wo, dtype=np.float32))
    bo = np.ascontiguousarray(np.asarray(L0_attn_bo, dtype=np.float32)
                              .reshape(1, D))
    cx = np.ascontiguousarray(np.asarray(L0_ffn_clip_x, dtype=np.float32)
                              .reshape(1, 1))
    cd = np.ascontiguousarray(np.asarray(L0_ffn_clip_down, dtype=np.float32)
                              .reshape(1, 1))
    wg = np.asarray(L0_ffn_w_gate, dtype=np.float32)
    wu = np.asarray(L0_ffn_w_up, dtype=np.float32)
    wdn = np.asarray(L0_ffn_w_down, dtype=np.float32)
    wgu = np.ascontiguousarray(np.concatenate([wg, wu], axis=0).T)
    wdt = np.ascontiguousarray(wdn.T)

    if _NC_CACHE is None:
        _NC_CACHE = build_nc()
    nc = _NC_CACHE

    in_maps = []
    for i in range(NCORE):
        in_maps.append({
            "x": np.ascontiguousarray(x[i]),
            "wq": wq, "wk": wk, "wv": wv, "wo": wo, "bo": bo,
            "cx": cx, "cd": cd, "wgu": wgu, "wdt": wdt,
        })
    res = run_bass_kernel_spmd(nc, in_maps, core_ids=list(range(NCORE)))
    return np.stack([np.asarray(res.results[i]["out"]) for i in range(NCORE)],
                    axis=0)


# version 24 (final)
# speedup vs baseline: 1.0411x; 1.0411x over previous
import numpy as np
from contextlib import ExitStack

import concourse.bass as bass
import concourse.mybir as mybir
from concourse.bass_utils import run_bass_kernel_spmd

F32 = mybir.dt.float32
BF16 = mybir.dt.bfloat16
U16 = mybir.dt.uint16
AF = mybir.ActivationFunctionType
ALU = mybir.AluOpType
AX = mybir.AxisListType

P = 128
NTOK = 4096
D = 1024
HID = 512
NH = 8
DH = 64
I1 = 2816
I2 = 2 * I1
NIC = I1 // P
NG = NTOK // P
NST = NTOK // 512
MAGIC = 12582912.0
EPS = 1e-5
SCALE = DH ** -0.5
QMAX = 127.0
NCORE = 8

ENGS = ("sync", "gpsimd", "vector", "scalar", "tensor")


class Sched:

    def __init__(self):
        self.ops = {e: [] for e in ENGS}
        self.cnt = {}
        self.sems = None

    def emit(self, eng, build=None, waits=(), incs=()):
        post = {}
        for s, d in incs:
            self.cnt[s] = self.cnt.get(s, 0) + d
            post[s] = self.cnt[s]
        waits = [w for w in waits if w is not None and w[1] > 0]

        def run(e, build=build, waits=tuple(waits), incs=tuple(incs)):
            for s, v in waits:
                e.wait_ge(self.sems[s], v)
            ins = build(e) if build is not None else None
            for s, d in incs:
                ins.then_inc(self.sems[s], d)

        self.ops[eng].append(run)
        return post


S = None


def _dma(queue, out, in_, waits=(), sem=None):
    if sem is None:
        sem = "dg" if queue == "gpsimd" else "ds"
    return S.emit(queue, lambda e, o=out, i=in_: e.dma_start(out=o, in_=i),
                  waits=waits, incs=[(sem, 16)])




def _finish(nc, S, engs, block_ctx, blocks):
    dq = [("ds", S.cnt.get("ds", 0)), ("dg", S.cnt.get("dg", 0)),
          ("so", S.cnt.get("so", 0)), ("dc", S.cnt.get("dc", 0)),
          ("dl", S.cnt.get("dl", 0))]
    for eng in engs:
        S.emit(eng, waits=dq)
    with nc.Block() as block:
        @block.sync
        def _(e):
            for f in S.ops["sync"]:
                f(e)

        @block.gpsimd
        def _(e):
            for f in S.ops["gpsimd"]:
                f(e)

        @block.vector
        def _(e):
            for f in S.ops["vector"]:
                f(e)

        @block.scalar
        def _(e):
            for f in S.ops["scalar"]:
                f(e)

        @block.tensor
        def _(e):
            for f in S.ops["tensor"]:
                f(e)
    return nc

PHASE_LIMIT = 99


def build_nc():
    global S
    nc = bass.Bass("TRN2", target_bir_lowering=False)

    x_hb = nc.dram_tensor("x", [NTOK, D], F32, kind="ExternalInput").ap()
    wq_hb = nc.dram_tensor("wq", [D, HID], F32, kind="ExternalInput").ap()
    wk_hb = nc.dram_tensor("wk", [D, HID], F32, kind="ExternalInput").ap()
    wv_hb = nc.dram_tensor("wv", [D, HID], F32, kind="ExternalInput").ap()
    wo_hb = nc.dram_tensor("wo", [HID, D], F32, kind="ExternalInput").ap()
    bo_hb = nc.dram_tensor("bo", [1, D], F32, kind="ExternalInput").ap()
    cx_hb = nc.dram_tensor("cx", [1, 1], F32, kind="ExternalInput").ap()
    cd_hb = nc.dram_tensor("cd", [1, 1], F32, kind="ExternalInput").ap()
    wgu_hb = nc.dram_tensor("wgu", [D, I2], F32, kind="ExternalInput").ap()
    wdt_hb = nc.dram_tensor("wdt", [I1, D], F32, kind="ExternalInput").ap()
    out_hb = nc.dram_tensor("out", [NTOK, D], F32, kind="ExternalOutput").ap()
    dbg_hb = nc.dram_tensor("dbg", [P, 32], F32, kind="ExternalOutput").ap()
    dbg2_hb = nc.dram_tensor("dbg2", [P, 64], F32, kind="ExternalOutput").ap()

    _dbg_kind = "ExternalOutput" if globals().get("DEBUG_IO") else "Internal"
    qx_dbg = nc.dram_tensor("qx_dbg", [NTOK, D], BF16, kind=_dbg_kind).ap()
    qf_dbg = nc.dram_tensor("qf_dbg", [P, 8 * 4096], BF16, kind=_dbg_kind).ap()
    xn_hb = nc.dram_tensor("xn_scr", [D, NTOK], F32, kind=_dbg_kind).ap()
    h_hb = nc.dram_tensor("h_scr", [I1, NTOK], F32, kind=_dbg_kind).ap()
    ao_hb = nc.dram_tensor("ao_scr", [NTOK, D], BF16, kind=_dbg_kind).ap()
    qgu_hb = nc.dram_tensor("qgu_scr", [D, I2], BF16, kind=_dbg_kind).ap()
    qwd_hb = nc.dram_tensor("qwd_scr", [I1, D], BF16, kind=_dbg_kind).ap()
    gx_in = nc.dram_tensor("gx_in", [1, 1], F32, kind="Internal").ap()
    gx_out = nc.dram_tensor("gx_out", [1, 1], F32, kind="Internal").ap()
    gh_in = nc.dram_tensor("gh_in", [1, 1], F32, kind="Internal").ap()
    gh_out = nc.dram_tensor("gh_out", [1, 1], F32, kind="Internal").ap()

    with ExitStack() as stack:
        ent = stack.enter_context

        ARENA_COLS = 51456
        arena = ent(nc.sbuf_tensor("arena", [P, ARENA_COLS], F32))
        spans = []

        def alloc(name, cols, t0, t1, dtype=F32, parts=P):
            cols = (cols + 7) & ~7
            taken = sorted((o, c) for (o, c, a, b, n) in spans
                           if not (b < t0 or a > t1))
            off = 0
            for o, c in taken:
                if off + cols <= o:
                    break
                off = max(off, o + c)
            assert off + cols <= ARENA_COLS, \
                f"arena overflow: {name} needs {cols} cols at {off}"
            spans.append((off, cols, t0, t1, name))
            globals()["DEBUG_SPANS"] = {n: (o, c) for (o, c, a, b, n) in spans}
            view = arena[0:parts, off:off + cols]
            if dtype == BF16:
                view = view.bitcast(BF16)
            return view

        ident_b = alloc("ident_b", 64, 0, 7, BF16)
        ident_f = alloc("ident_f", 128, 0, 7)
        ones_row = alloc("ones_row", 128, 0, 7, parts=1)
        wq_sb = alloc("wq_sb", 2048, 0, 1, BF16)
        wk_sb = alloc("wk_sb", 2048, 0, 1, BF16)
        wv_sb = alloc("wv_sb", 2048, 0, 1, BF16)
        wo_sb = alloc("wo_sb", 2048, 0, 3, BF16)
        bo_bc = alloc("bo_bc", D, 0, 3)
        sgu_cols = alloc("sgu_cols", 48, 0, 7)
        gsc_cols = alloc("gsc_cols", 48, 0, 7)
        swd_bc = alloc("swd_bc", D, 4, 5)
        sc_tile = alloc("sc_tile", D, 5, 6)
        rswd_bc = alloc("rswd_bc", D, 4, 5)
        mx_xn = alloc("mx_xn", 8, 0, 7)
        mx_h = alloc("mx_h", 8, 0, 7)
        ctx_acc = alloc("ctx_acc", 4 * 132, 0, 7)
        ctx_sb = alloc("ctx_sb", 4 * 64, 0, 7, BF16)
        v_tok = [alloc(f"v_tok{j}", 264, 0, 1, BF16) for j in range(2)]
        expk = [alloc(f"expk{j}", 256, 0, 1, BF16) for j in range(2)]
        expq = alloc("expq", 512, 0, 1)
        qs_tok = [alloc(f"qs_tok{j}", 256, 0, 1, BF16) for j in range(2)]
        stat = [alloc(f"stat{j}", 16, 0, 7) for j in range(2)]
        sqd = alloc("sqd", D, 1, 1)
        qsr = alloc("qsr", 8, 0, 7)
        qsums = alloc("qsums", 8, 0, 7)
        hist = alloc("hist", 64, 0, 7)
        cst = alloc("cst", 16, 0, 7)
        barsc = alloc("barsc", 8, 0, 7)
        bc_cols = alloc("bc_cols", 8, 0, 7)
        scal = alloc("scal", 24, 0, 7, parts=1)
        rs_row = alloc("rs_row", 2048, 0, 5, parts=1)
        qx_tl = [alloc(f"qx_tl{j}", 512, 2, 6, BF16) for j in range(2)]
        xq_ld = [alloc(f"xq_ld{j}", 2048, 2, 6) for j in range(2)]
        tqs = [alloc(f"tqs{j}", 2048, 2, 6) for j in range(2)]

        xA = [alloc(f"xA{j}", D, 1, 1) for j in range(2)]
        xnA = [alloc(f"xnA{j}", D, 1, 1) for j in range(2)]
        xnfm = [alloc(f"xnfm{j}", 512, 1, 1, BF16) for j in range(2)]
        xnf_st = [alloc(f"xnf_st{j}", 1024, 1, 1) for j in range(2)]
        wgub = [alloc(f"wgub{j}", 1408, 1, 1) for j in range(2)]
        ygu = alloc("ygu", 1408, 1, 1)
        Mgu = alloc("Mgu", I1, 1, 1)
        rsgu_bc = alloc("rsgu_bc", I1, 1, 1)
        qgu_o = [alloc(f"qgu_o{j}", 704, 1, 1, BF16) for j in range(2)]
        qsoft = alloc("qsoft", 8192, 1, 2, BF16)

        oa_s = [alloc(f"oa_s{j}", 1024, 2, 3, BF16) for j in range(2)]
        ao_b = [alloc(f"ao_b{j}", 512, 2, 3, BF16) for j in range(2)]

        qwgu = alloc("qwgu", 8 * I2 // 2, 3, 4, BF16)
        qxfm = [alloc(f"qxfm{j}", 2048, 2, 4, BF16) for j in range(2)]
        silu_b = [alloc(f"silu{j}", 512, 4, 4) for j in range(2)]
        h_b = [alloc(f"h_b{j}", D, 4, 4) for j in range(2)]
        wdb = [alloc(f"wdb{j}", D, 4, 5) for j in range(2)]
        ywd = alloc("ywd", D, 4, 5)
        Mwd = alloc("Mwd", 2 * D, 4, 5)
        qwd_o = [alloc(f"qwd_o{j}", 512, 4, 5, BF16) for j in range(2)]

        qwd_sb = alloc("qwd_sb", NIC * 512, 5, 6, BF16)
        qh_s = [alloc(f"qh_s{j}", NIC * 256, 5, 6, BF16) for j in range(2)]
        hr = [alloc(f"hr{j}", D, 5, 6) for j in range(2)]
        xr = [alloc(f"xr{j}", D, 6, 6) for j in range(2)]
        aor = [alloc(f"aor{j}", 512, 6, 6, BF16) for j in range(2)]
        asm = [alloc(f"asm{j}", D, 6, 6) for j in range(2)]
        asm2 = [alloc(f"asm2{j}", D, 6, 6) for j in range(2)]
        outb = [alloc(f"outb{j}", D, 6, 6) for j in range(2)]

        p_mm = [ent(nc.psum_tensor(f"p{i}", [P, 512], F32)) for i in range(4)]
        pt = ent(nc.psum_tensor("pt", [P, 1024], BF16))
        ptf = pt.bitcast(F32)
        pc = [ent(nc.psum_tensor(f"pc{i}", [P, 512], F32)) for i in range(3)]
        ctx_banks = [pc[0], pc[1], pc[2], p_mm[3]]

        sem_names = ["ds", "dg", "pe", "act", "dve", "pool", "cc", "so",
                     "bar", "dc", "dl"]
        sem_handles = {n: ent(nc.semaphore(n)) for n in sem_names}
        S = Sched()
        S.sems = sem_handles
        globals()["S"] = S

        state = {
            "pt_free": ("dve", 0),
            "pc0_free": ("dve", 0),
        }

        def barrier():
            dq = [("ds", S.cnt.get("ds", 0)), ("dg", S.cnt.get("dg", 0)),
                  ("so", S.cnt.get("so", 0)), ("dl", S.cnt.get("dl", 0))]
            S.emit("vector", lambda e: e.memset(barsc[:, 0:1], 0.0),
                   waits=dq, incs=[("bar", 1)])
            S.emit("gpsimd", lambda e: e.memset(barsc[:, 1:2], 0.0),
                   waits=dq, incs=[("bar", 1)])
            S.emit("scalar", lambda e: e.memzero(barsc[:, 2:3]),
                   waits=dq, incs=[("bar", 1)])
            S.emit("tensor", lambda e: e.nop(), waits=dq, incs=[("bar", 1)])
            S.emit("sync", lambda e: e.nop(), waits=dq, incs=[("bar", 1)])
            bv = S.cnt["bar"]
            for eng in ENGS:
                S.emit(eng, waits=[("bar", bv)])

        S.emit("gpsimd", lambda e: e.memset(ident_b, 0.0))
        S.emit("gpsimd", lambda e: e.affine_select(
            out=ident_b, in_=ident_b, compare_op=ALU.not_equal, fill=1.0,
            base=0, pattern=[[-1, P]], channel_multiplier=1))
        S.emit("gpsimd", lambda e: e.memset(ident_f, 0.0))
        S.emit("gpsimd", lambda e: e.affine_select(
            out=ident_f, in_=ident_f, compare_op=ALU.not_equal, fill=1.0,
            base=0, pattern=[[-1, P]], channel_multiplier=1))
        iv = S.emit("gpsimd", lambda e: e.memset(ones_row, 1.0),
                    incs=[("pool", 1)])
        ident_ready = ("pool", iv["pool"])
        S.emit("vector", lambda e: e.memset(cst[:, 0:1], EPS))
        cm = S.emit("vector", lambda e: e.memset(cst[:, 1:2], MAGIC),
                    incs=[("dve", 1)])
        nc.const_aps.register(F32, MAGIC, cst[:, 1:2])
        cst_ready = ("dve", cm["dve"])
        for buf in (stat[0], stat[1], scal, cst[0:P, 2:16], qsr, qsums, hist,
                    bc_cols, mx_xn[:, 0:1], mx_h[:, 0:1], ctx_acc, ctx_sb):
            S.emit("vector", lambda e, b=buf: e.memset(b, 0.0),
                   incs=[("dve", 1)])
        for j in range(2):
            ms = S.emit("vector", lambda e, jj=j: e.memset(
                v_tok[jj].rearrange("p (g c) -> p g c", c=132)[:, :, 128:129],
                1.0), incs=[("dve", 1)])
        memset_done = ("dve", ms["dve"])

        for (w_sb, w_hb, nchunk) in ((wq_sb, wq_hb, 8), (wk_sb, wk_hb, 8),
                                     (wv_sb, wv_hb, 8), (wo_sb, wo_hb, 4)):
            qv = _dma("gpsimd",
                      w_sb.rearrange("p (f n) -> p f n", f=nchunk),
                      w_hb.rearrange("(f p) n -> p f n", p=P))
        w_loaded = ("dg", qv["dg"])
        S.emit("scalar", lambda e: e.activation(cst[:, 12:13], cst[:, 1:2],
                                                AF.Square),
               waits=[cst_ready])
        S.emit("scalar", lambda e: e.activation(cst[:, 12:13], cst[:, 1:2],
                                                AF.Ln))
        S.emit("scalar", lambda e: e.activation(cst[:, 12:13], cst[:, 1:2],
                                                AF.Exp, scale=-1e-6))
        wu = S.emit("scalar", lambda e: e.activation(cst[:, 13:14],
                                                     cst[:, 0:1], AF.Copy),
                    incs=[("act", 1)])
        act_warm = ("act", wu["act"])
        qv = _dma("sync", scal[0:1, 0:1], cx_hb)
        qv = _dma("sync", scal[0:1, 1:2], cd_hb)
        qv = _dma("sync", rs_row[0:1, 0:D], bo_hb)
        bo_row_v = ("ds", qv["ds"])
        ev = None
        for cb in range(2):
            pv = S.emit("tensor", lambda e, cb=cb: e.matmul(
                p_mm[cb][:, 0:512], ones_row,
                rs_row[0:1, cb * 512:(cb + 1) * 512], start=True, stop=True),
                waits=[bo_row_v, ident_ready], incs=[("pe", 1)])
            ev = S.emit("vector", lambda e, cb=cb: e.tensor_copy(
                bo_bc[:, cb * 512:(cb + 1) * 512], p_mm[cb][:, 0:512]),
                waits=[("pe", pv["pe"])], incs=[("dve", 1)])
        qkv_free = [("dve", ev["dve"] - 1), ("dve", ev["dve"]), ("dve", 0)]

        gu = {"p1rd": {0: None, 1: None}, "p2rd": {0: None, 1: None},
              "yrd": None, "qo_j": 1, "qord": {0: None, 1: None},
              "qlrd": {0: None, 1: None}, "rsbc": {}, "laststore": None}

        def gu_steps():
            steps = []
            for qt in range(4):
                col0 = qt * 1408
                hcol = (qt % 2) * 1408
                ic_base = qt * 11
                for fc in range(8):
                    j = fc % 2

                    def s_load(qt=qt, fc=fc, j=j, col0=col0):
                        qv = _dma("sync", wgub[j],
                                  wgu_hb[fc * P:(fc + 1) * P, col0:col0 + 1408],
                                  waits=[gu["p1rd"][j], gu["p2rd"][j]])
                        gu[("ld", fc)] = ("ds", qv["ds"])

                    def s_max(fc=fc, j=j):
                        ab = S.emit("scalar", lambda e, s=wgub[j]: e.activation(
                            ygu, s, AF.Abs),
                            waits=[gu[("ld", fc)], gu["yrd"]],
                            incs=[("act", 1)])
                        gu["p1rd"][j] = ("act", ab["act"])
                        mx = Mgu[:, 0:1408]
                        if fc == 0:
                            qv = S.emit("vector", lambda e, mx=mx:
                                        e.tensor_copy(mx, ygu),
                                        waits=[("act", ab["act"]),
                                               gu.get("m_rd")],
                                        incs=[("dve", 1)])
                        else:
                            qv = S.emit("vector", lambda e, mx=mx:
                                        e.tensor_tensor(mx, mx, ygu, ALU.max),
                                        waits=[("act", ab["act"])],
                                        incs=[("dve", 1)])
                        gu["yrd"] = ("dve", qv["dve"])
                        gu["m_wr"] = ("dve", qv["dve"])

                    steps.append(s_load)
                    steps.append(s_max)

                def s_scales(qt=qt, hcol=hcol, ic_base=ic_base):
                    mrd = [gu["m_wr"]]
                    for b in range(11):
                        blk = Mgu[:, b * P: (b + 1) * P]
                        tv = S.emit("tensor", lambda e, s=blk: e.transpose(
                            ptf[:, 0:P], s, ident_f),
                            waits=mrd + [state["pt_free"]], incs=[("pe", 1)])
                        col = sgu_cols[:, ic_base + b: ic_base + b + 1]
                        rv = S.emit("vector", lambda e, c=col: e.tensor_reduce(
                            c, ptf[:, 0:P], AX.X, ALU.max),
                            waits=[("pe", tv["pe"])], incs=[("dve", 1)])
                        state["pt_free"] = ("dve", rv["dve"])
                        tv2 = S.emit("tensor", lambda e, c=col: e.transpose(
                            ptf[0:1, 0:P], c, ident_f),
                            waits=[("dve", rv["dve"]), state["pt_free"]],
                            incs=[("pe", 1)])
                        rv2 = S.emit("vector", lambda e, b=b: e.tensor_copy(
                            rs_row[0:1, b * P:(b + 1) * P], ptf[0:1, 0:P]),
                            waits=[("pe", tv2["pe"])], incs=[("dve", 1)])
                        state["pt_free"] = ("dve", rv2["dve"])
                        gu["m_rd"] = ("pe", tv2["pe"])
                    S.emit("vector", waits=[("dve", rv2["dve"])])
                    fr = S.emit("vector", lambda e: e.reciprocal(
                        rs_row[0:1, 0:1408], rs_row[0:1, 0:1408]),
                        incs=[("dve", 1)])
                    rv3 = S.emit("vector", lambda e: e.tensor_scalar_mul(
                        rs_row[0:1, 0:1408], rs_row[0:1, 0:1408], QMAX),
                        waits=[("dve", fr["dve"])], incs=[("dve", 1)])
                    last = None
                    for cb in range(0, 1408, 512):
                        n = min(512, 1408 - cb)
                        tv3 = S.emit("tensor", lambda e, cb=cb, n=n: e.matmul(
                            p_mm[0][:, 0:n], ones_row, rs_row[0:1, cb:cb + n],
                            start=True, stop=True),
                            waits=[("dve", rv3["dve"]), qkv_free[0]],
                            incs=[("pe", 1)])
                        last = S.emit("vector", lambda e, cb=cb, n=n, hcol=hcol:
                                      e.tensor_copy(
                                          rsgu_bc[:, hcol + cb:hcol + cb + n],
                                          p_mm[0][:, 0:n]),
                                      waits=[("pe", tv3["pe"])],
                                      incs=[("dve", 1)])
                        qkv_free[0] = ("dve", last["dve"])
                    gu["rsbc"][qt] = ("dve", last["dve"])

                steps.append(s_scales)

                for fc in range(8):
                    j = fc % 2

                    def s_load2(qt=qt, fc=fc, j=j, col0=col0):
                        qv = _dma("sync", wgub[j],
                                  wgu_hb[fc * P:(fc + 1) * P, col0:col0 + 1408],
                                  waits=[gu["p2rd"][j], gu["p1rd"][j]])
                        gu[("ld2", fc)] = ("ds", qv["ds"])

                    def s_quant(qt=qt, fc=fc, j=j, hcol=hcol, col0=col0):
                        yv = S.emit("vector", lambda e, s=wgub[j], hcol=hcol:
                                    e.tensor_tensor(
                                        ygu, s, rsgu_bc[:, hcol:hcol + 1408],
                                        ALU.mult),
                                    waits=[gu[("ld2", fc)], gu["rsbc"][qt],
                                           gu["yrd"]],
                                    incs=[("dve", 1)])
                        gu["p2rd"][j] = ("dve", yv["dve"])
                        jo = gu["qo_j"] = 1 - gu["qo_j"]
                        qv = S.emit("vector", lambda e, jo=jo: e.tensor_scalar(
                            qgu_o[jo], ygu, MAGIC, MAGIC, ALU.add, ALU.subtract),
                            waits=[gu["qord"][jo]],
                            incs=[("dve", 1)])
                        gu["yrd"] = ("dve", qv["dve"])
                        sv = _dma("scalar",
                                  qgu_hb[fc * P:(fc + 1) * P, col0:col0 + 1408],
                                  qgu_o[jo], waits=[("dve", qv["dve"])],
                                  sem="so")
                        gu["qord"][jo] = ("so", sv["so"])
                        gu["laststore"] = ("so", sv["so"])

                    steps.append(s_load2)
                    steps.append(s_quant)
            return steps

        prep_steps = gu_steps()
        prep_i = [0]

        def emit_prep(n):
            while n > 0 and prep_i[0] < len(prep_steps):
                prep_steps[prep_i[0]]()
                prep_i[0] += 1
                n -= 1


        xA_rd = [[], []]
        xnA_rd = [[], []]
        xnst_rd = [None, None]
        xnfm_rd = [None, None]
        stat_rd = [None, None]
        expk_rd = [None, None]
        vtok_rd = [None, None]
        qstok_rd = [None, None]
        expq_rd = None
        ctx_done = None

        for g in range(NG):
            sst, t = divmod(g, 4)
            j = g % 2
            st = stat[j]
            qv = _dma("sync", xA[j], x_hb[g * P:(g + 1) * P, :],
                      waits=xA_rd[j])
            x_ld = ("ds", qv["ds"])
            rv = S.emit("vector", lambda e, j=j, st=st: e.reduce_sum(
                st[:, 0:1], xA[j], AX.X),
                waits=[x_ld, stat_rd[j]] + ([memset_done] if g == 0 else []),
                incs=[("dve", 1)])
            av = S.emit("scalar", lambda e, j=j, st=st: e.activation(
                sqd, xA[j], AF.Square, accum_out=st[:, 8:9]),
                waits=[x_ld, stat_rd[j]] + ([act_warm] if g == 0 else []),
                incs=[("act", 1)])
            m2 = S.emit("vector", lambda e, st=st: e.scalar_tensor_tensor(
                st[:, 3:4], st[:, 0:1], 1.0 / (D * D), st[:, 0:1],
                ALU.mult, ALU.mult),
                waits=[("dve", rv["dve"]), ("act", av["act"])],
                incs=[("dve", 1)])
            nm = S.emit("vector", lambda e, st=st: e.tensor_scalar_mul(
                st[:, 5:6], st[:, 0:1], -1.0 / D), incs=[("dve", 1)])
            S.emit("vector", waits=[("dve", nm["dve"])])
            vv = S.emit("vector", lambda e, st=st: e.scalar_tensor_tensor(
                st[:, 4:5], st[:, 8:9], 1.0 / D, st[:, 3:4],
                ALU.mult, ALU.subtract), incs=[("dve", 1)])
            lnv_ = S.emit("scalar", lambda e, st=st: e.activation(
                st[:, 9:10], st[:, 4:5], AF.Ln, bias=cst[:, 13:14]),
                waits=[("dve", vv["dve"])], incs=[("act", 1)])
            avr = S.emit("scalar", lambda e, st=st: e.activation(
                st[:, 10:11], st[:, 9:10], AF.Exp, scale=-0.5),
                waits=[("act", lnv_["act"])], incs=[("act", 1)])
            xv = S.emit("vector", lambda e, j=j, st=st: e.tensor_scalar(
                xnA[j], xA[j], st[:, 5:6], st[:, 10:11], ALU.add, ALU.mult),
                waits=list(xnA_rd[j]) + [("act", avr["act"])],
                incs=[("dve", 1)])
            xA_rd[j] = [("dve", xv["dve"]), ("act", av["act"])]
            stat_rd[j] = ("dve", xv["dve"])
            fr = S.emit("vector", lambda e, j=j, st=st: e.tensor_reduce(
                st[:, 6:7], xnA[j], AX.X, ALU.max, apply_absolute_value=True),
                incs=[("dve", 1)])
            mxv = S.emit("vector", lambda e, st=st: e.tensor_max(
                mx_xn[:, 0:1], mx_xn[:, 0:1], st[:, 6:7]),
                waits=[("dve", fr["dve"])], incs=[("dve", 1)])
            mx_last = ("dve", mxv["dve"])
            fv = None
            for q4 in range(2):
                tv = None
                for c4 in range(4):
                    fc = q4 * 4 + c4
                    tv = S.emit("tensor", lambda e, j=j, fc=fc, c4=c4:
                                e.transpose(
                                    ptf[:, c4 * P:(c4 + 1) * P],
                                    xnA[j][:, fc * P:(fc + 1) * P], ident_f),
                                waits=([("dve", xv["dve"]), state["pt_free"]]
                                       if c4 == 0 else []),
                                incs=([("pe", 1)] if c4 == 3 else []))
                S.emit("vector", lambda e, j=j, q4=q4: e.tensor_copy(
                    xnf_st[j][:, q4 * 512:(q4 + 1) * 512], ptf[:, 0:512]),
                    waits=[("pe", tv["pe"])] + ([xnst_rd[j]] if q4 == 0 else []),
                    incs=[("dve", 1)])
                fv = S.emit("vector", lambda e, j=j, q4=q4: e.tensor_copy(
                    xnfm[j][:, q4 * 512:(q4 + 1) * 512], ptf[:, 0:512]),
                    waits=([xnfm_rd[j]] if q4 == 0 else []), incs=[("dve", 1)])
                state["pt_free"] = ("dve", fv["dve"])
            sv = _dma("scalar",
                      xn_hb[:, g * P:(g + 1) * P]
                      .rearrange("(c p) n -> p c n", p=P),
                      xnf_st[j].rearrange("p (c n) -> p c n", c=8),
                      waits=[("dve", fv["dve"])], sem="so")
            xnst_rd[j] = ("so", sv["so"])
            xnA_rd[j] = [("pe", tv["pe"]), ("dve", fr["dve"])]
            mmv = {}
            for pi, w_sb in enumerate((wq_sb, wk_sb, wv_sb)):
                m = None
                for fc in range(8):
                    first, last = fc == 0, fc == 7
                    w0 = []
                    if first:
                        w0 = [("dve", fv["dve"]), qkv_free[pi]]
                        if g == 0 and pi == 0:
                            w0.append(w_loaded)
                    m = S.emit("tensor", lambda e, j=j, fc=fc, pi=pi, w_sb=w_sb,
                               first=first, last=last: e.matmul(
                                   p_mm[pi][:, 0:512],
                                   xnfm[j][:, fc * P:(fc + 1) * P],
                                   w_sb[:, fc * 512:(fc + 1) * 512],
                                   start=first, stop=last),
                               waits=w0, incs=([("pe", 1)] if last else []))
                mmv[pi] = m["pe"]
            xnfm_rd[j] = ("pe", mmv[2])
            aq = None
            for hh in range(NH):
                aq = S.emit("scalar", lambda e, hh=hh: e.activation(
                    expq[:, hh * DH:(hh + 1) * DH],
                    p_mm[0][:, hh * DH:(hh + 1) * DH],
                    AF.Exp, accum_out=qsums[:, hh:hh + 1]),
                    waits=([("pe", mmv[0]), expq_rd] if hh == 0 else []),
                    incs=[("act", 1)])
            qkv_free[0] = ("act", aq["act"])
            fr = S.emit("vector", lambda e: e.reciprocal(qsr[:, 0:8],
                                                         qsums[:, 0:8]),
                        waits=[("act", aq["act"])], incs=[("dve", 1)])
            f2 = S.emit("vector", lambda e: e.tensor_scalar_mul(
                qsr[:, 0:8], qsr[:, 0:8], SCALE),
                waits=[("dve", fr["dve"])], incs=[("dve", 1)])
            S.emit("vector", waits=[("dve", f2["dve"])])
            qs = None
            for hh in range(NH):
                qs = S.emit("vector", lambda e, j=j, hh=hh: e.tensor_scalar_mul(
                    qs_tok[j][:, hh * DH:(hh + 1) * DH],
                    expq[:, hh * DH:(hh + 1) * DH], qsr[:, hh:hh + 1]),
                    waits=([qstok_rd[j]] if hh == 0 else []),
                    incs=[("dve", 1)])
            expq_rd = ("dve", qs["dve"])
            ek = S.emit("scalar", lambda e, j=j: e.activation(
                expk[j], p_mm[1][:, 0:512], AF.Exp),
                waits=[("pe", mmv[1]), expk_rd[j]], incs=[("act", 1)])
            qkv_free[1] = ("act", ek["act"])
            vt = S.emit("vector", lambda e, j=j: e.tensor_copy(
                v_tok[j].rearrange("p (g c) -> p g c", c=132)[:, :, 0:128],
                p_mm[2][:, 0:512].rearrange("p (g c) -> p g c", c=128)),
                waits=[("pe", mmv[2]), vtok_rd[j]], incs=[("dve", 1)])
            qkv_free[2] = ("dve", vt["dve"])
            cm = None
            for pr in range(4):
                w0 = [("act", ek["act"]), ("dve", vt["dve"])]
                if pr == 0 and t == 0:
                    w0.append(state["pc0_free"])
                cm = S.emit("tensor", lambda e, j=j, pr=pr, t=t: e.matmul(
                    ctx_banks[pr][:, 0:129],
                    expk[j][:, pr * P:(pr + 1) * P],
                    v_tok[j][:, pr * 132:pr * 132 + 129],
                    start=(t == 0), stop=(t == 3)),
                    waits=w0, incs=([("pe", 1)] if pr == 3 else []))
            expk_rd[j] = ("pe", cm["pe"])
            vtok_rd[j] = ("pe", cm["pe"])
            if t == 3:
                ca = None
                for pr in range(4):
                    ca = S.emit("vector", lambda e, pr=pr: e.tensor_add(
                        ctx_acc[:, pr * 132:pr * 132 + 129],
                        ctx_acc[:, pr * 132:pr * 132 + 129],
                        ctx_banks[pr][:, 0:129]),
                        waits=([("pe", cm["pe"])] if pr == 0 else []),
                        incs=[("dve", 1)])
                state["pc0_free"] = ("dve", ca["dve"])
                ctx_done = ("dve", ca["dve"])
            tv = None
            for c4 in range(4):
                tv = S.emit("tensor", lambda e, j=j, c4=c4: e.transpose(
                    pt[:, c4 * P:(c4 + 1) * P],
                    qs_tok[j][:, c4 * P:(c4 + 1) * P], ident_b),
                    waits=([("dve", qs["dve"]), state["pt_free"]]
                           if c4 == 0 else []),
                    incs=([("pe", 1)] if c4 == 3 else []))
            qstok_rd[j] = ("pe", tv["pe"])
            qf = S.emit("vector", lambda e, g=g: e.tensor_copy(
                qsoft.rearrange("p (c n) -> p c n", c=4)[:, :, g * P:(g + 1) * P],
                pt[:, 0:512].rearrange("p (c n) -> p c n", c=4)),
                waits=[("pe", tv["pe"])], incs=[("dve", 1)])
            state["pt_free"] = ("dve", qf["dve"])
            if g == NG - 1:
                tv = S.emit("tensor", lambda e: e.transpose(
                    ptf[0:1, 0:P], mx_xn[:, 0:1], ident_f),
                    waits=[mx_last, state["pt_free"]], incs=[("pe", 1)])
                rv = S.emit("vector", lambda e: e.tensor_reduce(
                    scal[0:1, 8:9], ptf[0:1, 0:P], AX.X, ALU.max),
                    waits=[("pe", tv["pe"])], incs=[("dve", 1)])
                state["pt_free"] = ("dve", rv["dve"])
                qv = _dma("gpsimd", gx_in, scal[0:1, 8:9],
                          waits=[("dve", rv["dve"])])
                cv1 = S.emit("gpsimd", lambda e: e.collective_compute(
                    "AllReduce", ALU.max,
                    replica_groups=[list(range(NCORE))],
                    ins=[gx_in.opt()], outs=[gx_out.opt()]),
                    waits=[("dg", qv["dg"])], incs=[("cc", 1)])
            emit_prep(3)

        emit_prep(len(prep_steps))

        barrier()
        qv = _dma("gpsimd", scal[0:1, 2:3], gx_out,
                  waits=[("cc", cv1["cc"])], sem="dc")
        gmax_x_ld = ("dc", qv["dc"])

        if PHASE_LIMIT < 2:
            return _finish(nc, S, ENGS, block_ctx=None, blocks=locals())

        xq_rd = [None, None]
        tqs_rd = [None, None]
        qxfm_rd = [None, None]
        qxfm_ok = {}

        def qx_half(sst, hs):
            j = (2 * sst + hs) % 2
            qv = _dma("sync",
                      xq_ld[j].rearrange("p (c n) -> p c n", c=4),
                      xn_hb[hs * 512:(hs + 1) * 512, sst * 512:(sst + 1) * 512]
                      .rearrange("(c p) n -> p c n", p=P),
                      waits=[xq_rd[j]])
            t1 = S.emit("vector", lambda e, j=j: e.tensor_scalar(
                tqs[j], xq_ld[j], pc[2][:, 0:1], MAGIC, ALU.mult, ALU.add),
                waits=[("ds", qv["ds"]), ("pe", bc_pe[0]), tqs_rd[j]],
                incs=[("dve", 1)])
            u1 = S.emit("vector", lambda e, j=j: e.tensor_scalar(
                xq_ld[j], tqs[j], MAGIC + QMAX, MAGIC - QMAX - 1.0,
                ALU.min, ALU.max), incs=[("dve", 1)])
            xq_rd[j] = ("dve", u1["dve"])
            tqs_rd[j] = ("dve", u1["dve"])
            w0 = [qxfm_rd[sst % 2]] if hs == 0 else []
            q1 = S.emit("vector", lambda e, sst=sst, hs=hs, j=j:
                        e.tensor_single_scalar(
                            qxfm[sst % 2][:, hs * 2048:(hs + 1) * 2048],
                            xq_ld[j], MAGIC, ALU.subtract),
                        waits=w0, incs=[("dve", 1)])
            if hs == 1:
                qxfm_ok[sst] = ("dve", q1["dve"])
            return q1

        cs = None
        for pr in range(4):
            rv = S.emit("vector", lambda e, pr=pr: e.reciprocal(
                stat[0][:, 6:7], ctx_acc[:, pr * 132 + 128:pr * 132 + 129]),
                waits=[ctx_done], incs=[("dve", 1)])
            S.emit("vector", waits=[("dve", rv["dve"])])
            for (r0, c0) in ((0, 0), (64, 64)):
                cs = S.emit("vector", lambda e, pr=pr, r0=r0, c0=c0:
                            e.tensor_scalar_mul(
                                ctx_sb[r0:r0 + 64,
                                       pr * 128 + c0:pr * 128 + c0 + 64],
                                ctx_acc[r0:r0 + 64,
                                        pr * 132 + c0:pr * 132 + c0 + 64],
                                stat[0][r0:r0 + 64, 6:7]),
                            incs=[("dve", 1)])
        sl = S.emit("vector", lambda e: e.tensor_copy(
            qx_tl[1][:, 0:1024], qsoft[:, 0:1024]),
            waits=[("dve", cs["dve"])], incs=[("dve", 1)])
        ctx_sb_ok = ("dve", sl["dve"])

        oa_rd = [None, None]
        ao_rd = [None, None]
        ap_free = [None, None]
        aoq_free = [None, None]
        last_attn_out_pe = 0
        for ns in range(NST):
            jn = ns % 2
            evs = None
            for pr in range(4):
                pb = p_mm[pr % 2]
                m = S.emit("tensor", lambda e, pr=pr, ns=ns, pb=pb: e.matmul(
                    pb[:, 0:512], ctx_sb[:, pr * 128:(pr + 1) * 128],
                    qsoft[:, pr * 4096 + ns * 512: pr * 4096 + (ns + 1) * 512],
                    start=True, stop=True),
                    waits=[ctx_sb_ok, ap_free[pr % 2]], incs=[("pe", 1)])
                last_attn_out_pe = m["pe"]
                evs = S.emit("vector", lambda e, pr=pr, jn=jn, pb=pb:
                             e.tensor_copy(
                                 oa_s[jn][:, pr * 512:(pr + 1) * 512],
                                 pb[:, 0:512]),
                             waits=[("pe", m["pe"])] + ([oa_rd[jn]]
                                                        if pr == 0 else []),
                             incs=[("dve", 1)])
                ap_free[pr % 2] = ("dve", evs["dve"])
            sl = S.emit("vector", lambda e, jn=jn: e.tensor_copy(
                qx_tl[0][:, 0:1024], oa_s[jn][:, 0:1024]),
                incs=[("dve", 1)])
            oa_ok = ("dve", sl["dve"])
            for nc2 in range(4):
                jb = nc2 % 2
                g2 = ns * 4 + nc2
                aov = None
                for half in range(2):
                    pb = p_mm[2 + ((nc2 * 2 + half) % 2)]
                    m = None
                    for ec in range(4):
                        w0 = []
                        if ec == 0:
                            w0 = [oa_ok, aoq_free[(nc2 * 2 + half) % 2]]
                        m = S.emit("tensor", lambda e, ec=ec, jn=jn, nc2=nc2,
                                   half=half, pb=pb: e.matmul(
                                       pb[:, 0:512],
                                       oa_s[jn][:, ec * 512 + nc2 * P:
                                                ec * 512 + (nc2 + 1) * P],
                                       wo_sb[:, ec * 1024 + half * 512:
                                             ec * 1024 + (half + 1) * 512],
                                       start=(ec == 0), stop=(ec == 3)),
                                   waits=w0,
                                   incs=([("pe", 1)] if ec == 3 else []))
                    oa_rd[jn] = ("pe", m["pe"])
                    aov = S.emit("vector", lambda e, jb=jb, half=half, pb=pb:
                                 e.tensor_tensor(
                                     ao_b[jb][:, half * 512:(half + 1) * 512],
                                     pb[:, 0:512],
                                     bo_bc[:, half * 512:(half + 1) * 512],
                                     ALU.add),
                                 waits=[("pe", m["pe"])] + ([ao_rd[jb]]
                                                            if half == 0 else []),
                                 incs=[("dve", 1)])
                    aoq_free[(nc2 * 2 + half) % 2] = ("dve", aov["dve"])
                sv = _dma("scalar", ao_hb[g2 * P:(g2 + 1) * P, :], ao_b[jb],
                          waits=[("dve", aov["dve"])], sem="so")
                ao_rd[jb] = ("so", sv["so"])
        lb = None
        for fc in range(8):
            lb = _dma("sync", qwgu[:, fc * I2:(fc + 1) * I2],
                      qgu_hb[fc * P:(fc + 1) * P, :],
                      waits=[("pe", last_attn_out_pe), gu["laststore"]])
        qwgu_ok = ("ds", lb["ds"])

        sg = S.emit("scalar", lambda e: e.activation(
            cst[0:1, 8:9], scal[0:1, 0:1], AF.Sigmoid), incs=[("act", 1)])
        t0 = S.emit("vector", lambda e: e.tensor_tensor(
            scal[0:1, 9:10], cst[0:1, 8:9], scal[0:1, 2:3], ALU.mult),
            waits=[("act", sg["act"]), gmax_x_ld], incs=[("dve", 1)])
        t1_ = S.emit("vector", lambda e: e.tensor_scalar_mul(
            scal[0:1, 10:11], scal[0:1, 9:10], 1.0 / QMAX),
            waits=[("dve", t0["dve"])], incs=[("dve", 1)])
        S.emit("vector", waits=[("dve", t1_["dve"])])
        fr = S.emit("vector", lambda e: e.reciprocal(
            scal[0:1, 11:12], scal[0:1, 10:11]), incs=[("dve", 1)])
        sxv = S.emit("vector", lambda e: e.tensor_scalar_mul(
            scal[0:1, 12:13], scal[0:1, 10:11], 1.0 / QMAX),
            waits=[("dve", fr["dve"])], incs=[("dve", 1)])
        S.emit("vector", waits=[("dve", sxv["dve"])])
        bcv = {}
        bc_pe = {}
        for (slot, col) in ((11, 0), (12, 1)):
            tv = S.emit("tensor", lambda e, slot=slot, col=col: e.matmul(
                pc[2][:, col:col + 1], ones_row, scal[0:1, slot:slot + 1],
                start=True, stop=True),
                waits=[("dve", sxv["dve"])], incs=[("pe", 1)])
            bc_pe[col] = tv["pe"]
            ev = S.emit("vector", lambda e, col=col: e.tensor_copy(
                bc_cols[:, col:col + 1], pc[2][:, col:col + 1]),
                waits=[("pe", tv["pe"])], incs=[("dve", 1)])
            bcv[col] = ev["dve"]
        rx = S.emit("scalar", lambda e: e.activation(
            cst[:, 10:11], bc_cols[:, 0:1], AF.Copy),
            waits=[("dve", bcv[0])], incs=[("act", 1)])
        rsx_col = cst[:, 10:11]
        gv = S.emit("vector", lambda e: e.tensor_scalar_mul(
            gsc_cols[:, 0:44], sgu_cols[:, 0:44], bc_cols[:, 1:2]),
            waits=[("dve", bcv[1])], incs=[("dve", 1)])
        gsc_ok = ("dve", gv["dve"])
        rsx_ok = ("act", rx["act"])

        for sst0 in range(2):
            for hs0 in range(2):
                qx_half(sst0, hs0)

        barrier()

        if PHASE_LIMIT < 3:
            return _finish(nc, S, ENGS, block_ctx=None, blocks=locals())
        wd = {"p1rd": {0: None, 1: None}, "p2rd": {0: None, 1: None},
              "yrd": None, "qord": {0: None, 1: None},
              "qlrd": {0: None, 1: None}, "rsbc": None, "laststore": None,
              "swdbc": None}

        def wd_steps():
            steps = []
            for ic in range(NIC):
                j = ic % 2

                def s_load(ic=ic, j=j):
                    qv = _dma("sync", wdb[j], wdt_hb[ic * P:(ic + 1) * P, :],
                              waits=[wd["p1rd"][j], wd["p2rd"][j]])
                    wd[("ld", ic)] = ("ds", qv["ds"])

                def s_max(ic=ic, j=j):
                    ab = S.emit("scalar", lambda e, s=wdb[j]: e.activation(
                        ywd, s, AF.Abs),
                        waits=[wd[("ld", ic)], wd["yrd"]], incs=[("act", 1)])
                    wd["p1rd"][j] = ("act", ab["act"])
                    mx = Mwd[:, 0:D]
                    if ic == 0:
                        qv = S.emit("vector", lambda e, mx=mx:
                                    e.tensor_copy(mx, ywd),
                                    waits=[("act", ab["act"])],
                                    incs=[("dve", 1)])
                    else:
                        qv = S.emit("vector", lambda e, mx=mx:
                                    e.tensor_tensor(mx, mx, ywd, ALU.max),
                                    waits=[("act", ab["act"])],
                                    incs=[("dve", 1)])
                    wd["yrd"] = ("dve", qv["dve"])
                    wd["m_wr"] = ("dve", qv["dve"])

                steps.append(s_load)
                steps.append(s_max)

            def s_scales():
                mrd = [wd["m_wr"]]
                for b in range(8):
                    blk = Mwd[:, b * P: (b + 1) * P]
                    tv = S.emit("tensor", lambda e, s=blk: e.transpose(
                        ptf[:, 0:P], s, ident_f),
                        waits=mrd + [state["pt_free"]], incs=[("pe", 1)])
                    rv = S.emit("vector", lambda e: e.tensor_reduce(
                        stat[0][:, 5:6], ptf[:, 0:P], AX.X, ALU.max),
                        waits=[("pe", tv["pe"])], incs=[("dve", 1)])
                    state["pt_free"] = ("dve", rv["dve"])
                    tv2 = S.emit("tensor", lambda e: e.transpose(
                        ptf[0:1, 0:P], stat[0][:, 5:6], ident_f),
                        waits=[("dve", rv["dve"]), state["pt_free"]],
                        incs=[("pe", 1)])
                    rv2 = S.emit("vector", lambda e, b=b: e.tensor_copy(
                        rs_row[0:1, b * P:(b + 1) * P], ptf[0:1, 0:P]),
                        waits=[("pe", tv2["pe"])], incs=[("dve", 1)])
                    state["pt_free"] = ("dve", rv2["dve"])
                S.emit("vector", waits=[("dve", rv2["dve"])])
                c2 = S.emit("vector", lambda e: e.tensor_copy(
                    rs_row[0:1, 1024:2048], rs_row[0:1, 0:1024]),
                    incs=[("dve", 1)])
                S.emit("vector", waits=[("dve", c2["dve"])])
                fr = S.emit("vector", lambda e: e.reciprocal(
                    rs_row[0:1, 1024:2048], rs_row[0:1, 1024:2048]),
                    incs=[("dve", 1)])
                rv3 = S.emit("vector", lambda e: e.tensor_scalar_mul(
                    rs_row[0:1, 1024:2048], rs_row[0:1, 1024:2048], QMAX),
                    waits=[("dve", fr["dve"])], incs=[("dve", 1)])
                pc1_free = ("dve", 0)
                for cb in range(0, 1024, 512):
                    tv3 = S.emit("tensor", lambda e, cb=cb: e.matmul(
                        pc[1][:, 0:512], ones_row,
                        rs_row[0:1, cb:cb + 512], start=True, stop=True),
                        waits=[("dve", rv3["dve"]), pc1_free],
                        incs=[("pe", 1)])
                    rv4 = S.emit("vector", lambda e, cb=cb: e.tensor_scalar_mul(
                        swd_bc[:, cb:cb + 512], pc[1][:, 0:512], 1.0 / QMAX),
                        waits=[("pe", tv3["pe"])], incs=[("dve", 1)])
                    tv4 = S.emit("tensor", lambda e, cb=cb: e.matmul(
                        pc[1][:, 0:512], ones_row,
                        rs_row[0:1, 1024 + cb:1024 + cb + 512],
                        start=True, stop=True),
                        waits=[("dve", rv4["dve"])], incs=[("pe", 1)])
                    rv5 = S.emit("vector", lambda e, cb=cb: e.tensor_copy(
                        rswd_bc[:, cb:cb + 512], pc[1][:, 0:512]),
                        waits=[("pe", tv4["pe"])], incs=[("dve", 1)])
                    pc1_free = ("dve", rv5["dve"])
                wd["rsbc"] = ("dve", rv5["dve"])

            steps.append(s_scales)

            for ic in range(NIC):
                j = ic % 2

                def s_load2(ic=ic, j=j):
                    qv = _dma("sync", wdb[j], wdt_hb[ic * P:(ic + 1) * P, :],
                              waits=[wd["p2rd"][j], wd["p1rd"][j]])
                    wd[("ld2", ic)] = ("ds", qv["ds"])

                def s_quant(ic=ic, j=j):
                    yv = S.emit("vector", lambda e, s=wdb[j]: e.tensor_tensor(
                        ywd, s, rswd_bc, ALU.mult),
                        waits=[wd[("ld2", ic)], wd["rsbc"], wd["yrd"]],
                        incs=[("dve", 1)])
                    wd["p2rd"][j] = ("dve", yv["dve"])
                    qv = S.emit("vector", lambda e, j=j: e.tensor_scalar(
                        qwd_o[j], ywd, MAGIC, MAGIC, ALU.add, ALU.subtract),
                        waits=[wd["qord"][j]],
                        incs=[("dve", 1)])
                    wd["yrd"] = ("dve", qv["dve"])
                    sv = _dma("scalar", qwd_hb[ic * P:(ic + 1) * P, :],
                              qwd_o[j], waits=[("dve", qv["dve"])], sem="so")
                    wd["qord"][j] = ("so", sv["so"])
                    wd["laststore"] = ("so", sv["so"])

                steps.append(s_load2)
                steps.append(s_quant)
            return steps

        wd_prep = wd_steps()
        wd_i = [0]

        def emit_wd(n):
            while n > 0 and wd_i[0] < len(wd_prep):
                wd_prep[wd_i[0]]()
                wd_i[0] += 1
                n -= 1

        gate_free = [None, None]
        up_free = [None, None]
        silu_rd = [None, None]
        hb_rd = [None, None]
        last_ffn1_pe = 0
        for sst in range(NST):
            for ic in range(NIC):
                jb = ic % 2
                m = None
                for fc in range(8):
                    w0 = []
                    if fc == 0:
                        w0 = [gate_free[jb], qxfm_ok[sst]]
                        if sst == 0 and ic == 0:
                            w0.append(qwgu_ok)
                    m = S.emit("tensor", lambda e, sst=sst, ic=ic, fc=fc, jb=jb:
                               e.matmul(
                                   p_mm[jb][:, 0:512],
                                   qwgu[:, fc * I2 + ic * P:
                                        fc * I2 + (ic + 1) * P],
                                   qxfm[sst % 2][:, fc * 512:(fc + 1) * 512],
                                   start=(fc == 0), stop=(fc == 7)),
                               waits=w0, incs=([("pe", 1)] if fc == 7 else []))
                g_pe = m["pe"]
                for fc in range(8):
                    w0 = [up_free[jb]] if fc == 0 else []
                    m = S.emit("tensor", lambda e, sst=sst, ic=ic, fc=fc, jb=jb:
                               e.matmul(
                                   p_mm[2 + jb][:, 0:512],
                                   qwgu[:, fc * I2 + (NIC + ic) * P:
                                        fc * I2 + (NIC + ic + 1) * P],
                                   qxfm[sst % 2][:, fc * 512:(fc + 1) * 512],
                                   start=(fc == 0), stop=(fc == 7)),
                               waits=w0, incs=([("pe", 1)] if fc == 7 else []))
                u_pe = m["pe"]
                last_ffn1_pe = u_pe
                sl = S.emit("scalar", lambda e, ic=ic, jb=jb: e.activation(
                    silu_b[jb], p_mm[jb][:, 0:512], AF.Silu,
                    scale=gsc_cols[:, ic:ic + 1]),
                    waits=[("pe", g_pe), gsc_ok, silu_rd[jb]],
                    incs=[("act", 1)])
                gate_free[jb] = ("act", sl["act"])
                jh = (ic // 2) % 2
                w0 = [("pe", u_pe), ("act", sl["act"])]
                if ic % 2 == 0:
                    w0.append(hb_rd[jh])
                hv = S.emit("vector", lambda e, ic=ic, jb=jb, jh=jh:
                            e.scalar_tensor_tensor(
                                h_b[jh][:, (ic % 2) * 512:(ic % 2 + 1) * 512],
                                p_mm[2 + jb][:, 0:512],
                                gsc_cols[:, NIC + ic:NIC + ic + 1],
                                silu_b[jb], ALU.mult, ALU.mult),
                            waits=w0, incs=[("dve", 1)])
                up_free[jb] = ("dve", hv["dve"])
                silu_rd[jb] = ("dve", hv["dve"])
                fr = S.emit("vector", lambda e, ic=ic, jh=jh: e.tensor_reduce(
                    stat[1][:, 5:6],
                    h_b[jh][:, (ic % 2) * 512:(ic % 2 + 1) * 512],
                    AX.X, ALU.max, apply_absolute_value=True),
                    incs=[("dve", 1)])
                mv = S.emit("vector", lambda e: e.tensor_max(
                    mx_h[:, 0:1], mx_h[:, 0:1], stat[1][:, 5:6]),
                    waits=[("dve", fr["dve"])], incs=[("dve", 1)])
                if ic % 2 == 1:
                    m0 = ic - 1
                    sv = _dma("scalar",
                              h_hb[m0 * P:(m0 + 2) * P,
                                   sst * 512:(sst + 1) * 512]
                              .rearrange("(c p) n -> p c n", p=P),
                              h_b[jh].rearrange("p (c n) -> p c n", c=2),
                              waits=[("dve", mv["dve"])], sem="so")
                    hb_rd[jh] = ("so", sv["so"])
                if ic in (5, 11, 16, 21):
                    emit_wd(2)
            qxfm_rd[sst % 2] = ("pe", last_ffn1_pe)
            if sst + 2 < NST:
                for hs0 in range(2):
                    qx_half(sst + 2, hs0)
        emit_wd(len(wd_prep))

        tv = S.emit("tensor", lambda e: e.transpose(
            ptf[0:1, 0:P], mx_h[:, 0:1], ident_f),
            waits=[("dve", S.cnt["dve"]), state["pt_free"]], incs=[("pe", 1)])
        rv = S.emit("vector", lambda e: e.tensor_reduce(
            scal[0:1, 13:14], ptf[0:1, 0:P], AX.X, ALU.max),
            waits=[("pe", tv["pe"])], incs=[("dve", 1)])
        state["pt_free"] = ("dve", rv["dve"])
        qv = _dma("gpsimd", gh_in, scal[0:1, 13:14], waits=[("dve", rv["dve"])])
        cv2 = S.emit("gpsimd", lambda e: e.collective_compute(
            "AllReduce", ALU.max, replica_groups=[list(range(NCORE))],
            ins=[gh_in.opt()], outs=[gh_out.opt()]),
            waits=[("dg", qv["dg"])], incs=[("cc", 1)])
        barrier()
        qv = _dma("gpsimd", scal[0:1, 3:4], gh_out, waits=[("cc", cv2["cc"])],
                  sem="dc")
        gmax_h_ld = ("dc", qv["dc"])

        if PHASE_LIMIT < 4:
            return _finish(nc, S, ENGS, block_ctx=None, blocks=locals())
        lb = _dma("sync",
                  qwd_sb.rearrange("p (c n) -> p c n", c=NIC),
                  qwd_hb.rearrange("(c p) n -> p c n", p=P),
                  waits=[("pe", last_ffn1_pe), wd["laststore"]])
        qwd_ok = ("ds", lb["ds"])

        sg = S.emit("scalar", lambda e: e.activation(
            cst[0:1, 9:10], scal[0:1, 1:2], AF.Sigmoid), incs=[("act", 1)])
        t0 = S.emit("vector", lambda e: e.tensor_tensor(
            scal[0:1, 14:15], cst[0:1, 9:10], scal[0:1, 3:4], ALU.mult),
            waits=[("act", sg["act"]), gmax_h_ld], incs=[("dve", 1)])
        t1_ = S.emit("vector", lambda e: e.tensor_scalar_mul(
            scal[0:1, 15:16], scal[0:1, 14:15], 1.0 / QMAX),
            waits=[("dve", t0["dve"])], incs=[("dve", 1)])
        S.emit("vector", waits=[("dve", t1_["dve"])])
        shv = S.emit("vector", lambda e: e.reciprocal(
            scal[0:1, 16:17], scal[0:1, 15:16]), incs=[("dve", 1)])
        bcv = {}
        for (slot, col) in ((16, 2), (15, 3)):
            tv = S.emit("tensor", lambda e, slot=slot, col=col: e.matmul(
                pc[2][:, col:col + 1], ones_row, scal[0:1, slot:slot + 1],
                start=True, stop=True),
                waits=[("dve", shv["dve"])], incs=[("pe", 1)])
            bc_pe[col] = tv["pe"]
            ev = S.emit("vector", lambda e, col=col: e.tensor_copy(
                bc_cols[:, col:col + 1], pc[2][:, col:col + 1]),
                waits=[("pe", tv["pe"])], incs=[("dve", 1)])
            bcv[col] = ev["dve"]
        rh = S.emit("scalar", lambda e: e.activation(
            cst[:, 11:12], bc_cols[:, 2:3], AF.Copy),
            waits=[("dve", bcv[2])], incs=[("act", 1)])
        rsh_col = cst[:, 11:12]
        rsh_ok = ("act", rh["act"])
        scv = S.emit("vector", lambda e: e.tensor_scalar_mul(
            sc_tile, swd_bc, bc_cols[:, 3:4]),
            waits=[("dve", bcv[3])], incs=[("dve", 1)])
        sc_ok = ("dve", scv["dve"])

        hr_rd = [None, None]
        qhs_rd = [None, None]
        qhs_ok = {}

        def emit_qh_pair(sst, m):
            j = m % 2
            qv = _dma("sync",
                      hr[j].rearrange("p (c n) -> p c n", c=2),
                      h_hb[2 * m * P:(2 * m + 2) * P,
                           sst * 512:(sst + 1) * 512]
                      .rearrange("(c p) n -> p c n", p=P),
                      waits=[hr_rd[j]])
            tv0 = S.emit("scalar", lambda e, j=j: e.activation(
                tqs[j][:, 0:1024], hr[j], AF.Copy,
                scale=rsh_col, bias=MAGIC),
                waits=[("ds", qv["ds"]), rsh_ok, tqs_rd[j]],
                incs=[("act", 1)])
            hr_rd[j] = ("act", tv0["act"])
            uv = S.emit("vector", lambda e, j=j: e.tensor_scalar(
                xq_ld[j][:, 0:1024], tqs[j][:, 0:1024],
                MAGIC + QMAX, MAGIC - QMAX - 1.0,
                ALU.min, ALU.max),
                waits=[("act", tv0["act"]), xq_rd[j]], incs=[("dve", 1)])
            tqs_rd[j] = ("dve", uv["dve"])
            w0 = [qhs_rd[sst % 2]] if m == 0 else []
            qv2 = S.emit("vector", lambda e, j=j, sst=sst, m=m:
                         e.tensor_single_scalar(
                             qh_s[sst % 2][:, m * 1024:(m + 1) * 1024],
                             xq_ld[j][:, 0:1024], MAGIC, ALU.subtract),
                         waits=w0, incs=[("dve", 1)])
            xq_rd[j] = ("dve", qv2["dve"])
            if m == 10:
                sl = S.emit("vector", lambda e, sst=sst: e.tensor_copy(
                    qx_tl[0], qh_s[sst % 2][:, 10 * 1024:11 * 1024]),
                    incs=[("dve", 1)])
                qhs_ok[sst] = ("dve", sl["dve"])

        for m in range(11):
            emit_qh_pair(0, m)
        for m in range(11):
            emit_qh_pair(1, m)

        if PHASE_LIMIT < 5:
            return _finish(nc, S, ENGS, block_ctx=None, blocks=locals())
        dn_free = [None, None, None, None]
        out_pend = []
        asm_rd = [None, None]
        asm2_rd = [None, None]
        outb_rd = [None, None]
        xr_rd = [None, None]
        aor_rd = [None, None]
        for sst in range(NST):
            for nc2 in range(4):
                jn = nc2 % 2
                g2 = sst * 4 + nc2
                qv = _dma("scalar", xr[jn], x_hb[g2 * P:(g2 + 1) * P, :],
                          waits=[xr_rd[jn]], sem="dl")
                x_ld = ("dl", qv["dl"])
                qa = _dma("scalar", aor[jn], ao_hb[g2 * P:(g2 + 1) * P, :],
                          waits=[aor_rd[jn]], sem="dl")
                ao_ld = ("dl", qa["dl"])
                for half in range(2):
                    bk = (nc2 * 2 + half) % 4
                    m = None
                    for ic in range(NIC):
                        w0 = []
                        if ic == 0:
                            w0 = [dn_free[bk], qhs_ok[sst]]
                            if sst == 0 and nc2 == 0 and half == 0:
                                w0.append(qwd_ok)
                        m = S.emit("tensor", lambda e, sst=sst, nc2=nc2,
                                   half=half, ic=ic, bk=bk: e.matmul(
                                       p_mm[bk][:, 0:512],
                                       qh_s[sst % 2][:, ic * 512 + nc2 * P:
                                                     ic * 512 + (nc2 + 1) * P],
                                       qwd_sb[:, ic * 1024 + half * 512:
                                              ic * 1024 + (half + 1) * 512],
                                       start=(ic == 0), stop=(ic == NIC - 1)),
                                   waits=w0,
                                   incs=([("pe", 1)] if ic == NIC - 1 else []))
                    w0 = [("pe", m["pe"]), sc_ok]
                    if half == 0:
                        w0.append(asm_rd[jn])
                    t1 = S.emit("vector", lambda e, jn=jn, half=half, bk=bk:
                                e.tensor_tensor(
                                    asm[jn][:, half * 512:(half + 1) * 512],
                                    p_mm[bk][:, 0:512],
                                    sc_tile[:, half * 512:(half + 1) * 512],
                                    ALU.mult),
                                waits=w0, incs=[("dve", 1)])
                    dn_free[bk] = ("dve", t1["dve"])
                t2 = S.emit("vector", lambda e, jn=jn: e.tensor_tensor(
                    asm2[jn], asm[jn], aor[jn], ALU.add),
                    waits=[ao_ld, asm2_rd[jn]],
                    incs=[("dve", 1)])
                asm_rd[jn] = ("dve", t2["dve"])
                aor_rd[jn] = ("dve", t2["dve"])
                t3 = S.emit("vector", lambda e, jn=jn: e.tensor_add(
                    outb[jn], asm2[jn], xr[jn]),
                    waits=[x_ld, outb_rd[jn]],
                    incs=[("dve", 1)])
                asm2_rd[jn] = ("dve", t3["dve"])
                xr_rd[jn] = ("dve", t3["dve"])
                sv = _dma("scalar", out_hb[g2 * P:(g2 + 1) * P, :], outb[jn],
                          waits=[("dve", t3["dve"])], sem="so")
                outb_rd[jn] = ("so", sv["so"])
                last_ffn2_pe = m["pe"]
            qhs_rd[sst % 2] = ("pe", last_ffn2_pe)
            if sst + 2 < NST:
                for m2 in range(11):
                    emit_qh_pair(sst + 2, m2)

        dwaits = [("dve", S.cnt["dve"]), ("act", S.cnt["act"]),
                  ("pe", S.cnt["pe"]), ("pool", S.cnt["pool"])]

        def _dbg_dma(dst, srcv, waits=()):
            if not globals().get("DEBUG_IO"):
                return
            S.emit("gpsimd", lambda e, d=dst, s=srcv: e.dma_start(
                out=d, in_=s, allow_slow_non_contiguous=True),
                waits=waits, incs=[("dg", 16)])
        _dbg_dma(dbg_hb[0:1, 0:16], scal[0:1, 0:16], dwaits)
        _dbg_dma(dbg_hb[0:P, 16:20], bc_cols[0:P, 0:4])
        _dbg_dma(dbg_hb[0:P, 20:21], mx_xn[0:P, 0:1])
        _dbg_dma(dbg_hb[0:P, 21:22], mx_h[0:P, 0:1])
        _dbg_dma(dbg_hb[0:P, 22:26], sgu_cols[0:P, 0:4])
        _dbg_dma(dbg_hb[0:P, 26:32], stat[1][0:P, 4:10])
        _dbg_dma(dbg2_hb[0:P, 0:16], stat[0][0:P, 0:16])
        _dbg_dma(dbg2_hb[0:P, 16:32], stat[1][0:P, 0:16])
        _dbg_dma(dbg2_hb[0:1, 32:56], scal[0:1, 0:24])
        _dbg_dma(dbg2_hb[0:P, 56:60], qsr[0:P, 0:4])
        _dbg_dma(dbg2_hb[0:P, 60:64], gsc_cols[0:P, 0:4])


        so_total = S.cnt["so"]
        dg_total = S.cnt["dg"]
        for eng in ("scalar", "vector", "gpsimd"):
            S.emit(eng, waits=[("so", so_total), ("dg", dg_total)])

        with nc.Block() as block:
            @block.sync
            def _(e):
                for f in S.ops["sync"]:
                    f(e)

            @block.gpsimd
            def _(e):
                for f in S.ops["gpsimd"]:
                    f(e)

            @block.vector
            def _(e):
                for f in S.ops["vector"]:
                    f(e)

            @block.scalar
            def _(e):
                for f in S.ops["scalar"]:
                    f(e)

            @block.tensor
            def _(e):
                for f in S.ops["tensor"]:
                    f(e)

    return nc


_NC_CACHE = None


def kernel(L0_attn_x, L0_attn_Wq, L0_attn_Wk, L0_attn_Wv, L0_attn_Wo,
           L0_attn_bo, L0_ffn_clip_x, L0_ffn_clip_down, L0_ffn_w_gate,
           L0_ffn_w_up, L0_ffn_w_down):
    global _NC_CACHE
    x = np.ascontiguousarray(np.asarray(L0_attn_x, dtype=np.float32))
    wq = np.ascontiguousarray(np.asarray(L0_attn_Wq, dtype=np.float32))
    wk = np.ascontiguousarray(np.asarray(L0_attn_Wk, dtype=np.float32))
    wv = np.ascontiguousarray(np.asarray(L0_attn_Wv, dtype=np.float32))
    wo = np.ascontiguousarray(np.asarray(L0_attn_Wo, dtype=np.float32))
    bo = np.ascontiguousarray(np.asarray(L0_attn_bo, dtype=np.float32)
                              .reshape(1, D))
    cx = np.ascontiguousarray(np.asarray(L0_ffn_clip_x, dtype=np.float32)
                              .reshape(1, 1))
    cd = np.ascontiguousarray(np.asarray(L0_ffn_clip_down, dtype=np.float32)
                              .reshape(1, 1))
    wg = np.asarray(L0_ffn_w_gate, dtype=np.float32)
    wu = np.asarray(L0_ffn_w_up, dtype=np.float32)
    wdn = np.asarray(L0_ffn_w_down, dtype=np.float32)
    wgu = np.ascontiguousarray(np.concatenate([wg, wu], axis=0).T)
    wdt = np.ascontiguousarray(wdn.T)

    if _NC_CACHE is None:
        _NC_CACHE = build_nc()
    nc = _NC_CACHE

    in_maps = []
    for i in range(NCORE):
        in_maps.append({
            "x": np.ascontiguousarray(x[i]),
            "wq": wq, "wk": wk, "wv": wv, "wo": wo, "bo": bo,
            "cx": cx, "cd": cd, "wgu": wgu, "wdt": wdt,
        })
    res = run_bass_kernel_spmd(nc, in_maps, core_ids=list(range(NCORE)))
    return np.stack([np.asarray(res.results[i]["out"]) for i in range(NCORE)],
                    axis=0)


# version 25
# speedup vs baseline: 1.0465x; 1.0052x over previous
import numpy as np
from contextlib import ExitStack

import concourse.bass as bass
import concourse.mybir as mybir
from concourse.bass_utils import run_bass_kernel_spmd

F32 = mybir.dt.float32
BF16 = mybir.dt.bfloat16
U16 = mybir.dt.uint16
AF = mybir.ActivationFunctionType
ALU = mybir.AluOpType
AX = mybir.AxisListType

P = 128
NTOK = 4096
D = 1024
HID = 512
NH = 8
DH = 64
I1 = 2816
I2 = 2 * I1
NIC = I1 // P
NG = NTOK // P
NST = NTOK // 512
MAGIC = 12582912.0
EPS = 1e-5
SCALE = DH ** -0.5
QMAX = 127.0
NCORE = 8

ENGS = ("sync", "gpsimd", "vector", "scalar", "tensor")


class Sched:

    def __init__(self):
        self.ops = {e: [] for e in ENGS}
        self.cnt = {}
        self.sems = None

    def emit(self, eng, build=None, waits=(), incs=()):
        post = {}
        for s, d in incs:
            self.cnt[s] = self.cnt.get(s, 0) + d
            post[s] = self.cnt[s]
        waits = [w for w in waits if w is not None and w[1] > 0]

        def run(e, build=build, waits=tuple(waits), incs=tuple(incs)):
            for s, v in waits:
                e.wait_ge(self.sems[s], v)
            ins = build(e) if build is not None else None
            for s, d in incs:
                ins.then_inc(self.sems[s], d)

        self.ops[eng].append(run)
        return post


S = None


def _dma(queue, out, in_, waits=(), sem=None):
    if sem is None:
        sem = "dg" if queue == "gpsimd" else "ds"
    return S.emit(queue, lambda e, o=out, i=in_: e.dma_start(out=o, in_=i),
                  waits=waits, incs=[(sem, 16)])




def _finish(nc, S, engs, block_ctx, blocks):
    dq = [("ds", S.cnt.get("ds", 0)), ("dg", S.cnt.get("dg", 0)),
          ("so", S.cnt.get("so", 0)), ("dc", S.cnt.get("dc", 0)),
          ("dl", S.cnt.get("dl", 0))]
    for eng in engs:
        S.emit(eng, waits=dq)
    with nc.Block() as block:
        @block.sync
        def _(e):
            for f in S.ops["sync"]:
                f(e)

        @block.gpsimd
        def _(e):
            for f in S.ops["gpsimd"]:
                f(e)

        @block.vector
        def _(e):
            for f in S.ops["vector"]:
                f(e)

        @block.scalar
        def _(e):
            for f in S.ops["scalar"]:
                f(e)

        @block.tensor
        def _(e):
            for f in S.ops["tensor"]:
                f(e)
    return nc

PHASE_LIMIT = 99


def build_nc():
    global S
    nc = bass.Bass("TRN2", target_bir_lowering=False)

    x_hb = nc.dram_tensor("x", [NTOK, D], F32, kind="ExternalInput").ap()
    wq_hb = nc.dram_tensor("wq", [D, HID], F32, kind="ExternalInput").ap()
    wk_hb = nc.dram_tensor("wk", [D, HID], F32, kind="ExternalInput").ap()
    wv_hb = nc.dram_tensor("wv", [D, HID], F32, kind="ExternalInput").ap()
    wo_hb = nc.dram_tensor("wo", [HID, D], F32, kind="ExternalInput").ap()
    bo_hb = nc.dram_tensor("bo", [1, D], F32, kind="ExternalInput").ap()
    cx_hb = nc.dram_tensor("cx", [1, 1], F32, kind="ExternalInput").ap()
    cd_hb = nc.dram_tensor("cd", [1, 1], F32, kind="ExternalInput").ap()
    wgu_hb = nc.dram_tensor("wgu", [D, I2], F32, kind="ExternalInput").ap()
    wdt_hb = nc.dram_tensor("wdt", [I1, D], F32, kind="ExternalInput").ap()
    out_hb = nc.dram_tensor("out", [NTOK, D], F32, kind="ExternalOutput").ap()
    dbg_hb = nc.dram_tensor("dbg", [P, 32], F32, kind="ExternalOutput").ap()
    dbg2_hb = nc.dram_tensor("dbg2", [P, 64], F32, kind="ExternalOutput").ap()

    _dbg_kind = "ExternalOutput" if globals().get("DEBUG_IO") else "Internal"
    qx_dbg = nc.dram_tensor("qx_dbg", [NTOK, D], BF16, kind=_dbg_kind).ap()
    qf_dbg = nc.dram_tensor("qf_dbg", [P, 8 * 4096], BF16, kind=_dbg_kind).ap()
    xn_hb = nc.dram_tensor("xn_scr", [D, NTOK], F32, kind=_dbg_kind).ap()
    h_hb = nc.dram_tensor("h_scr", [I1, NTOK], F32, kind=_dbg_kind).ap()
    ao_hb = nc.dram_tensor("ao_scr", [NTOK, D], BF16, kind=_dbg_kind).ap()
    qgu_hb = nc.dram_tensor("qgu_scr", [D, I2], BF16, kind=_dbg_kind).ap()
    qwd_hb = nc.dram_tensor("qwd_scr", [I1, D], BF16, kind=_dbg_kind).ap()
    gx_in = nc.dram_tensor("gx_in", [1, 1], F32, kind="Internal").ap()
    gx_out = nc.dram_tensor("gx_out", [1, 1], F32, kind="Internal").ap()
    gh_in = nc.dram_tensor("gh_in", [1, 1], F32, kind="Internal").ap()
    gh_out = nc.dram_tensor("gh_out", [1, 1], F32, kind="Internal").ap()

    with ExitStack() as stack:
        ent = stack.enter_context

        ARENA_COLS = 51456
        arena = ent(nc.sbuf_tensor("arena", [P, ARENA_COLS], F32))
        spans = []

        def alloc(name, cols, t0, t1, dtype=F32, parts=P):
            cols = (cols + 7) & ~7
            taken = sorted((o, c) for (o, c, a, b, n) in spans
                           if not (b < t0 or a > t1))
            off = 0
            for o, c in taken:
                if off + cols <= o:
                    break
                off = max(off, o + c)
            assert off + cols <= ARENA_COLS, \
                f"arena overflow: {name} needs {cols} cols at {off}"
            spans.append((off, cols, t0, t1, name))
            globals()["DEBUG_SPANS"] = {n: (o, c) for (o, c, a, b, n) in spans}
            view = arena[0:parts, off:off + cols]
            if dtype == BF16:
                view = view.bitcast(BF16)
            return view

        ident_b = alloc("ident_b", 64, 0, 7, BF16)
        ident_f = alloc("ident_f", 128, 0, 7)
        ones_row = alloc("ones_row", 128, 0, 7, parts=1)
        wq_sb = alloc("wq_sb", 2048, 0, 1, BF16)
        wk_sb = alloc("wk_sb", 2048, 0, 1, BF16)
        wv_sb = alloc("wv_sb", 2048, 0, 1, BF16)
        wo_sb = alloc("wo_sb", 2048, 0, 3, BF16)
        bo_bc = alloc("bo_bc", D, 0, 3)
        sgu_cols = alloc("sgu_cols", 48, 0, 7)
        gsc_cols = alloc("gsc_cols", 48, 0, 7)
        swd_bc = alloc("swd_bc", D, 4, 5)
        sc_tile = alloc("sc_tile", D, 5, 6)
        rswd_bc = alloc("rswd_bc", D, 4, 5)
        mx_xn = alloc("mx_xn", 8, 0, 7)
        mx_h = alloc("mx_h", 8, 0, 7)
        ctx_acc = alloc("ctx_acc", 4 * 132, 0, 7)
        ctx_sb = alloc("ctx_sb", 4 * 64, 0, 7, BF16)
        v_tok = [alloc(f"v_tok{j}", 264, 0, 1, BF16) for j in range(2)]
        expk = [alloc(f"expk{j}", 256, 0, 1, BF16) for j in range(2)]
        expq = alloc("expq", 512, 0, 1)
        qs_tok = [alloc(f"qs_tok{j}", 256, 0, 1, BF16) for j in range(2)]
        stat = [alloc(f"stat{j}", 16, 0, 7) for j in range(2)]
        sqd = alloc("sqd", D, 1, 1)
        qsr = alloc("qsr", 8, 0, 7)
        qsums = alloc("qsums", 8, 0, 7)
        hist = alloc("hist", 64, 0, 7)
        cst = alloc("cst", 16, 0, 7)
        barsc = alloc("barsc", 8, 0, 7)
        bc_cols = alloc("bc_cols", 8, 0, 7)
        scal = alloc("scal", 24, 0, 7, parts=1)
        rs_row = alloc("rs_row", 2048, 0, 5, parts=1)
        qx_tl = [alloc(f"qx_tl{j}", 512, 2, 6, BF16) for j in range(2)]
        xq_ld = [alloc(f"xq_ld{j}", 2048, 2, 6) for j in range(2)]
        tqs = [alloc(f"tqs{j}", 2048, 2, 6) for j in range(2)]

        xA = [alloc(f"xA{j}", D, 1, 1) for j in range(2)]
        xnA = [alloc(f"xnA{j}", D, 1, 1) for j in range(2)]
        xnfm = [alloc(f"xnfm{j}", 512, 1, 1, BF16) for j in range(2)]
        xnf_st = [alloc(f"xnf_st{j}", 1024, 1, 1) for j in range(2)]
        wgub = [alloc(f"wgub{j}", 1408, 1, 1) for j in range(2)]
        ygu = alloc("ygu", 1408, 1, 1)
        Mgu = alloc("Mgu", I1, 1, 1)
        rsgu_bc = alloc("rsgu_bc", I1, 1, 1)
        qgu_o = [alloc(f"qgu_o{j}", 704, 1, 1, BF16) for j in range(2)]
        qsoft = alloc("qsoft", 8192, 1, 2, BF16)

        oa_s = [alloc(f"oa_s{j}", 1024, 2, 3, BF16) for j in range(2)]
        ao_b = [alloc(f"ao_b{j}", 512, 2, 3, BF16) for j in range(2)]

        qwgu = alloc("qwgu", 8 * I2 // 2, 3, 4, BF16)
        qxfm = [alloc(f"qxfm{j}", 2048, 2, 4, BF16) for j in range(2)]
        silu_b = [alloc(f"silu{j}", 512, 4, 4) for j in range(2)]
        h_b = [alloc(f"h_b{j}", D, 4, 4) for j in range(2)]
        wdb = [alloc(f"wdb{j}", D, 4, 5) for j in range(2)]
        ywd = alloc("ywd", D, 4, 5)
        Mwd = alloc("Mwd", 2 * D, 4, 5)
        qwd_o = [alloc(f"qwd_o{j}", 512, 4, 5, BF16) for j in range(2)]

        qwd_sb = alloc("qwd_sb", NIC * 512, 5, 6, BF16)
        qh_s = [alloc(f"qh_s{j}", NIC * 256, 5, 6, BF16) for j in range(2)]
        hr = [alloc(f"hr{j}", D, 5, 6) for j in range(2)]
        xr = [alloc(f"xr{j}", D, 6, 6) for j in range(2)]
        aor = [alloc(f"aor{j}", 512, 6, 6, BF16) for j in range(2)]
        asm = [alloc(f"asm{j}", D, 6, 6) for j in range(2)]
        asm2 = [alloc(f"asm2{j}", D, 6, 6) for j in range(2)]
        outb = [alloc(f"outb{j}", D, 6, 6) for j in range(2)]

        p_mm = [ent(nc.psum_tensor(f"p{i}", [P, 512], F32)) for i in range(4)]
        pt = ent(nc.psum_tensor("pt", [P, 1024], BF16))
        ptf = pt.bitcast(F32)
        pc = [ent(nc.psum_tensor(f"pc{i}", [P, 512], F32)) for i in range(3)]
        ctx_banks = [pc[0], pc[1], pc[2], p_mm[3]]

        sem_names = ["ds", "dg", "pe", "act", "dve", "pool", "cc", "so",
                     "bar", "dc", "dl"]
        sem_handles = {n: ent(nc.semaphore(n)) for n in sem_names}
        S = Sched()
        S.sems = sem_handles
        globals()["S"] = S

        state = {
            "pt_free": ("dve", 0),
            "pc0_free": ("dve", 0),
        }

        def barrier():
            dq = [("ds", S.cnt.get("ds", 0)), ("dg", S.cnt.get("dg", 0)),
                  ("so", S.cnt.get("so", 0)), ("dl", S.cnt.get("dl", 0))]
            S.emit("vector", lambda e: e.memset(barsc[:, 0:1], 0.0),
                   waits=dq, incs=[("bar", 1)])
            S.emit("gpsimd", lambda e: e.memset(barsc[:, 1:2], 0.0),
                   waits=dq, incs=[("bar", 1)])
            S.emit("scalar", lambda e: e.memzero(barsc[:, 2:3]),
                   waits=dq, incs=[("bar", 1)])
            S.emit("tensor", lambda e: e.nop(), waits=dq, incs=[("bar", 1)])
            S.emit("sync", lambda e: e.nop(), waits=dq, incs=[("bar", 1)])
            bv = S.cnt["bar"]
            for eng in ENGS:
                S.emit(eng, waits=[("bar", bv)])

        S.emit("gpsimd", lambda e: e.memset(ident_b, 0.0))
        S.emit("gpsimd", lambda e: e.affine_select(
            out=ident_b, in_=ident_b, compare_op=ALU.not_equal, fill=1.0,
            base=0, pattern=[[-1, P]], channel_multiplier=1))
        S.emit("gpsimd", lambda e: e.memset(ident_f, 0.0))
        S.emit("gpsimd", lambda e: e.affine_select(
            out=ident_f, in_=ident_f, compare_op=ALU.not_equal, fill=1.0,
            base=0, pattern=[[-1, P]], channel_multiplier=1))
        iv = S.emit("gpsimd", lambda e: e.memset(ones_row, 1.0),
                    incs=[("pool", 1)])
        ident_ready = ("pool", iv["pool"])
        S.emit("vector", lambda e: e.memset(cst[:, 0:1], EPS))
        cm = S.emit("vector", lambda e: e.memset(cst[:, 1:2], MAGIC),
                    incs=[("dve", 1)])
        nc.const_aps.register(F32, MAGIC, cst[:, 1:2])
        cst_ready = ("dve", cm["dve"])
        for buf in (stat[0], stat[1], scal, cst[0:P, 2:16], qsr, qsums, hist,
                    bc_cols, mx_xn[:, 0:1], mx_h[:, 0:1], ctx_acc, ctx_sb):
            S.emit("vector", lambda e, b=buf: e.memset(b, 0.0),
                   incs=[("dve", 1)])
        for j in range(2):
            ms = S.emit("vector", lambda e, jj=j: e.memset(
                v_tok[jj].rearrange("p (g c) -> p g c", c=132)[:, :, 128:129],
                1.0), incs=[("dve", 1)])
        memset_done = ("dve", ms["dve"])

        wtok = []
        for (w_sb, w_hb, nchunk) in ((wq_sb, wq_hb, 8), (wk_sb, wk_hb, 8),
                                     (wv_sb, wv_hb, 8), (wo_sb, wo_hb, 4)):
            qv = _dma("gpsimd",
                      w_sb.rearrange("p (f n) -> p f n", f=nchunk),
                      w_hb.rearrange("(f p) n -> p f n", p=P))
            wtok.append(("dg", qv["dg"]))
        S.emit("scalar", lambda e: e.activation(cst[:, 12:13], cst[:, 1:2],
                                                AF.Square),
               waits=[cst_ready])
        S.emit("scalar", lambda e: e.activation(cst[:, 12:13], cst[:, 1:2],
                                                AF.Ln))
        S.emit("scalar", lambda e: e.activation(cst[:, 12:13], cst[:, 1:2],
                                                AF.Exp, scale=-1e-6))
        wu = S.emit("scalar", lambda e: e.activation(cst[:, 13:14],
                                                     cst[:, 0:1], AF.Copy),
                    incs=[("act", 1)])
        act_warm = ("act", wu["act"])
        qv = _dma("sync", scal[0:1, 0:1], cx_hb)
        qv = _dma("sync", scal[0:1, 1:2], cd_hb)
        qv = _dma("sync", rs_row[0:1, 0:D], bo_hb)
        bo_row_v = ("ds", qv["ds"])
        ev = None
        for cb in range(2):
            pv = S.emit("tensor", lambda e, cb=cb: e.matmul(
                p_mm[cb][:, 0:512], ones_row,
                rs_row[0:1, cb * 512:(cb + 1) * 512], start=True, stop=True),
                waits=[bo_row_v, ident_ready], incs=[("pe", 1)])
            ev = S.emit("vector", lambda e, cb=cb: e.tensor_copy(
                bo_bc[:, cb * 512:(cb + 1) * 512], p_mm[cb][:, 0:512]),
                waits=[("pe", pv["pe"])], incs=[("dve", 1)])
        qkv_free = [("dve", ev["dve"] - 1), ("dve", ev["dve"]), ("dve", 0)]

        gu = {"p1rd": {0: None, 1: None}, "p2rd": {0: None, 1: None},
              "yrd": None, "qo_j": 1, "qord": {0: None, 1: None},
              "qlrd": {0: None, 1: None}, "rsbc": {}, "laststore": None}

        def gu_steps():
            steps = []
            for qt in range(4):
                col0 = qt * 1408
                hcol = (qt % 2) * 1408
                ic_base = qt * 11
                for fc in range(8):
                    j = fc % 2

                    def s_load(qt=qt, fc=fc, j=j, col0=col0):
                        qv = _dma("sync", wgub[j],
                                  wgu_hb[fc * P:(fc + 1) * P, col0:col0 + 1408],
                                  waits=[gu["p1rd"][j], gu["p2rd"][j]])
                        gu[("ld", fc)] = ("ds", qv["ds"])

                    def s_max(fc=fc, j=j):
                        ab = S.emit("scalar", lambda e, s=wgub[j]: e.activation(
                            ygu, s, AF.Abs),
                            waits=[gu[("ld", fc)], gu["yrd"]],
                            incs=[("act", 1)])
                        gu["p1rd"][j] = ("act", ab["act"])
                        mx = Mgu[:, 0:1408]
                        if fc == 0:
                            qv = S.emit("vector", lambda e, mx=mx:
                                        e.tensor_copy(mx, ygu),
                                        waits=[("act", ab["act"]),
                                               gu.get("m_rd")],
                                        incs=[("dve", 1)])
                        else:
                            qv = S.emit("vector", lambda e, mx=mx:
                                        e.tensor_tensor(mx, mx, ygu, ALU.max),
                                        waits=[("act", ab["act"])],
                                        incs=[("dve", 1)])
                        gu["yrd"] = ("dve", qv["dve"])
                        gu["m_wr"] = ("dve", qv["dve"])

                    steps.append(s_load)
                    steps.append(s_max)

                def s_scales(qt=qt, hcol=hcol, ic_base=ic_base):
                    mrd = [gu["m_wr"]]
                    for b in range(11):
                        blk = Mgu[:, b * P: (b + 1) * P]
                        tv = S.emit("tensor", lambda e, s=blk: e.transpose(
                            ptf[:, 0:P], s, ident_f),
                            waits=mrd + [state["pt_free"]], incs=[("pe", 1)])
                        col = sgu_cols[:, ic_base + b: ic_base + b + 1]
                        rv = S.emit("vector", lambda e, c=col: e.tensor_reduce(
                            c, ptf[:, 0:P], AX.X, ALU.max),
                            waits=[("pe", tv["pe"])], incs=[("dve", 1)])
                        state["pt_free"] = ("dve", rv["dve"])
                        tv2 = S.emit("tensor", lambda e, c=col: e.transpose(
                            ptf[0:1, 0:P], c, ident_f),
                            waits=[("dve", rv["dve"]), state["pt_free"]],
                            incs=[("pe", 1)])
                        rv2 = S.emit("vector", lambda e, b=b: e.tensor_copy(
                            rs_row[0:1, b * P:(b + 1) * P], ptf[0:1, 0:P]),
                            waits=[("pe", tv2["pe"])], incs=[("dve", 1)])
                        state["pt_free"] = ("dve", rv2["dve"])
                        gu["m_rd"] = ("pe", tv2["pe"])
                    S.emit("vector", waits=[("dve", rv2["dve"])])
                    fr = S.emit("vector", lambda e: e.reciprocal(
                        rs_row[0:1, 0:1408], rs_row[0:1, 0:1408]),
                        incs=[("dve", 1)])
                    rv3 = S.emit("vector", lambda e: e.tensor_scalar_mul(
                        rs_row[0:1, 0:1408], rs_row[0:1, 0:1408], QMAX),
                        waits=[("dve", fr["dve"])], incs=[("dve", 1)])
                    last = None
                    for cb in range(0, 1408, 512):
                        n = min(512, 1408 - cb)
                        tv3 = S.emit("tensor", lambda e, cb=cb, n=n: e.matmul(
                            p_mm[0][:, 0:n], ones_row, rs_row[0:1, cb:cb + n],
                            start=True, stop=True),
                            waits=[("dve", rv3["dve"]), qkv_free[0]],
                            incs=[("pe", 1)])
                        last = S.emit("vector", lambda e, cb=cb, n=n, hcol=hcol:
                                      e.tensor_copy(
                                          rsgu_bc[:, hcol + cb:hcol + cb + n],
                                          p_mm[0][:, 0:n]),
                                      waits=[("pe", tv3["pe"])],
                                      incs=[("dve", 1)])
                        qkv_free[0] = ("dve", last["dve"])
                    gu["rsbc"][qt] = ("dve", last["dve"])

                steps.append(s_scales)

                for fc in range(8):
                    j = fc % 2

                    def s_load2(qt=qt, fc=fc, j=j, col0=col0):
                        qv = _dma("sync", wgub[j],
                                  wgu_hb[fc * P:(fc + 1) * P, col0:col0 + 1408],
                                  waits=[gu["p2rd"][j], gu["p1rd"][j]])
                        gu[("ld2", fc)] = ("ds", qv["ds"])

                    def s_quant(qt=qt, fc=fc, j=j, hcol=hcol, col0=col0):
                        yv = S.emit("vector", lambda e, s=wgub[j], hcol=hcol:
                                    e.tensor_tensor(
                                        ygu, s, rsgu_bc[:, hcol:hcol + 1408],
                                        ALU.mult),
                                    waits=[gu[("ld2", fc)], gu["rsbc"][qt],
                                           gu["yrd"]],
                                    incs=[("dve", 1)])
                        gu["p2rd"][j] = ("dve", yv["dve"])
                        jo = gu["qo_j"] = 1 - gu["qo_j"]
                        qv = S.emit("vector", lambda e, jo=jo: e.tensor_scalar(
                            qgu_o[jo], ygu, MAGIC, MAGIC, ALU.add, ALU.subtract),
                            waits=[gu["qord"][jo]],
                            incs=[("dve", 1)])
                        gu["yrd"] = ("dve", qv["dve"])
                        sv = _dma("scalar",
                                  qgu_hb[fc * P:(fc + 1) * P, col0:col0 + 1408],
                                  qgu_o[jo], waits=[("dve", qv["dve"])],
                                  sem="so")
                        gu["qord"][jo] = ("so", sv["so"])
                        gu["laststore"] = ("so", sv["so"])

                    steps.append(s_load2)
                    steps.append(s_quant)
            return steps

        prep_steps = gu_steps()
        prep_i = [0]

        def emit_prep(n):
            while n > 0 and prep_i[0] < len(prep_steps):
                prep_steps[prep_i[0]]()
                prep_i[0] += 1
                n -= 1


        xA_rd = [[], []]
        xnA_rd = [[], []]
        xnst_rd = [None, None]
        xnfm_rd = [None, None]
        stat_rd = [None, None]
        expk_rd = [None, None]
        vtok_rd = [None, None]
        qstok_rd = [None, None]
        expq_rd = None
        ctx_done = None

        for g in range(NG):
            sst, t = divmod(g, 4)
            j = g % 2
            st = stat[j]
            qv = _dma("sync", xA[j], x_hb[g * P:(g + 1) * P, :],
                      waits=xA_rd[j])
            x_ld = ("ds", qv["ds"])
            rv = S.emit("vector", lambda e, j=j, st=st: e.reduce_sum(
                st[:, 0:1], xA[j], AX.X),
                waits=[x_ld, stat_rd[j]] + ([memset_done] if g == 0 else []),
                incs=[("dve", 1)])
            av = S.emit("scalar", lambda e, j=j, st=st: e.activation(
                sqd, xA[j], AF.Square, accum_out=st[:, 8:9]),
                waits=[x_ld, stat_rd[j]] + ([act_warm] if g == 0 else []),
                incs=[("act", 1)])
            m2 = S.emit("vector", lambda e, st=st: e.scalar_tensor_tensor(
                st[:, 3:4], st[:, 0:1], 1.0 / (D * D), st[:, 0:1],
                ALU.mult, ALU.mult),
                waits=[("dve", rv["dve"]), ("act", av["act"])],
                incs=[("dve", 1)])
            nm = S.emit("vector", lambda e, st=st: e.tensor_scalar_mul(
                st[:, 5:6], st[:, 0:1], -1.0 / D), incs=[("dve", 1)])
            S.emit("vector", waits=[("dve", nm["dve"])])
            vv = S.emit("vector", lambda e, st=st: e.scalar_tensor_tensor(
                st[:, 4:5], st[:, 8:9], 1.0 / D, st[:, 3:4],
                ALU.mult, ALU.subtract), incs=[("dve", 1)])
            lnv_ = S.emit("scalar", lambda e, st=st: e.activation(
                st[:, 9:10], st[:, 4:5], AF.Ln, bias=cst[:, 13:14]),
                waits=[("dve", vv["dve"])], incs=[("act", 1)])
            avr = S.emit("scalar", lambda e, st=st: e.activation(
                st[:, 10:11], st[:, 9:10], AF.Exp, scale=-0.5),
                waits=[("act", lnv_["act"])], incs=[("act", 1)])
            xv = S.emit("vector", lambda e, j=j, st=st: e.tensor_scalar(
                xnA[j], xA[j], st[:, 5:6], st[:, 10:11], ALU.add, ALU.mult),
                waits=list(xnA_rd[j]) + [("act", avr["act"])],
                incs=[("dve", 1)])
            xA_rd[j] = [("dve", xv["dve"]), ("act", av["act"])]
            stat_rd[j] = ("dve", xv["dve"])
            fr = S.emit("vector", lambda e, j=j, st=st: e.tensor_reduce(
                st[:, 6:7], xnA[j], AX.X, ALU.max, apply_absolute_value=True),
                incs=[("dve", 1)])
            mxv = S.emit("vector", lambda e, st=st: e.tensor_max(
                mx_xn[:, 0:1], mx_xn[:, 0:1], st[:, 6:7]),
                waits=[("dve", fr["dve"])], incs=[("dve", 1)])
            mx_last = ("dve", mxv["dve"])
            fv = None
            for q4 in range(2):
                tv = None
                for c4 in range(4):
                    fc = q4 * 4 + c4
                    tv = S.emit("tensor", lambda e, j=j, fc=fc, c4=c4:
                                e.transpose(
                                    ptf[:, c4 * P:(c4 + 1) * P],
                                    xnA[j][:, fc * P:(fc + 1) * P], ident_f),
                                waits=([("dve", xv["dve"]), state["pt_free"]]
                                       if c4 == 0 else []),
                                incs=([("pe", 1)] if c4 == 3 else []))
                S.emit("vector", lambda e, j=j, q4=q4: e.tensor_copy(
                    xnf_st[j][:, q4 * 512:(q4 + 1) * 512], ptf[:, 0:512]),
                    waits=[("pe", tv["pe"])] + ([xnst_rd[j]] if q4 == 0 else []),
                    incs=[("dve", 1)])
                fv = S.emit("vector", lambda e, j=j, q4=q4: e.tensor_copy(
                    xnfm[j][:, q4 * 512:(q4 + 1) * 512], ptf[:, 0:512]),
                    waits=([xnfm_rd[j]] if q4 == 0 else []), incs=[("dve", 1)])
                state["pt_free"] = ("dve", fv["dve"])
            sv = _dma("scalar",
                      xn_hb[:, g * P:(g + 1) * P]
                      .rearrange("(c p) n -> p c n", p=P),
                      xnf_st[j].rearrange("p (c n) -> p c n", c=8),
                      waits=[("dve", fv["dve"])], sem="so")
            xnst_rd[j] = ("so", sv["so"])
            xnA_rd[j] = [("pe", tv["pe"]), ("dve", fr["dve"])]
            mmv = {}
            for pi, w_sb in enumerate((wq_sb, wk_sb, wv_sb)):
                m = None
                for fc in range(8):
                    first, last = fc == 0, fc == 7
                    w0 = []
                    if first:
                        w0 = [("dve", fv["dve"]), qkv_free[pi]]
                        if g == 0:
                            w0.append(wtok[pi])
                    m = S.emit("tensor", lambda e, j=j, fc=fc, pi=pi, w_sb=w_sb,
                               first=first, last=last: e.matmul(
                                   p_mm[pi][:, 0:512],
                                   xnfm[j][:, fc * P:(fc + 1) * P],
                                   w_sb[:, fc * 512:(fc + 1) * 512],
                                   start=first, stop=last),
                               waits=w0, incs=([("pe", 1)] if last else []))
                mmv[pi] = m["pe"]
            xnfm_rd[j] = ("pe", mmv[2])
            aq = None
            for hh in range(NH):
                aq = S.emit("scalar", lambda e, hh=hh: e.activation(
                    expq[:, hh * DH:(hh + 1) * DH],
                    p_mm[0][:, hh * DH:(hh + 1) * DH],
                    AF.Exp, accum_out=qsums[:, hh:hh + 1]),
                    waits=([("pe", mmv[0]), expq_rd] if hh == 0 else []),
                    incs=[("act", 1)])
            qkv_free[0] = ("act", aq["act"])
            fr = S.emit("vector", lambda e: e.reciprocal(qsr[:, 0:8],
                                                         qsums[:, 0:8]),
                        waits=[("act", aq["act"])], incs=[("dve", 1)])
            f2 = S.emit("vector", lambda e: e.tensor_scalar_mul(
                qsr[:, 0:8], qsr[:, 0:8], SCALE),
                waits=[("dve", fr["dve"])], incs=[("dve", 1)])
            S.emit("vector", waits=[("dve", f2["dve"])])
            qs = None
            for hh in range(NH):
                qs = S.emit("vector", lambda e, j=j, hh=hh: e.tensor_scalar_mul(
                    qs_tok[j][:, hh * DH:(hh + 1) * DH],
                    expq[:, hh * DH:(hh + 1) * DH], qsr[:, hh:hh + 1]),
                    waits=([qstok_rd[j]] if hh == 0 else []),
                    incs=[("dve", 1)])
            expq_rd = ("dve", qs["dve"])
            ek = S.emit("scalar", lambda e, j=j: e.activation(
                expk[j], p_mm[1][:, 0:512], AF.Exp),
                waits=[("pe", mmv[1]), expk_rd[j]], incs=[("act", 1)])
            qkv_free[1] = ("act", ek["act"])
            vt = S.emit("vector", lambda e, j=j: e.tensor_copy(
                v_tok[j].rearrange("p (g c) -> p g c", c=132)[:, :, 0:128],
                p_mm[2][:, 0:512].rearrange("p (g c) -> p g c", c=128)),
                waits=[("pe", mmv[2]), vtok_rd[j]], incs=[("dve", 1)])
            qkv_free[2] = ("dve", vt["dve"])
            cm = None
            for pr in range(4):
                w0 = [("act", ek["act"]), ("dve", vt["dve"])]
                if pr == 0 and t == 0:
                    w0.append(state["pc0_free"])
                cm = S.emit("tensor", lambda e, j=j, pr=pr, t=t: e.matmul(
                    ctx_banks[pr][:, 0:129],
                    expk[j][:, pr * P:(pr + 1) * P],
                    v_tok[j][:, pr * 132:pr * 132 + 129],
                    start=(t == 0), stop=(t == 3)),
                    waits=w0, incs=([("pe", 1)] if pr == 3 else []))
            expk_rd[j] = ("pe", cm["pe"])
            vtok_rd[j] = ("pe", cm["pe"])
            if t == 3:
                ca = None
                for pr in range(4):
                    ca = S.emit("vector", lambda e, pr=pr: e.tensor_add(
                        ctx_acc[:, pr * 132:pr * 132 + 129],
                        ctx_acc[:, pr * 132:pr * 132 + 129],
                        ctx_banks[pr][:, 0:129]),
                        waits=([("pe", cm["pe"])] if pr == 0 else []),
                        incs=[("dve", 1)])
                state["pc0_free"] = ("dve", ca["dve"])
                ctx_done = ("dve", ca["dve"])
            tv = None
            for c4 in range(4):
                tv = S.emit("tensor", lambda e, j=j, c4=c4: e.transpose(
                    pt[:, c4 * P:(c4 + 1) * P],
                    qs_tok[j][:, c4 * P:(c4 + 1) * P], ident_b),
                    waits=([("dve", qs["dve"]), state["pt_free"]]
                           if c4 == 0 else []),
                    incs=([("pe", 1)] if c4 == 3 else []))
            qstok_rd[j] = ("pe", tv["pe"])
            qf = S.emit("vector", lambda e, g=g: e.tensor_copy(
                qsoft.rearrange("p (c n) -> p c n", c=4)[:, :, g * P:(g + 1) * P],
                pt[:, 0:512].rearrange("p (c n) -> p c n", c=4)),
                waits=[("pe", tv["pe"])], incs=[("dve", 1)])
            state["pt_free"] = ("dve", qf["dve"])
            if g == NG - 1:
                tv = S.emit("tensor", lambda e: e.transpose(
                    ptf[0:1, 0:P], mx_xn[:, 0:1], ident_f),
                    waits=[mx_last, state["pt_free"]], incs=[("pe", 1)])
                rv = S.emit("vector", lambda e: e.tensor_reduce(
                    scal[0:1, 8:9], ptf[0:1, 0:P], AX.X, ALU.max),
                    waits=[("pe", tv["pe"])], incs=[("dve", 1)])
                state["pt_free"] = ("dve", rv["dve"])
                qv = _dma("gpsimd", gx_in, scal[0:1, 8:9],
                          waits=[("dve", rv["dve"])])
                cv1 = S.emit("gpsimd", lambda e: e.collective_compute(
                    "AllReduce", ALU.max,
                    replica_groups=[list(range(NCORE))],
                    ins=[gx_in.opt()], outs=[gx_out.opt()]),
                    waits=[("dg", qv["dg"])], incs=[("cc", 1)])
            emit_prep(3)

        emit_prep(len(prep_steps))

        barrier()
        qv = _dma("gpsimd", scal[0:1, 2:3], gx_out,
                  waits=[("cc", cv1["cc"])], sem="dc")
        gmax_x_ld = ("dc", qv["dc"])

        if PHASE_LIMIT < 2:
            return _finish(nc, S, ENGS, block_ctx=None, blocks=locals())

        xq_rd = [None, None]
        tqs_rd = [None, None]
        qxfm_rd = [None, None]
        qxfm_ok = {}

        def qx_half(sst, hs):
            j = (2 * sst + hs) % 2
            qv = _dma("sync",
                      xq_ld[j].rearrange("p (c n) -> p c n", c=4),
                      xn_hb[hs * 512:(hs + 1) * 512, sst * 512:(sst + 1) * 512]
                      .rearrange("(c p) n -> p c n", p=P),
                      waits=[xq_rd[j]])
            t1 = S.emit("vector", lambda e, j=j: e.tensor_scalar(
                tqs[j], xq_ld[j], pc[2][:, 0:1], MAGIC, ALU.mult, ALU.add),
                waits=[("ds", qv["ds"]), ("pe", bc_pe[0]), tqs_rd[j]],
                incs=[("dve", 1)])
            u1 = S.emit("vector", lambda e, j=j: e.tensor_scalar(
                xq_ld[j], tqs[j], MAGIC + QMAX, MAGIC - QMAX - 1.0,
                ALU.min, ALU.max), incs=[("dve", 1)])
            xq_rd[j] = ("dve", u1["dve"])
            tqs_rd[j] = ("dve", u1["dve"])
            w0 = [qxfm_rd[sst % 2]] if hs == 0 else []
            q1 = S.emit("vector", lambda e, sst=sst, hs=hs, j=j:
                        e.tensor_single_scalar(
                            qxfm[sst % 2][:, hs * 2048:(hs + 1) * 2048],
                            xq_ld[j], MAGIC, ALU.subtract),
                        waits=w0, incs=[("dve", 1)])
            if hs == 1:
                qxfm_ok[sst] = ("dve", q1["dve"])
            return q1

        cs = None
        for pr in range(4):
            rv = S.emit("vector", lambda e, pr=pr: e.reciprocal(
                stat[0][:, 6:7], ctx_acc[:, pr * 132 + 128:pr * 132 + 129]),
                waits=[ctx_done], incs=[("dve", 1)])
            S.emit("vector", waits=[("dve", rv["dve"])])
            for (r0, c0) in ((0, 0), (64, 64)):
                cs = S.emit("vector", lambda e, pr=pr, r0=r0, c0=c0:
                            e.tensor_scalar_mul(
                                ctx_sb[r0:r0 + 64,
                                       pr * 128 + c0:pr * 128 + c0 + 64],
                                ctx_acc[r0:r0 + 64,
                                        pr * 132 + c0:pr * 132 + c0 + 64],
                                stat[0][r0:r0 + 64, 6:7]),
                            incs=[("dve", 1)])
        sl = S.emit("vector", lambda e: e.tensor_copy(
            qx_tl[1][:, 0:1024], qsoft[:, 0:1024]),
            waits=[("dve", cs["dve"])], incs=[("dve", 1)])
        ctx_sb_ok = ("dve", sl["dve"])

        oa_rd = [None, None]
        ao_rd = [None, None]
        ap_free = [None, None]
        aoq_free = [None, None]
        last_attn_out_pe = 0
        for ns in range(NST):
            jn = ns % 2
            evs = None
            for pr in range(4):
                pb = p_mm[pr % 2]
                m = S.emit("tensor", lambda e, pr=pr, ns=ns, pb=pb: e.matmul(
                    pb[:, 0:512], ctx_sb[:, pr * 128:(pr + 1) * 128],
                    qsoft[:, pr * 4096 + ns * 512: pr * 4096 + (ns + 1) * 512],
                    start=True, stop=True),
                    waits=[ctx_sb_ok, ap_free[pr % 2]], incs=[("pe", 1)])
                last_attn_out_pe = m["pe"]
                evs = S.emit("vector", lambda e, pr=pr, jn=jn, pb=pb:
                             e.tensor_copy(
                                 oa_s[jn][:, pr * 512:(pr + 1) * 512],
                                 pb[:, 0:512]),
                             waits=[("pe", m["pe"])] + ([oa_rd[jn]]
                                                        if pr == 0 else []),
                             incs=[("dve", 1)])
                ap_free[pr % 2] = ("dve", evs["dve"])
            sl = S.emit("vector", lambda e, jn=jn: e.tensor_copy(
                qx_tl[0][:, 0:1024], oa_s[jn][:, 0:1024]),
                incs=[("dve", 1)])
            oa_ok = ("dve", sl["dve"])
            for nc2 in range(4):
                jb = nc2 % 2
                g2 = ns * 4 + nc2
                aov = None
                for half in range(2):
                    pb = p_mm[2 + ((nc2 * 2 + half) % 2)]
                    m = None
                    for ec in range(4):
                        w0 = []
                        if ec == 0:
                            w0 = [oa_ok, aoq_free[(nc2 * 2 + half) % 2]]
                        m = S.emit("tensor", lambda e, ec=ec, jn=jn, nc2=nc2,
                                   half=half, pb=pb: e.matmul(
                                       pb[:, 0:512],
                                       oa_s[jn][:, ec * 512 + nc2 * P:
                                                ec * 512 + (nc2 + 1) * P],
                                       wo_sb[:, ec * 1024 + half * 512:
                                             ec * 1024 + (half + 1) * 512],
                                       start=(ec == 0), stop=(ec == 3)),
                                   waits=w0,
                                   incs=([("pe", 1)] if ec == 3 else []))
                    oa_rd[jn] = ("pe", m["pe"])
                    aov = S.emit("vector", lambda e, jb=jb, half=half, pb=pb:
                                 e.tensor_tensor(
                                     ao_b[jb][:, half * 512:(half + 1) * 512],
                                     pb[:, 0:512],
                                     bo_bc[:, half * 512:(half + 1) * 512],
                                     ALU.add),
                                 waits=[("pe", m["pe"])] + ([ao_rd[jb]]
                                                            if half == 0 else []),
                                 incs=[("dve", 1)])
                    aoq_free[(nc2 * 2 + half) % 2] = ("dve", aov["dve"])
                sv = _dma("scalar", ao_hb[g2 * P:(g2 + 1) * P, :], ao_b[jb],
                          waits=[("dve", aov["dve"])], sem="so")
                ao_rd[jb] = ("so", sv["so"])
        lb = None
        for fc in range(8):
            lb = _dma("sync", qwgu[:, fc * I2:(fc + 1) * I2],
                      qgu_hb[fc * P:(fc + 1) * P, :],
                      waits=[("pe", last_attn_out_pe), gu["laststore"]])
        qwgu_ok = ("ds", lb["ds"])

        sg = S.emit("scalar", lambda e: e.activation(
            cst[0:1, 8:9], scal[0:1, 0:1], AF.Sigmoid), incs=[("act", 1)])
        t0 = S.emit("vector", lambda e: e.tensor_tensor(
            scal[0:1, 9:10], cst[0:1, 8:9], scal[0:1, 2:3], ALU.mult),
            waits=[("act", sg["act"]), gmax_x_ld], incs=[("dve", 1)])
        t1_ = S.emit("vector", lambda e: e.tensor_scalar_mul(
            scal[0:1, 10:11], scal[0:1, 9:10], 1.0 / QMAX),
            waits=[("dve", t0["dve"])], incs=[("dve", 1)])
        S.emit("vector", waits=[("dve", t1_["dve"])])
        fr = S.emit("vector", lambda e: e.reciprocal(
            scal[0:1, 11:12], scal[0:1, 10:11]), incs=[("dve", 1)])
        sxv = S.emit("vector", lambda e: e.tensor_scalar_mul(
            scal[0:1, 12:13], scal[0:1, 10:11], 1.0 / QMAX),
            waits=[("dve", fr["dve"])], incs=[("dve", 1)])
        S.emit("vector", waits=[("dve", sxv["dve"])])
        bcv = {}
        bc_pe = {}
        for (slot, col) in ((11, 0), (12, 1)):
            tv = S.emit("tensor", lambda e, slot=slot, col=col: e.matmul(
                pc[2][:, col:col + 1], ones_row, scal[0:1, slot:slot + 1],
                start=True, stop=True),
                waits=[("dve", sxv["dve"])], incs=[("pe", 1)])
            bc_pe[col] = tv["pe"]
            ev = S.emit("vector", lambda e, col=col: e.tensor_copy(
                bc_cols[:, col:col + 1], pc[2][:, col:col + 1]),
                waits=[("pe", tv["pe"])], incs=[("dve", 1)])
            bcv[col] = ev["dve"]
        rx = S.emit("scalar", lambda e: e.activation(
            cst[:, 10:11], bc_cols[:, 0:1], AF.Copy),
            waits=[("dve", bcv[0])], incs=[("act", 1)])
        rsx_col = cst[:, 10:11]
        gv = S.emit("vector", lambda e: e.tensor_scalar_mul(
            gsc_cols[:, 0:44], sgu_cols[:, 0:44], bc_cols[:, 1:2]),
            waits=[("dve", bcv[1])], incs=[("dve", 1)])
        gsc_ok = ("dve", gv["dve"])
        rsx_ok = ("act", rx["act"])

        for sst0 in range(2):
            for hs0 in range(2):
                qx_half(sst0, hs0)

        barrier()

        if PHASE_LIMIT < 3:
            return _finish(nc, S, ENGS, block_ctx=None, blocks=locals())
        wd = {"p1rd": {0: None, 1: None}, "p2rd": {0: None, 1: None},
              "yrd": None, "qord": {0: None, 1: None},
              "qlrd": {0: None, 1: None}, "rsbc": None, "laststore": None,
              "swdbc": None}

        def wd_steps():
            steps = []
            for ic in range(NIC):
                j = ic % 2

                def s_load(ic=ic, j=j):
                    qv = _dma("sync", wdb[j], wdt_hb[ic * P:(ic + 1) * P, :],
                              waits=[wd["p1rd"][j], wd["p2rd"][j]])
                    wd[("ld", ic)] = ("ds", qv["ds"])

                def s_max(ic=ic, j=j):
                    ab = S.emit("scalar", lambda e, s=wdb[j]: e.activation(
                        ywd, s, AF.Abs),
                        waits=[wd[("ld", ic)], wd["yrd"]], incs=[("act", 1)])
                    wd["p1rd"][j] = ("act", ab["act"])
                    mx = Mwd[:, 0:D]
                    if ic == 0:
                        qv = S.emit("vector", lambda e, mx=mx:
                                    e.tensor_copy(mx, ywd),
                                    waits=[("act", ab["act"])],
                                    incs=[("dve", 1)])
                    else:
                        qv = S.emit("vector", lambda e, mx=mx:
                                    e.tensor_tensor(mx, mx, ywd, ALU.max),
                                    waits=[("act", ab["act"])],
                                    incs=[("dve", 1)])
                    wd["yrd"] = ("dve", qv["dve"])
                    wd["m_wr"] = ("dve", qv["dve"])

                steps.append(s_load)
                steps.append(s_max)

            def s_scales():
                mrd = [wd["m_wr"]]
                for b in range(8):
                    blk = Mwd[:, b * P: (b + 1) * P]
                    tv = S.emit("tensor", lambda e, s=blk: e.transpose(
                        ptf[:, 0:P], s, ident_f),
                        waits=mrd + [state["pt_free"]], incs=[("pe", 1)])
                    rv = S.emit("vector", lambda e: e.tensor_reduce(
                        stat[0][:, 5:6], ptf[:, 0:P], AX.X, ALU.max),
                        waits=[("pe", tv["pe"])], incs=[("dve", 1)])
                    state["pt_free"] = ("dve", rv["dve"])
                    tv2 = S.emit("tensor", lambda e: e.transpose(
                        ptf[0:1, 0:P], stat[0][:, 5:6], ident_f),
                        waits=[("dve", rv["dve"]), state["pt_free"]],
                        incs=[("pe", 1)])
                    rv2 = S.emit("vector", lambda e, b=b: e.tensor_copy(
                        rs_row[0:1, b * P:(b + 1) * P], ptf[0:1, 0:P]),
                        waits=[("pe", tv2["pe"])], incs=[("dve", 1)])
                    state["pt_free"] = ("dve", rv2["dve"])
                S.emit("vector", waits=[("dve", rv2["dve"])])
                c2 = S.emit("vector", lambda e: e.tensor_copy(
                    rs_row[0:1, 1024:2048], rs_row[0:1, 0:1024]),
                    incs=[("dve", 1)])
                S.emit("vector", waits=[("dve", c2["dve"])])
                fr = S.emit("vector", lambda e: e.reciprocal(
                    rs_row[0:1, 1024:2048], rs_row[0:1, 1024:2048]),
                    incs=[("dve", 1)])
                rv3 = S.emit("vector", lambda e: e.tensor_scalar_mul(
                    rs_row[0:1, 1024:2048], rs_row[0:1, 1024:2048], QMAX),
                    waits=[("dve", fr["dve"])], incs=[("dve", 1)])
                pc1_free = ("dve", 0)
                for cb in range(0, 1024, 512):
                    tv3 = S.emit("tensor", lambda e, cb=cb: e.matmul(
                        pc[1][:, 0:512], ones_row,
                        rs_row[0:1, cb:cb + 512], start=True, stop=True),
                        waits=[("dve", rv3["dve"]), pc1_free],
                        incs=[("pe", 1)])
                    rv4 = S.emit("vector", lambda e, cb=cb: e.tensor_scalar_mul(
                        swd_bc[:, cb:cb + 512], pc[1][:, 0:512], 1.0 / QMAX),
                        waits=[("pe", tv3["pe"])], incs=[("dve", 1)])
                    tv4 = S.emit("tensor", lambda e, cb=cb: e.matmul(
                        pc[1][:, 0:512], ones_row,
                        rs_row[0:1, 1024 + cb:1024 + cb + 512],
                        start=True, stop=True),
                        waits=[("dve", rv4["dve"])], incs=[("pe", 1)])
                    rv5 = S.emit("vector", lambda e, cb=cb: e.tensor_copy(
                        rswd_bc[:, cb:cb + 512], pc[1][:, 0:512]),
                        waits=[("pe", tv4["pe"])], incs=[("dve", 1)])
                    pc1_free = ("dve", rv5["dve"])
                wd["rsbc"] = ("dve", rv5["dve"])

            steps.append(s_scales)

            for ic in range(NIC):
                j = ic % 2

                def s_load2(ic=ic, j=j):
                    qv = _dma("sync", wdb[j], wdt_hb[ic * P:(ic + 1) * P, :],
                              waits=[wd["p2rd"][j], wd["p1rd"][j]])
                    wd[("ld2", ic)] = ("ds", qv["ds"])

                def s_quant(ic=ic, j=j):
                    yv = S.emit("vector", lambda e, s=wdb[j]: e.tensor_tensor(
                        ywd, s, rswd_bc, ALU.mult),
                        waits=[wd[("ld2", ic)], wd["rsbc"], wd["yrd"]],
                        incs=[("dve", 1)])
                    wd["p2rd"][j] = ("dve", yv["dve"])
                    qv = S.emit("vector", lambda e, j=j: e.tensor_scalar(
                        qwd_o[j], ywd, MAGIC, MAGIC, ALU.add, ALU.subtract),
                        waits=[wd["qord"][j]],
                        incs=[("dve", 1)])
                    wd["yrd"] = ("dve", qv["dve"])
                    sv = _dma("scalar", qwd_hb[ic * P:(ic + 1) * P, :],
                              qwd_o[j], waits=[("dve", qv["dve"])], sem="so")
                    wd["qord"][j] = ("so", sv["so"])
                    wd["laststore"] = ("so", sv["so"])

                steps.append(s_load2)
                steps.append(s_quant)
            return steps

        wd_prep = wd_steps()
        wd_i = [0]

        def emit_wd(n):
            while n > 0 and wd_i[0] < len(wd_prep):
                wd_prep[wd_i[0]]()
                wd_i[0] += 1
                n -= 1

        gate_free = [None, None]
        up_free = [None, None]
        silu_rd = [None, None]
        hb_rd = [None, None]
        last_ffn1_pe = 0
        for sst in range(NST):
            for ic in range(NIC):
                jb = ic % 2
                m = None
                for fc in range(8):
                    w0 = []
                    if fc == 0:
                        w0 = [gate_free[jb], qxfm_ok[sst]]
                        if sst == 0 and ic == 0:
                            w0.append(qwgu_ok)
                    m = S.emit("tensor", lambda e, sst=sst, ic=ic, fc=fc, jb=jb:
                               e.matmul(
                                   p_mm[jb][:, 0:512],
                                   qwgu[:, fc * I2 + ic * P:
                                        fc * I2 + (ic + 1) * P],
                                   qxfm[sst % 2][:, fc * 512:(fc + 1) * 512],
                                   start=(fc == 0), stop=(fc == 7)),
                               waits=w0, incs=([("pe", 1)] if fc == 7 else []))
                g_pe = m["pe"]
                for fc in range(8):
                    w0 = [up_free[jb]] if fc == 0 else []
                    m = S.emit("tensor", lambda e, sst=sst, ic=ic, fc=fc, jb=jb:
                               e.matmul(
                                   p_mm[2 + jb][:, 0:512],
                                   qwgu[:, fc * I2 + (NIC + ic) * P:
                                        fc * I2 + (NIC + ic + 1) * P],
                                   qxfm[sst % 2][:, fc * 512:(fc + 1) * 512],
                                   start=(fc == 0), stop=(fc == 7)),
                               waits=w0, incs=([("pe", 1)] if fc == 7 else []))
                u_pe = m["pe"]
                last_ffn1_pe = u_pe
                sl = S.emit("scalar", lambda e, ic=ic, jb=jb: e.activation(
                    silu_b[jb], p_mm[jb][:, 0:512], AF.Silu,
                    scale=gsc_cols[:, ic:ic + 1]),
                    waits=[("pe", g_pe), gsc_ok, silu_rd[jb]],
                    incs=[("act", 1)])
                gate_free[jb] = ("act", sl["act"])
                jh = (ic // 2) % 2
                w0 = [("pe", u_pe), ("act", sl["act"])]
                if ic % 2 == 0:
                    w0.append(hb_rd[jh])
                hv = S.emit("vector", lambda e, ic=ic, jb=jb, jh=jh:
                            e.scalar_tensor_tensor(
                                h_b[jh][:, (ic % 2) * 512:(ic % 2 + 1) * 512],
                                p_mm[2 + jb][:, 0:512],
                                gsc_cols[:, NIC + ic:NIC + ic + 1],
                                silu_b[jb], ALU.mult, ALU.mult),
                            waits=w0, incs=[("dve", 1)])
                up_free[jb] = ("dve", hv["dve"])
                silu_rd[jb] = ("dve", hv["dve"])
                fr = S.emit("vector", lambda e, ic=ic, jh=jh: e.tensor_reduce(
                    stat[1][:, 5:6],
                    h_b[jh][:, (ic % 2) * 512:(ic % 2 + 1) * 512],
                    AX.X, ALU.max, apply_absolute_value=True),
                    incs=[("dve", 1)])
                mv = S.emit("vector", lambda e: e.tensor_max(
                    mx_h[:, 0:1], mx_h[:, 0:1], stat[1][:, 5:6]),
                    waits=[("dve", fr["dve"])], incs=[("dve", 1)])
                if ic % 2 == 1:
                    m0 = ic - 1
                    sv = _dma("scalar",
                              h_hb[m0 * P:(m0 + 2) * P,
                                   sst * 512:(sst + 1) * 512]
                              .rearrange("(c p) n -> p c n", p=P),
                              h_b[jh].rearrange("p (c n) -> p c n", c=2),
                              waits=[("dve", mv["dve"])], sem="so")
                    hb_rd[jh] = ("so", sv["so"])
                if ic in (5, 11, 16, 21):
                    emit_wd(2)
            qxfm_rd[sst % 2] = ("pe", last_ffn1_pe)
            if sst + 2 < NST:
                for hs0 in range(2):
                    qx_half(sst + 2, hs0)
        emit_wd(len(wd_prep))

        tv = S.emit("tensor", lambda e: e.transpose(
            ptf[0:1, 0:P], mx_h[:, 0:1], ident_f),
            waits=[("dve", S.cnt["dve"]), state["pt_free"]], incs=[("pe", 1)])
        rv = S.emit("vector", lambda e: e.tensor_reduce(
            scal[0:1, 13:14], ptf[0:1, 0:P], AX.X, ALU.max),
            waits=[("pe", tv["pe"])], incs=[("dve", 1)])
        state["pt_free"] = ("dve", rv["dve"])
        qv = _dma("gpsimd", gh_in, scal[0:1, 13:14], waits=[("dve", rv["dve"])])
        cv2 = S.emit("gpsimd", lambda e: e.collective_compute(
            "AllReduce", ALU.max, replica_groups=[list(range(NCORE))],
            ins=[gh_in.opt()], outs=[gh_out.opt()]),
            waits=[("dg", qv["dg"])], incs=[("cc", 1)])
        barrier()
        qv = _dma("gpsimd", scal[0:1, 3:4], gh_out, waits=[("cc", cv2["cc"])],
                  sem="dc")
        gmax_h_ld = ("dc", qv["dc"])

        if PHASE_LIMIT < 4:
            return _finish(nc, S, ENGS, block_ctx=None, blocks=locals())
        lb = _dma("sync",
                  qwd_sb.rearrange("p (c n) -> p c n", c=NIC),
                  qwd_hb.rearrange("(c p) n -> p c n", p=P),
                  waits=[("pe", last_ffn1_pe), wd["laststore"]])
        qwd_ok = ("ds", lb["ds"])

        sg = S.emit("scalar", lambda e: e.activation(
            cst[0:1, 9:10], scal[0:1, 1:2], AF.Sigmoid), incs=[("act", 1)])
        t0 = S.emit("vector", lambda e: e.tensor_tensor(
            scal[0:1, 14:15], cst[0:1, 9:10], scal[0:1, 3:4], ALU.mult),
            waits=[("act", sg["act"]), gmax_h_ld], incs=[("dve", 1)])
        t1_ = S.emit("vector", lambda e: e.tensor_scalar_mul(
            scal[0:1, 15:16], scal[0:1, 14:15], 1.0 / QMAX),
            waits=[("dve", t0["dve"])], incs=[("dve", 1)])
        S.emit("vector", waits=[("dve", t1_["dve"])])
        shv = S.emit("vector", lambda e: e.reciprocal(
            scal[0:1, 16:17], scal[0:1, 15:16]), incs=[("dve", 1)])
        bcv = {}
        for (slot, col) in ((16, 2), (15, 3)):
            tv = S.emit("tensor", lambda e, slot=slot, col=col: e.matmul(
                pc[2][:, col:col + 1], ones_row, scal[0:1, slot:slot + 1],
                start=True, stop=True),
                waits=[("dve", shv["dve"])], incs=[("pe", 1)])
            bc_pe[col] = tv["pe"]
            ev = S.emit("vector", lambda e, col=col: e.tensor_copy(
                bc_cols[:, col:col + 1], pc[2][:, col:col + 1]),
                waits=[("pe", tv["pe"])], incs=[("dve", 1)])
            bcv[col] = ev["dve"]
        rh = S.emit("scalar", lambda e: e.activation(
            cst[:, 11:12], bc_cols[:, 2:3], AF.Copy),
            waits=[("dve", bcv[2])], incs=[("act", 1)])
        rsh_col = cst[:, 11:12]
        rsh_ok = ("act", rh["act"])
        scv = S.emit("vector", lambda e: e.tensor_scalar_mul(
            sc_tile, swd_bc, bc_cols[:, 3:4]),
            waits=[("dve", bcv[3])], incs=[("dve", 1)])
        sc_ok = ("dve", scv["dve"])

        hr_rd = [None, None]
        qhs_rd = [None, None]
        qhs_ok = {}

        def emit_qh_pair(sst, m):
            j = m % 2
            qv = _dma("sync",
                      hr[j].rearrange("p (c n) -> p c n", c=2),
                      h_hb[2 * m * P:(2 * m + 2) * P,
                           sst * 512:(sst + 1) * 512]
                      .rearrange("(c p) n -> p c n", p=P),
                      waits=[hr_rd[j]])
            tv0 = S.emit("scalar", lambda e, j=j: e.activation(
                tqs[j][:, 0:1024], hr[j], AF.Copy,
                scale=rsh_col, bias=MAGIC),
                waits=[("ds", qv["ds"]), rsh_ok, tqs_rd[j]],
                incs=[("act", 1)])
            hr_rd[j] = ("act", tv0["act"])
            uv = S.emit("vector", lambda e, j=j: e.tensor_scalar(
                xq_ld[j][:, 0:1024], tqs[j][:, 0:1024],
                MAGIC + QMAX, MAGIC - QMAX - 1.0,
                ALU.min, ALU.max),
                waits=[("act", tv0["act"]), xq_rd[j]], incs=[("dve", 1)])
            tqs_rd[j] = ("dve", uv["dve"])
            w0 = [qhs_rd[sst % 2]] if m == 0 else []
            qv2 = S.emit("vector", lambda e, j=j, sst=sst, m=m:
                         e.tensor_single_scalar(
                             qh_s[sst % 2][:, m * 1024:(m + 1) * 1024],
                             xq_ld[j][:, 0:1024], MAGIC, ALU.subtract),
                         waits=w0, incs=[("dve", 1)])
            xq_rd[j] = ("dve", qv2["dve"])
            if m == 10:
                sl = S.emit("vector", lambda e, sst=sst: e.tensor_copy(
                    qx_tl[0], qh_s[sst % 2][:, 10 * 1024:11 * 1024]),
                    incs=[("dve", 1)])
                qhs_ok[sst] = ("dve", sl["dve"])

        for m in range(11):
            emit_qh_pair(0, m)
        for m in range(11):
            emit_qh_pair(1, m)

        if PHASE_LIMIT < 5:
            return _finish(nc, S, ENGS, block_ctx=None, blocks=locals())
        dn_free = [None, None, None, None]
        out_pend = []
        asm_rd = [None, None]
        asm2_rd = [None, None]
        outb_rd = [None, None]
        xr_rd = [None, None]
        aor_rd = [None, None]
        for sst in range(NST):
            for nc2 in range(4):
                jn = nc2 % 2
                g2 = sst * 4 + nc2
                qv = _dma("scalar", xr[jn], x_hb[g2 * P:(g2 + 1) * P, :],
                          waits=[xr_rd[jn]], sem="dl")
                x_ld = ("dl", qv["dl"])
                qa = _dma("scalar", aor[jn], ao_hb[g2 * P:(g2 + 1) * P, :],
                          waits=[aor_rd[jn]], sem="dl")
                ao_ld = ("dl", qa["dl"])
                for half in range(2):
                    bk = (nc2 * 2 + half) % 4
                    m = None
                    for ic in range(NIC):
                        w0 = []
                        if ic == 0:
                            w0 = [dn_free[bk], qhs_ok[sst]]
                            if sst == 0 and nc2 == 0 and half == 0:
                                w0.append(qwd_ok)
                        m = S.emit("tensor", lambda e, sst=sst, nc2=nc2,
                                   half=half, ic=ic, bk=bk: e.matmul(
                                       p_mm[bk][:, 0:512],
                                       qh_s[sst % 2][:, ic * 512 + nc2 * P:
                                                     ic * 512 + (nc2 + 1) * P],
                                       qwd_sb[:, ic * 1024 + half * 512:
                                              ic * 1024 + (half + 1) * 512],
                                       start=(ic == 0), stop=(ic == NIC - 1)),
                                   waits=w0,
                                   incs=([("pe", 1)] if ic == NIC - 1 else []))
                    w0 = [("pe", m["pe"]), sc_ok]
                    if half == 0:
                        w0.append(asm_rd[jn])
                    t1 = S.emit("vector", lambda e, jn=jn, half=half, bk=bk:
                                e.tensor_tensor(
                                    asm[jn][:, half * 512:(half + 1) * 512],
                                    p_mm[bk][:, 0:512],
                                    sc_tile[:, half * 512:(half + 1) * 512],
                                    ALU.mult),
                                waits=w0, incs=[("dve", 1)])
                    dn_free[bk] = ("dve", t1["dve"])
                t2 = S.emit("vector", lambda e, jn=jn: e.tensor_tensor(
                    asm2[jn], asm[jn], aor[jn], ALU.add),
                    waits=[ao_ld, asm2_rd[jn]],
                    incs=[("dve", 1)])
                asm_rd[jn] = ("dve", t2["dve"])
                aor_rd[jn] = ("dve", t2["dve"])
                t3 = S.emit("vector", lambda e, jn=jn: e.tensor_add(
                    outb[jn], asm2[jn], xr[jn]),
                    waits=[x_ld, outb_rd[jn]],
                    incs=[("dve", 1)])
                asm2_rd[jn] = ("dve", t3["dve"])
                xr_rd[jn] = ("dve", t3["dve"])
                sv = _dma("scalar", out_hb[g2 * P:(g2 + 1) * P, :], outb[jn],
                          waits=[("dve", t3["dve"])], sem="so")
                outb_rd[jn] = ("so", sv["so"])
                last_ffn2_pe = m["pe"]
            qhs_rd[sst % 2] = ("pe", last_ffn2_pe)
            if sst + 2 < NST:
                for m2 in range(11):
                    emit_qh_pair(sst + 2, m2)

        dwaits = [("dve", S.cnt["dve"]), ("act", S.cnt["act"]),
                  ("pe", S.cnt["pe"]), ("pool", S.cnt["pool"])]

        def _dbg_dma(dst, srcv, waits=()):
            if not globals().get("DEBUG_IO"):
                return
            S.emit("gpsimd", lambda e, d=dst, s=srcv: e.dma_start(
                out=d, in_=s, allow_slow_non_contiguous=True),
                waits=waits, incs=[("dg", 16)])
        _dbg_dma(dbg_hb[0:1, 0:16], scal[0:1, 0:16], dwaits)
        _dbg_dma(dbg_hb[0:P, 16:20], bc_cols[0:P, 0:4])
        _dbg_dma(dbg_hb[0:P, 20:21], mx_xn[0:P, 0:1])
        _dbg_dma(dbg_hb[0:P, 21:22], mx_h[0:P, 0:1])
        _dbg_dma(dbg_hb[0:P, 22:26], sgu_cols[0:P, 0:4])
        _dbg_dma(dbg_hb[0:P, 26:32], stat[1][0:P, 4:10])
        _dbg_dma(dbg2_hb[0:P, 0:16], stat[0][0:P, 0:16])
        _dbg_dma(dbg2_hb[0:P, 16:32], stat[1][0:P, 0:16])
        _dbg_dma(dbg2_hb[0:1, 32:56], scal[0:1, 0:24])
        _dbg_dma(dbg2_hb[0:P, 56:60], qsr[0:P, 0:4])
        _dbg_dma(dbg2_hb[0:P, 60:64], gsc_cols[0:P, 0:4])


        so_total = S.cnt["so"]
        dg_total = S.cnt["dg"]
        for eng in ("scalar", "vector", "gpsimd"):
            S.emit(eng, waits=[("so", so_total), ("dg", dg_total)])

        with nc.Block() as block:
            @block.sync
            def _(e):
                for f in S.ops["sync"]:
                    f(e)

            @block.gpsimd
            def _(e):
                for f in S.ops["gpsimd"]:
                    f(e)

            @block.vector
            def _(e):
                for f in S.ops["vector"]:
                    f(e)

            @block.scalar
            def _(e):
                for f in S.ops["scalar"]:
                    f(e)

            @block.tensor
            def _(e):
                for f in S.ops["tensor"]:
                    f(e)

    return nc


_NC_CACHE = None


def kernel(L0_attn_x, L0_attn_Wq, L0_attn_Wk, L0_attn_Wv, L0_attn_Wo,
           L0_attn_bo, L0_ffn_clip_x, L0_ffn_clip_down, L0_ffn_w_gate,
           L0_ffn_w_up, L0_ffn_w_down):
    global _NC_CACHE
    x = np.ascontiguousarray(np.asarray(L0_attn_x, dtype=np.float32))
    wq = np.ascontiguousarray(np.asarray(L0_attn_Wq, dtype=np.float32))
    wk = np.ascontiguousarray(np.asarray(L0_attn_Wk, dtype=np.float32))
    wv = np.ascontiguousarray(np.asarray(L0_attn_Wv, dtype=np.float32))
    wo = np.ascontiguousarray(np.asarray(L0_attn_Wo, dtype=np.float32))
    bo = np.ascontiguousarray(np.asarray(L0_attn_bo, dtype=np.float32)
                              .reshape(1, D))
    cx = np.ascontiguousarray(np.asarray(L0_ffn_clip_x, dtype=np.float32)
                              .reshape(1, 1))
    cd = np.ascontiguousarray(np.asarray(L0_ffn_clip_down, dtype=np.float32)
                              .reshape(1, 1))
    wg = np.asarray(L0_ffn_w_gate, dtype=np.float32)
    wu = np.asarray(L0_ffn_w_up, dtype=np.float32)
    wdn = np.asarray(L0_ffn_w_down, dtype=np.float32)
    wgu = np.ascontiguousarray(np.concatenate([wg, wu], axis=0).T)
    wdt = np.ascontiguousarray(wdn.T)

    if _NC_CACHE is None:
        _NC_CACHE = build_nc()
    nc = _NC_CACHE

    in_maps = []
    for i in range(NCORE):
        in_maps.append({
            "x": np.ascontiguousarray(x[i]),
            "wq": wq, "wk": wk, "wv": wv, "wo": wo, "bo": bo,
            "cx": cx, "cd": cd, "wgu": wgu, "wdt": wdt,
        })
    res = run_bass_kernel_spmd(nc, in_maps, core_ids=list(range(NCORE)))
    return np.stack([np.asarray(res.results[i]["out"]) for i in range(NCORE)],
                    axis=0)
